# version 02
# baseline (speedup 1.0000x reference)
import numpy as np
import ml_dtypes
import concourse.bass as bass
import concourse.mybir as mybir
from concourse.bass_utils import run_bass_kernel_spmd

F32 = mybir.dt.float32
BF16 = mybir.dt.bfloat16
AF = mybir.ActivationFunctionType
ALU = mybir.AluOpType
AX = mybir.AxisListType

T = 4096
NP = 8
NCH = 32
NE = 34
CONV_CHUNK = 256

ENGINES = ("sync", "tensor", "vector", "scalar", "gpsimd")


class Prog:

    def __init__(self, nc, sems):
        self.nc = nc
        self.sems = sems
        self.counts = {k: 0 for k in sems}
        self.ev = {}
        self.pending = {e: [] for e in ENGINES}
        self.maxwait = {e: {} for e in ENGINES}

    def op(self, eng, fn, waits=(), sig=None, sem=None, amt=1):
        for w in waits:
            assert w in self.ev, f"wait on undefined event {w}"
        if sig is not None:
            assert sem is not None
            self.counts[sem] += amt
            assert sig not in self.ev, f"event {sig} redefined"
            self.ev[sig] = (sem, self.counts[sem])
        self.pending[eng].append((fn, tuple(waits), sig, sem, amt))

    def dma(self, eng, fn, waits=(), sig=None, sem=None):
        assert sem is not None
        self.op(eng, fn, waits, sig, sem, amt=16)

    def alias(self, name, sem):
        assert name not in self.ev
        self.ev[name] = (sem, self.counts[sem])

    def flush(self):
        nc = self.nc
        if not any(self.pending.values()):
            return
        with nc.Block() as block:
            for eng_name in ENGINES:
                ops = self.pending[eng_name]
                if not ops:
                    continue
                mw = self.maxwait[eng_name]

                def emit(eng, ops=ops, mw=mw):
                    for fn, waits, sig, sem, amt in ops:
                        for w in waits:
                            wsem, wval = self.ev[w]
                            if wval <= mw.get(wsem, 0):
                                continue
                            mw[wsem] = wval
                            eng.wait_ge(self.sems[wsem], wval)
                        instr = fn(eng) if fn is not None else None
                        if sig is not None:
                            instr.then_inc(self.sems[sem], amt)
                getattr(block, eng_name)(emit)
        self.pending = {e: [] for e in ENGINES}


def build(dbg=(), stop=None):
    nc = bass.Bass("TRN2", target_bir_lowering=False)
    d = {}

    def din(name, shape, dt=F32):
        d[name] = nc.dram_tensor(name, shape, dt, kind="ExternalInput").ap()
    din("x", [T, 1024])
    din("wqkv_src", [NP, 128, 3072])
    din("wout_src", [128, 8192])
    din("eg_src", [NE, 128, 4096])
    din("eu_src", [NE, 128, 4096])
    din("ed_src", [NE, 128, 4096])
    din("gw_src", [128, 256])
    din("omega_src", [128, 1024])
    din("bqkv_pp", [128, 24])
    din("bout_src", [1, 1024])
    din("quad_pp", [128, 2])
    din("gbias_src", [1, 32])
    din("ident_bf", [128, 128], BF16)
    din("maskrep_bf", [128, 256], BF16)
    din("blkdiag_bf", [128, 128], BF16)
    din("ones1_bf", [1, 128], BF16)
    din("ones1_f", [1, 128])

    out = nc.dram_tensor("out", [T, 1024], F32, kind="ExternalOutput").ap()
    dbg_t = {}
    dbg_spec = {"t_attnT": ([128, NP * T], BF16), "t_y": ([32, 128, 1024], F32),
                "t_wroute": ([128, 1024], F32), "t_qf": ([128, T], BF16),
                "t_kf": ([128, T], BF16), "t_vt": ([128, T], BF16)}
    for nm in dbg:
        shape, dt = dbg_spec[nm]
        dbg_t[nm] = nc.dram_tensor(nm, shape, dt, kind="ExternalOutput").ap()

    scr = {}
    def dscr(name, shape, dt=BF16):
        scr[name] = nc.dram_tensor(name, shape, dt, kind="Internal").ap()
    dscr("wqkv_scr", [NP, 128, 3072])
    dscr("wout_scr", [128, 8192])
    dscr("eg_scr", [NE, 128, 4096])
    dscr("eu_scr", [NE, 128, 4096])
    dscr("ed_scr", [NE, 128, 4096])
    dscr("y_scr", [32, 128, 1024], F32)

    conv = []
    def add_conv(src2d, dst2d, width):
        for j in range(width // CONV_CHUNK):
            conv.append((src2d[:, j * CONV_CHUNK:(j + 1) * CONV_CHUNK],
                         dst2d[:, j * CONV_CHUNK:(j + 1) * CONV_CHUNK]))
    for p in range(NP):
        add_conv(d["wqkv_src"][p], scr["wqkv_scr"][p], 3072)
    n_per_pair = 3072 // CONV_CHUNK
    add_conv(d["wout_src"], scr["wout_scr"], 8192)
    n_wout_end = len(conv)
    n_per_exp = 3 * (4096 // CONV_CHUNK)
    for e in range(NE):
        for nm in ("eg", "eu", "ed"):
            add_conv(d[nm + "_src"][e], scr[nm + "_scr"][e], 4096)
    NCONV = len(conv)
    def conv_ready(k):
        return [f"cst_{k - 1}"] + ([f"cst_{k - 2}"] if k >= 2 else [])
    wqkv_ready = lambda p: conv_ready(n_per_pair * (p + 1))
    wout_ready = conv_ready(n_wout_end)
    exp_ready = lambda e: conv_ready(n_wout_end + n_per_exp * (e + 1))

    from contextlib import ExitStack
    stack = ExitStack()
    e_ = stack.enter_context

    sems = {k: e_(nc.semaphore(k)) for k in (
        "pe", "ve", "ae", "gp", "si", "sx0", "sx1", "cv0", "cv1", "gq0", "gq1",
        "wp", "swo", "yw0", "yw1", "eb0", "eb1", "sac", "sout", "stp")}
    P = Prog(nc, sems)

    sbg = {}
    def alloc(name, shape, dt):
        g = nc.sbuf_tensor(name, shape, dt)
        h = g.__enter__()
        sbg[name] = g
        return h

    U8 = mybir.dt.uint8
    RKB = 64 * 1024
    arena = alloc("arena", [128, 3 * RKB], U8)
    _rbase = {"r1": 0, "r2": RKB, "r3": 2 * RKB}
    _bump = dict(_rbase)

    def aalloc(shape, dt, region):
        assert shape[0] == 128
        esz = 4 if dt == F32 else 2
        nbytes = shape[1] * esz
        off = (_bump[region] + 511) // 512 * 512
        assert off + nbytes <= _rbase[region] + RKB, (region, off, nbytes)
        _bump[region] = off + nbytes
        return arena[:, off: off + nbytes].bitcast(dt)

    def areset(region, keep=0):
        _bump[region] = _rbase[region] + keep

    ident = alloc("ident", [128, 128], BF16)
    maskr = alloc("maskr", [128, 256], BF16)
    blkd = alloc("blkd", [128, 128], BF16)
    ones1b = alloc("ones1b", [1, 128], BF16)
    ones1f = alloc("ones1f", [1, 128], F32)
    bqkv = alloc("bqkv", [128, 24], F32)
    quad = alloc("quad", [128, 2], F32)
    qs2s = alloc("qs2s", [128, 1], F32)
    qnegs = alloc("qnegs", [128, 1], F32)
    qspp = alloc("qspp", [128, 1], F32)
    omg = alloc("omg", [128, 1024], BF16)
    gwb = alloc("gwb", [128, 256], BF16)
    boutf = alloc("boutf", [1, 1024], F32)
    boutb = alloc("boutb", [1, 1024], BF16)
    gbias = alloc("gbias", [1, 32], F32)
    lncol = alloc("lncol", [128, 8], F32)
    epsc = alloc("epsc", [128, 1], F32)
    cin = [alloc(f"cin{i}", [128, CONV_CHUNK], F32) for i in range(2)]
    cout = [alloc(f"cout{i}", [128, CONV_CHUNK], BF16) for i in range(2)]

    pb = [e_(nc.psum_tensor(f"pb{i}", [128, 512], F32)) for i in range(8)]
    pbb = [b[:].bitcast(BF16) for b in pb]

    XT = aalloc([128, 8 * T], BF16, "r1")
    ATT = aalloc([128, NP * T], BF16, "r2")
    wpair = aalloc([128, 3072], BF16, "r3")
    r3_mark = _bump["r3"]
    xld = [aalloc([128, 1024], F32, "r3") for i in range(2)]
    xbf = aalloc([128, 1024], BF16, "r3")
    omg_f = aalloc([128, 1024], F32, "r3")
    gwf = aalloc([128, 256], F32, "r3")

    for nm, dst in (("ident_bf", ident), ("maskrep_bf", maskr), ("blkdiag_bf", blkd),
                    ("ones1_bf", ones1b), ("ones1_f", ones1f), ("bqkv_pp", bqkv),
                    ("quad_pp", quad), ("omega_src", omg_f), ("gw_src", gwf),
                    ("bout_src", boutf), ("gbias_src", gbias)):
        P.dma("sync", lambda g, nm=nm, dst=dst: g.dma_start(out=dst[:], in_=d[nm]),
              sig=f"ld_{nm}", sem="si")

    P.alias("init_done", "si")
    P.op("vector", lambda g: g.memset(epsc[:], 1e-5), sig="v_eps", sem="ve")
    P.op("vector", lambda g: g.tensor_scalar_max(qs2s[:], quad[:, 0:1], 0.0),
         waits=["init_done"], sig="v_q1", sem="ve")
    P.op("scalar", lambda g: g.activation(qs2s[:], qs2s[:], AF.Sqrt, scale=2.0),
         waits=["v_q1"], sig="a_q1", sem="ae")
    P.op("vector", lambda g: g.tensor_scalar_mul(qnegs[:], quad[:, 0:1], -1.0),
         waits=["init_done"], sig="v_q2", sem="ve")
    P.op("vector", lambda g: g.tensor_scalar_max(qspp[:], quad[:, 1:2], 0.0),
         waits=["init_done"], sig="v_q3", sem="ve")
    P.op("scalar", lambda g: g.activation(qspp[:], qspp[:], AF.Sqrt), waits=["v_q3"], sig="a_q2", sem="ae")
    P.op("scalar", lambda g: g.activation(qspp[:], qspp[:], AF.Sqrt), waits=["a_q2"], sig="a_q3", sem="ae")
    P.op("vector", lambda g: g.tensor_scalar_mul(qspp[:], qspp[:], 32.0 ** -0.5),
         waits=["a_q3"], sig="v_q4", sem="ve")

    P.op("gpsimd", lambda g: g.tensor_copy(omg[:], omg_f[:]), waits=["init_done"], sig="g_omg", sem="gp")
    P.op("gpsimd", lambda g: g.tensor_copy(gwb[:], gwf[:]), waits=["init_done"], sig="g_gw", sem="gp")
    P.op("gpsimd", lambda g: g.tensor_copy(boutb[:], boutf[:]), waits=["init_done"], sig="g_bout", sem="gp")

    conv_emitted = [0]
    def emit_conv(upto):
        upto = min(upto, NCONV)
        for i in range(conv_emitted[0], upto):
            b = i % 2
            lw = [f"gcv_{i - 2}"] if i >= 2 else []
            P.dma("sync", lambda g, i=i, b=b: g.dma_start(out=cin[b][:], in_=conv[i][0]),
                  waits=lw, sig=f"cld_{i}", sem=f"cv{i % 2}")
            cw = [f"cld_{i}"] + ([f"cst_{i - 2}"] if i >= 2 else [])
            P.op("gpsimd", lambda g, b=b: g.tensor_copy(cout[b][:], cin[b][:]),
                 waits=cw, sig=f"gcv_{i}", sem="gp")
            P.dma("gpsimd", lambda g, i=i, b=b: g.dma_start(out=conv[i][1], in_=cout[b][:]),
                  waits=[f"gcv_{i}"], sig=f"cst_{i}", sem=f"gq{i % 2}")
        conv_emitted[0] = upto

    def finish(waits):
        w = list(waits)
        ce = conv_emitted[0]
        if ce >= 1:
            w.append(f"cst_{ce - 1}")
        if ce >= 2:
            w.append(f"cst_{ce - 2}")
        P.op("sync", None, waits=w)
        P.flush()
        stack.close()
        return nc, dbg_t

    xr = d["x"].rearrange("(n p) c -> n p c", p=128)
    for t in range(32):
        lw = [f"v_xbf_{t - 2}"] if t >= 2 else []
        P.dma("sync", lambda g, t=t: g.dma_start(out=xld[t % 2][:], in_=xr[t]),
              waits=lw, sig=f"xld_{t}", sem=f"sx{t % 2}")
        bw = [f"xld_{t}"] + ([f"pe_x_{t - 1}_7"] if t >= 1 else [])
        P.op("vector", lambda g, t=t: g.tensor_copy(xbf[:], xld[t % 2][:]),
             waits=bw, sig=f"v_xbf_{t}", sem="ve")
        for gc in range(8):
            bk = 6 + (gc % 2)
            tw = [f"v_xbf_{t}", "init_done"]
            if (t, gc) > (0, 1):
                pt, pg = (t, gc - 2) if gc >= 2 else (t - 1, gc + 6)
                tw.append(f"a_x_{pt}_{pg}")
            P.op("tensor", lambda g, t=t, gc=gc, bk=bk: g.transpose(
                pbb[bk][:, 0:128], xbf[:, gc * 128:(gc + 1) * 128], ident[:]),
                waits=tw, sig=f"pe_x_{t}_{gc}", sem="pe")
            P.op("scalar", lambda g, t=t, gc=gc, bk=bk: g.copy(
                XT[:, gc * T + t * 128: gc * T + (t + 1) * 128], pbb[bk][:, 0:128]),
                waits=[f"pe_x_{t}_{gc}"], sig=f"a_x_{t}_{gc}", sem="ae")

    emit_conv(n_per_pair)

    if stop == "phase0":
        return finish(["a_x_31_7"])

    areset("r3", keep=r3_mark - _rbase["r3"])
    qkf = [aalloc([128, T], BF16, "r3") for i in range(2)]
    vT = aalloc([128, T], BF16, "r3")
    qkts = [aalloc([128, 512], BF16, "r3") for i in range(2)]
    sqs = aalloc([128, 512], BF16, "r3")
    nrm2g = aalloc([128, 512], F32, "r3")
    rinvbc = aalloc([128, 512], F32, "r3")
    znT = aalloc([128, 512], BF16, "r3")
    poly = aalloc([128, 512], F32, "r3")
    Vt = aalloc([128, NCH * 130], BF16, "r3")
    Ktok = [aalloc([128, 128], BF16, "r3") for i in range(2)]
    upd = aalloc([128, 2 * 260], F32, "r3")
    stbf = aalloc([128, NCH * 65], BF16, "r3")
    stf = aalloc([128, 65], F32, "r3")
    dcol = aalloc([128, 2], F32, "r3")
    smk = [aalloc([128, 256], BF16, "r3") for i in range(2)]
    otok = [aalloc([128, 128], BF16, "r3") for i in range(2)]

    P.op("gpsimd", lambda g: g.memset(
        Vt[:].rearrange("p (ch j w) -> p ch j w", ch=NCH, j=2)[:, :, :, 64:65], 1.0),
        sig="g_vones", sem="gp")

    for p in range(NP):
        pw = wqkv_ready(p)[:]
        if p >= 1:
            pw.append(f"pe_vmm_{p - 1}_7")
        P.dma("sync", lambda g, p=p: g.dma_start(out=wpair[:], in_=scr["wqkv_scr"][p]),
              waits=pw, sig=f"wp_{p}", sem="wp")

        for tb in range(8):
            for qk in range(2):
                fi = (p * 8 + tb) * 2 + qk
                mmw = [f"wp_{p}", "a_x_31_7"]
                if fi >= 2:
                    mmw.append(f"a_qke_{fi - 2}")
                if tb == 0 and qk == 0 and p >= 1:
                    mmw.append(f"v_smk_{p * NCH - 1}")
                for gc in range(8):
                    P.op("tensor", lambda g, p=p, tb=tb, qk=qk, gc=gc: g.matmul(
                        pb[qk][:], wpair[:, gc * 384 + qk * 128: gc * 384 + qk * 128 + 128],
                        XT[:, gc * T + tb * 512: gc * T + (tb + 1) * 512],
                        start=(gc == 0), stop=(gc == 7)),
                        waits=(mmw if gc == 0 else []),
                        sig=(f"pe_qk_{fi}" if gc == 7 else None), sem="pe")
            vw = [f"wp_{p}", "a_x_31_7"]
            vi = p * 8 + tb
            if vi >= 2:
                vw.append(f"a_ve_{vi - 2}")
            if tb == 0 and p >= 1:
                vw.append(f"v_ot_{p * NCH - 1}")
            vbk = 2 + (tb % 2)
            for gc in range(8):
                P.op("tensor", lambda g, p=p, tb=tb, gc=gc, vbk=vbk: g.matmul(
                    pb[vbk][:], wpair[:, gc * 384 + 256: gc * 384 + 384],
                    XT[:, gc * T + tb * 512: gc * T + (tb + 1) * 512],
                    start=(gc == 0), stop=(gc == 7)),
                    waits=(vw if gc == 0 else []),
                    sig=(f"pe_vmm_{p}_{tb}" if gc == 7 else None), sem="pe")
            P.op("scalar", lambda g, p=p, tb=tb, vbk=vbk: g.activation(
                vT[:, tb * 512:(tb + 1) * 512], pb[vbk][:], AF.Identity,
                bias=bqkv[:, 3 * p + 2: 3 * p + 3]),
                waits=[f"pe_vmm_{p}_{tb}", "init_done"], sig=f"a_ve_{vi}", sem="ae")

            for qk in range(2):
                fi = (p * 8 + tb) * 2 + qk
                ew = [f"pe_qk_{fi}", "init_done"]
                if fi >= 2:
                    ew.append(f"v_zn_{fi - 2}")
                P.op("scalar", lambda g, p=p, qk=qk: g.activation(
                    qkts[qk][:], pb[qk][:], AF.Identity, bias=bqkv[:, 3 * p + qk: 3 * p + qk + 1]),
                    waits=ew, sig=f"a_qke_{fi}", sem="ae")
                sw = [f"a_qke_{fi}"]
                if fi >= 1:
                    sw.append(f"pe_n_{fi - 1}")
                P.op("scalar", lambda g, qk=qk: g.activation(sqs[:], qkts[qk][:], AF.Square),
                     waits=sw, sig=f"a_sq_{fi}", sem="ae")
                nw = [f"a_sq_{fi}", "init_done"]
                if fi >= 1:
                    nw.append(f"v_ng_{fi - 1}")
                P.op("tensor", lambda g: g.matmul(pb[4][:], blkd[:], sqs[:], start=True, stop=True),
                     waits=nw, sig=f"pe_n_{fi}", sem="pe")
                gw_ = [f"pe_n_{fi}"]
                if fi >= 1:
                    gw_.append(f"a_rinv_{fi - 1}")
                P.op("vector", lambda g: g.tensor_scalar_max(nrm2g[:], pb[4][:], 1e-24),
                     waits=gw_, sig=f"v_ng_{fi}", sem="ve")
                P.op("scalar", lambda g: g.activation(nrm2g[:], nrm2g[:], AF.Ln),
                     waits=[f"v_ng_{fi}"], sig=f"a_ln_{fi}", sem="ae")
                rw = [f"a_ln_{fi}"]
                if fi >= 1:
                    rw.append(f"v_zn_{fi - 1}")
                P.op("scalar", lambda g: g.activation(rinvbc[:], nrm2g[:], AF.Exp, scale=-0.5),
                     waits=rw, sig=f"a_rinv_{fi}", sem="ae")
                zw = [f"a_rinv_{fi}"]
                if fi >= 1:
                    zw.append(f"pe_pr_{fi - 1}")
                P.op("vector", lambda g, qk=qk: g.tensor_tensor(znT[:], qkts[qk][:], rinvbc[:], ALU.mult),
                     waits=zw, sig=f"v_zn_{fi}", sem="ve")
                prw = [f"v_zn_{fi}", "g_omg", "g_gw"]
                if fi >= 1:
                    prw.append(f"a_poly_{fi - 1}")
                P.op("tensor", lambda g, p=p: g.matmul(
                    pb[5][:], omg[:, p * 128:(p + 1) * 128], znT[:], start=True, stop=True),
                    waits=prw, sig=f"pe_pr_{fi}", sem="pe")
                eaw = [f"pe_pr_{fi}", "a_q1", "v_q2"]
                if fi >= 1:
                    eaw.append(f"a_prf_{fi - 1}")
                P.op("vector", lambda g: g.tensor_scalar(nrm2g[:], pb[5][:], qs2s[:, 0:1],
                                                         qnegs[:, 0:1], ALU.mult, ALU.add),
                     waits=eaw, sig=f"v_ea_{fi}", sem="ve")
                P.op("vector", lambda g: g.tensor_scalar(nrm2g[:], nrm2g[:], 20.0, -20.0,
                                                         ALU.min, ALU.max),
                     waits=[f"v_ea_{fi}"], sig=f"v_cl_{fi}", sem="ve")
                pw_ = [f"v_cl_{fi}"]
                if fi >= 1:
                    pw_.append(f"v_f_{fi - 1}")
                P.op("scalar", lambda g: g.activation(rinvbc[:], nrm2g[:], AF.Exp),
                     waits=pw_, sig=f"a_prf_{fi}", sem="ae")
                pyw = [f"v_ea_{fi}", "v_q4"]
                if fi >= 1:
                    pyw.append(f"v_f_{fi - 1}")
                P.op("scalar", lambda g: g.activation(poly[:], pb[5][:], AF.Square, scale=qspp[:, 0:1]),
                     waits=pyw, sig=f"a_poly_{fi}", sem="ae")
                fw_ = [f"a_prf_{fi}", f"a_poly_{fi}"]
                if tb == 0 and p >= 1:
                    fw_.append(f"pe_ctx_{p * NCH - 1}")
                P.op("vector", lambda g, qk=qk, tb=tb: g.tensor_tensor(
                    qkf[qk][:, tb * 512:(tb + 1) * 512], rinvbc[:], poly[:], ALU.mult),
                    waits=fw_, sig=f"v_f_{fi}", sem="ve")
            if tb % 2 == 1:
                emit_conv(n_per_pair * NP + (p * 8 + tb) * 12)

        if p == 0 and stop == "feat0":
            return finish([f"v_f_{16 - 1}", f"a_ve_{7}"])
        pair_f = f"v_f_{(p + 1) * 16 - 1}"
        P.op("vector", lambda g: g.memset(stf[:], 0.0),
             waits=([f"pe_ctx_{p * NCH - 1}"] if p >= 1 else []), sig=f"v_st0_{p}", sem="ve")
        for c4 in range(8):
            for j in range(4):
                ch = c4 * 4 + j
                k = p * NCH + ch
                vbk2 = 2 + (ch % 2)
                tw = ["g_vones", pair_f, f"a_ve_{p * 8 + 7}"]
                if k >= 2:
                    tw.append(f"a_vte_{k - 2}")
                P.op("tensor", lambda g, ch=ch, vbk2=vbk2: g.transpose(
                    pbb[vbk2][:, 0:128], vT[:, ch * 128:(ch + 1) * 128], ident[:]),
                    waits=tw, sig=f"pe_vtr_{p}_{ch}", sem="pe")
                P.op("scalar", lambda g, ch=ch, vbk2=vbk2: g.copy(
                    Vt[:, ch * 130:(ch + 1) * 130].rearrange("p (j w) -> p j w", j=2)[:, :, 0:64],
                    pbb[vbk2][:, 0:128].rearrange("p (j w) -> p j w", j=2)),
                    waits=[f"pe_vtr_{p}_{ch}"], sig=f"a_vte_{k}", sem="ae")
                kbk = 6 + (ch % 2)
                ktw = [f"a_kte_{k - 2}"] if k >= 2 else []
                P.op("tensor", lambda g, ch=ch, kbk=kbk: g.transpose(
                    pbb[kbk][:, 0:128], qkf[1][:, ch * 128:(ch + 1) * 128], ident[:]),
                    waits=ktw, sig=f"pe_ktr_{p}_{ch}", sem="pe")
                kew = [f"pe_ktr_{p}_{ch}"]
                if k >= 2:
                    gp_, gc_ = (k - 2) // NCH, (k - 2) % NCH
                    kew.append(f"pe_upd2_{gp_}_{gc_ // 4}_{gc_ % 4}")
                P.op("scalar", lambda g, ch=ch, kbk=kbk: g.copy(Ktok[ch % 2][:], pbb[kbk][:, 0:128]),
                     waits=kew, sig=f"a_kte_{k}", sem="ae")
                uw = [f"a_kte_{k}", f"a_vte_{k}"]
                if c4 >= 2 and j == 0:
                    uw.append(f"a_ue_{p}_{c4 - 2}")
                P.op("tensor", lambda g, ch=ch, j=j: g.matmul(
                    pb[4][0:64, j * 65:(j + 1) * 65], Ktok[ch % 2][:, 0:64],
                    Vt[:, ch * 130: ch * 130 + 65], start=True, stop=True),
                    waits=uw, sem="pe")
                P.op("tensor", lambda g, ch=ch, j=j: g.matmul(
                    pb[4][64:128, j * 65:(j + 1) * 65], Ktok[ch % 2][:, 64:128],
                    Vt[:, ch * 130 + 65: ch * 130 + 130], start=True, stop=True,
                    tile_position=(0, 64)),
                    waits=[], sig=f"pe_upd2_{p}_{c4}_{j}", sem="pe")
            ew = [f"pe_upd2_{p}_{c4}_3"]
            if c4 >= 2:
                ew.append(f"v_sta_{p}_{(c4 - 2) * 4 + 3}")
            elif p >= 1:
                ew.append(f"v_sta_{p - 1}_{(c4 + 6) * 4 + 3}")
            P.op("scalar", lambda g, c4=c4: g.copy(upd[:, (c4 % 2) * 260:(c4 % 2) * 260 + 260],
                                                   pb[4][:, 0:260]),
                 waits=ew, sig=f"a_ue_{p}_{c4}", sem="ae")
            for j2 in range(4):
                ch2 = c4 * 4 + j2
                cw = [f"v_st0_{p}"] if ch2 == 0 else [f"v_sta_{p}_{ch2 - 1}"]
                if p >= 1 and ch2 <= 1:
                    cw.append(f"pe_ctx_{p * NCH - 1}")
                P.op("vector", lambda g, ch2=ch2: g.tensor_copy(stbf[:, ch2 * 65:(ch2 + 1) * 65], stf[:]),
                     waits=cw, sig=f"v_stb_{p}_{ch2}", sem="ve")
                P.op("vector", lambda g, ch2=ch2, c4=c4, j2=j2: g.tensor_tensor(
                    stf[:], stf[:], upd[:, (c4 % 2) * 260 + j2 * 65:
                                         (c4 % 2) * 260 + j2 * 65 + 65], ALU.add),
                    waits=[f"a_ue_{p}_{c4}", f"v_stb_{p}_{ch2}"], sig=f"v_sta_{p}_{ch2}", sem="ve")

        if p == 0 and stop == "stageA0":
            return finish([f"v_sta_0_31", f"a_ue_0_7", f"a_vte_{NCH - 1}"])
        for ch in range(NCH):
            k = p * NCH + ch
            sbk = ch % 2
            sw = [f"v_stb_{p}_{ch}", pair_f]
            if k >= 2:
                sw.append(f"v_smk_{k - 2}")
            P.op("tensor", lambda g, ch=ch, sbk=sbk: g.matmul(
                pb[sbk][:, 0:128], qkf[1][0:64, ch * 128:(ch + 1) * 128],
                qkf[0][0:64, ch * 128:(ch + 1) * 128], start=True, stop=True),
                waits=sw, sem="pe")
            P.op("tensor", lambda g, sbk=sbk: g.matmul(
                pb[sbk][0:1, 500:501], ident[:, 0:1], ident[:, 0:1],
                start=True, stop=True), waits=[], sem="pe")
            P.op("tensor", lambda g, ch=ch, sbk=sbk: g.matmul(
                pb[sbk][:, 128:256], qkf[1][64:128, ch * 128:(ch + 1) * 128],
                qkf[0][64:128, ch * 128:(ch + 1) * 128], start=True, stop=True),
                waits=[], sig=f"pe_S_{k}", sem="pe")
            if p == 0 and ch == 0 and stop == "sbS":
                return finish([f"pe_S_{k}"])
            P.op("vector", lambda g, ch=ch, sbk=sbk: g.tensor_tensor(
                smk[ch % 2][:], pb[sbk][:, 0:256], maskr[:], ALU.mult),
                waits=[f"pe_S_{k}", "init_done"], sig=f"v_smk_{k}", sem="ve")
            if p == 0 and ch == 0 and stop == "sbM":
                return finish([f"v_smk_{k}"])
            cbk = 2 + (ch % 2)
            cw = [f"v_smk_{k}", f"a_vte_{p * NCH + NCH - 1}"]
            if k >= 2:
                cw.append(f"v_ot_{k - 2}")
            P.op("tensor", lambda g, ch=ch, cbk=cbk: g.matmul(
                pb[cbk][:, 0:65], smk[ch % 2][:, 0:128], Vt[:, ch * 130: ch * 130 + 65],
                start=True, stop=False), waits=cw, sem="pe")
            P.op("tensor", lambda g, ch=ch, cbk=cbk: g.matmul(
                pb[cbk][:, 0:65], qkf[0][0:64, ch * 128:(ch + 1) * 128],
                stbf[0:64, ch * 65:(ch + 1) * 65], start=False, stop=True),
                waits=[], sem="pe")
            P.op("tensor", lambda g, ch=ch, cbk=cbk: g.matmul(
                pb[cbk][:, 65:130], smk[ch % 2][:, 128:256], Vt[:, ch * 130 + 65: ch * 130 + 130],
                start=True, stop=False), waits=[], sem="pe")
            P.op("tensor", lambda g, ch=ch, cbk=cbk: g.matmul(
                pb[cbk][:, 65:130], qkf[0][64:128, ch * 128:(ch + 1) * 128],
                stbf[64:128, ch * 65:(ch + 1) * 65], start=False, stop=True),
                waits=[], sig=f"pe_ctx_{k}", sem="pe")
            if p == 0 and ch == 0 and stop == "sbC":
                return finish([f"pe_ctx_{k}"])
            dw = [f"pe_ctx_{k}"]
            if k >= 2:
                dw.append(f"pe_otr_{k - 2}")
            if k >= 1:
                dw.append(f"v_om_{k - 1}_1")
            for h01 in range(2):
                base = h01 * 65
                P.op("vector", lambda g, cbk=cbk, base=base: g.tensor_scalar_max(
                    dcol[:, 0:1], pb[cbk][:, base + 64: base + 65], 1e-6),
                    waits=(dw if h01 == 0 else [f"v_dr_{k}_0", f"v_om_{k}_0"]),
                    sig=f"v_dx_{k}_{h01}", sem="ve")
                P.op("vector", lambda g: g.reciprocal(dcol[:, 1:2], dcol[:, 0:1]),
                     waits=[f"v_dx_{k}_{h01}"], sig=f"v_dr_{k}_{h01}", sem="ve")
                P.op("vector", lambda g, ch=ch, cbk=cbk, base=base, h01=h01: g.tensor_scalar_mul(
                    otok[ch % 2][:, h01 * 64:(h01 + 1) * 64], pb[cbk][:, base: base + 64],
                    dcol[:, 1:2]),
                    waits=[f"v_dr_{k}_{h01}"], sig=f"v_om_{k}_{h01}", sem="ve")
            P.alias(f"v_ot_{k}", "ve")
            if p == 0 and ch == 0 and stop == "sbD":
                return finish([f"v_ot_{k}"])
            obk = 6 + (ch % 2)
            ow = [f"v_ot_{k}"]
            if k >= 2:
                ow.append(f"a_ot_{k - 2}")
            P.op("tensor", lambda g, ch=ch, obk=obk: g.transpose(
                pbb[obk][:, 0:128], otok[ch % 2][:], ident[:]),
                waits=ow, sig=f"pe_otr_{k}", sem="pe")
            P.op("scalar", lambda g, p=p, ch=ch, obk=obk: g.copy(
                ATT[:, p * T + ch * 128: p * T + (ch + 1) * 128], pbb[obk][:, 0:128]),
                waits=[f"pe_otr_{k}"], sig=f"a_ot_{k}", sem="ae")
            if p == 0 and stop == f"sb{ch}":
                return finish([f"a_ot_{k}"])

        if stop == f"pair{p}":
            return finish([f"a_ot_{(p + 1) * NCH - 1}"])

    att_done = f"a_ot_{NP * NCH - 1}"
    tap_r3, tap_r2 = [], []
    if "t_qf" in dbg_t:
        P.dma("sync", lambda g: g.dma_start(out=dbg_t["t_qf"], in_=qkf[0][:]),
              waits=[f"v_f_{NP * 16 - 1}"], sig="tap_qf", sem="stp")
        tap_r3.append("tap_qf")
    if "t_kf" in dbg_t:
        P.dma("sync", lambda g: g.dma_start(out=dbg_t["t_kf"], in_=qkf[1][:]),
              waits=[f"v_f_{NP * 16 - 1}"], sig="tap_kf", sem="stp")
        tap_r3.append("tap_kf")
    if "t_vt" in dbg_t:
        P.dma("sync", lambda g: g.dma_start(out=dbg_t["t_vt"], in_=vT[:]),
              waits=[f"a_ve_{NP * 8 - 1}"], sig="tap_vt", sem="stp")
        tap_r3.append("tap_vt")
    if "t_attnT" in dbg_t:
        P.dma("sync", lambda g: g.dma_start(out=dbg_t["t_attnT"], in_=ATT[:]),
              waits=[att_done], sig="tap_att", sem="stp")
        tap_r2.append("tap_att")
    if tap_r3 or tap_r2:
        P.alias("tapsA_done", "stp")
        tap_r3 = ["tapsA_done"] if tap_r3 else []
        tap_r2 = ["tapsA_done"] if tap_r2 else []
    P.flush()

    areset("r1")
    areset("r3")
    woutb = aalloc([128, 8192], BF16, "r1")
    x2 = [aalloc([128, 1024], F32, "r1") for i in range(2)]
    s_sb = aalloc([128, 1024], F32, "r1")
    sq2 = aalloc([128, 1024], F32, "r1")
    y32 = [aalloc([128, 1024], F32, "r1") for i in range(2)]
    ybf = aalloc([128, 1024], BF16, "r1")
    YT = aalloc([128, 8 * T], BF16, "r3")

    P.dma("sync", lambda g: g.dma_start(out=woutb[:], in_=scr["wout_scr"]),
          waits=wout_ready + [att_done], sig="ld_wout", sem="swo")

    for t in range(32):
        lw = [f"v_y_{t - 2}"] if t >= 2 else [att_done]
        P.dma("sync", lambda g, t=t: g.dma_start(out=x2[t % 2][:], in_=xr[t]),
              waits=lw, sig=f"x2_{t}", sem=f"sx{t % 2}")
        for oh in range(2):
            mw = ["ld_wout", att_done, "g_bout", "init_done"]
            if t >= 2:
                mw.append(f"v_s_{t - 2}_{oh}")
            for gc in range(8):
                P.op("tensor", lambda g, t=t, oh=oh, gc=gc: g.matmul(
                    pb[oh][:], ATT[:, gc * T + t * 128: gc * T + (t + 1) * 128],
                    woutb[:, gc * 1024 + oh * 512: gc * 1024 + (oh + 1) * 512],
                    start=(gc == 0), stop=False),
                    waits=(mw if gc == 0 else []), sem="pe")
            P.op("tensor", lambda g, oh=oh: g.matmul(
                pb[oh][:], ones1b[:], boutb[0:1, oh * 512:(oh + 1) * 512],
                start=False, stop=True),
                waits=[], sig=f"pe_op_{t}_{oh}", sem="pe")
        for oh in range(2):
            P.op("vector", lambda g, t=t, oh=oh: g.scalar_tensor_tensor(
                s_sb[:, oh * 512:(oh + 1) * 512], pb[oh][:], 1.0,
                x2[t % 2][:, oh * 512:(oh + 1) * 512], ALU.mult, ALU.add,
                accum_out=lncol[:, oh: oh + 1]),
                waits=[f"pe_op_{t}_{oh}", f"x2_{t}"]
                + ([f"a_sqs_{t - 1}_{oh}", f"v_y_{t - 1}"] if t >= 1 else []),
                sig=f"v_s_{t}_{oh}", sem="ve")
        P.op("vector", lambda g: g.tensor_tensor(lncol[:, 2:3], lncol[:, 0:1], lncol[:, 1:2], ALU.add),
             waits=[f"v_s_{t}_0", f"v_s_{t}_1"], sig=f"v_la_{t}", sem="ve")
        P.op("vector", lambda g: g.tensor_scalar_mul(lncol[:, 2:3], lncol[:, 2:3], 1.0 / 1024.0),
             waits=[f"v_la_{t}"], sig=f"v_mean_{t}", sem="ve")
        for oh in range(2):
            P.op("scalar", lambda g, oh=oh: g.activation(
                sq2[:, oh * 512:(oh + 1) * 512], s_sb[:, oh * 512:(oh + 1) * 512], AF.Square,
                accum_out=lncol[:, 5 + oh: 6 + oh]),
                waits=[f"v_s_{t}_{oh}"], sig=f"a_sqs_{t}_{oh}", sem="ae")
        P.op("vector", lambda g: g.tensor_tensor(lncol[:, 4:5], lncol[:, 5:6], lncol[:, 6:7], ALU.add),
             waits=[f"a_sqs_{t}_0", f"a_sqs_{t}_1"], sig=f"v_lb_{t}", sem="ve")
        P.op("vector", lambda g: g.tensor_tensor(lncol[:, 3:4], lncol[:, 2:3], lncol[:, 2:3], ALU.mult),
             waits=[f"v_mean_{t}"], sig=f"v_lc_{t}", sem="ve")
        P.op("vector", lambda g: g.tensor_scalar(lncol[:, 4:5], lncol[:, 4:5], 1.0 / 1024.0,
                                                 lncol[:, 3:4], ALU.mult, ALU.subtract),
             waits=[f"v_lb_{t}", f"v_lc_{t}"], sig=f"v_var_{t}", sem="ve")
        P.op("scalar", lambda g: g.activation(lncol[:, 4:5], lncol[:, 4:5], AF.Ln, bias=epsc[:, 0:1]),
             waits=[f"v_var_{t}", "v_eps"], sig=f"a_lnv_{t}", sem="ae")
        P.op("scalar", lambda g: g.activation(lncol[:, 7:8], lncol[:, 4:5], AF.Exp, scale=-0.5),
             waits=[f"a_lnv_{t}"], sig=f"a_rstd_{t}", sem="ae")
        P.op("vector", lambda g, t=t: g.tensor_scalar(
            y32[t % 2][:], s_sb[:], lncol[:, 2:3], lncol[:, 7:8], ALU.subtract, ALU.mult),
            waits=[f"a_rstd_{t}"] + ([f"ywr_{t - 2}"] if t >= 2 else []), sig=f"v_y_{t}", sem="ve")
        bw = [f"v_y_{t}"] + ([f"pe_yt_{t - 1}_7"] if t >= 1 else [])
        P.op("vector", lambda g, t=t: g.tensor_copy(ybf[:], y32[t % 2][:]),
             waits=bw, sig=f"v_ybf_{t}", sem="ve")
        P.dma("sync", lambda g, t=t: g.dma_start(out=scr["y_scr"][t], in_=y32[t % 2][:]),
              waits=[f"v_y_{t}"], sig=f"ywr_{t}", sem=f"yw{t % 2}")
        for gc in range(8):
            bk = 2 + (gc % 2)
            tw = [f"v_ybf_{t}"]
            if (t, gc) > (0, 1):
                pt, pg = (t, gc - 2) if gc >= 2 else (t - 1, gc + 6)
                tw.append(f"a_yt_{pt}_{pg}")
            elif t == 0:
                tw.append(f"v_ot_{NP * NCH - 1}")
                tw += tap_r3
            P.op("tensor", lambda g, t=t, gc=gc, bk=bk: g.transpose(
                pbb[bk][:, 0:128], ybf[:, gc * 128:(gc + 1) * 128], ident[:]),
                waits=tw, sig=f"pe_yt_{t}_{gc}", sem="pe")
            P.op("scalar", lambda g, t=t, gc=gc, bk=bk: g.copy(
                YT[:, gc * T + t * 128: gc * T + (t + 1) * 128], pbb[bk][:, 0:128]),
                waits=[f"pe_yt_{t}_{gc}"], sig=f"a_yt_{t}_{gc}", sem="ae")
        if t % 4 == 3:
            emit_conv(conv_emitted[0] + 24)

    yt_done = "a_yt_31_7"
    if "t_y" in dbg_t:
        P.dma("sync", lambda g: g.dma_start(out=dbg_t["t_y"], in_=scr["y_scr"][:]),
              waits=[f"ywr_{31}", f"ywr_{30}"], sig="tap_y", sem="stp")
    if stop == "phaseB":
        w = [yt_done, "ywr_31", "ywr_30"]
        if "t_y" in dbg_t:
            w.append("tap_y")
        return finish(w)
    P.flush()

    areset("r2")
    WR = aalloc([128, 1024], F32, "r2")
    r2_mark = _bump["r2"]
    scn = aalloc([128, 256], F32, "r2")
    scc = aalloc([128, 256], F32, "r2")
    mskd = aalloc([128, 256], F32, "r2")
    selb = aalloc([128, 256], F32, "r2")
    gsml = aalloc([128, 64], F32, "r2")
    rcol = aalloc([128, 32], F32, "r2")
    biasbc = aalloc([128, 32], F32, "r2")

    P.op("tensor", lambda g: g.matmul(pb[4][:, 0:32], ones1f[:], gbias[:], start=True, stop=True),
         waits=["init_done", yt_done, f"v_ng_{NP * 16 - 1}"] + tap_r2, sig="pe_gb", sem="pe")
    P.op("scalar", lambda g: g.copy(biasbc[:], pb[4][:, 0:32]), waits=["pe_gb"], sig="a_gb", sem="ae")

    def v3(x, i, j):
        return x[:].rearrange("p (a b) -> p a b", a=i, b=j)

    for tg in range(4):
        for j in range(8):
            tile = tg * 8 + j
            mw = [yt_done, "g_gw"]
            if tg == 0 and j == 0:
                mw.append(f"a_poly_{NP * 16 - 1}")
            if tg >= 1:
                mw.append(f"v_wr_{tg - 1}")
            for gc in range(8):
                P.op("tensor", lambda g, tile=tile, j=j, gc=gc: g.matmul(
                    pb[5 + j // 4][:, (j % 4) * 128:(j % 4) * 128 + 32],
                    YT[:, gc * T + tile * 128: gc * T + (tile + 1) * 128],
                    gwb[:, gc * 32:(gc + 1) * 32],
                    start=(gc == 0), stop=(gc == 7)),
                    waits=(mw if gc == 0 else []),
                    sig=(f"pe_lg_{tile}" if gc == 7 else None), sem="pe")
        for j in range(8):
            P.op("scalar", lambda g, j=j: g.activation(
                scn[:, j * 32:(j + 1) * 32], pb[5 + j // 4][:, (j % 4) * 128:(j % 4) * 128 + 32],
                AF.Sigmoid),
                waits=[f"pe_lg_{tg * 8 + (j // 4) * 4 + 3}"], sig=f"a_sg_{tg}_{j}", sem="ae")
        vv = lambda nm: f"{nm}_{tg}"
        gm1, gm2 = gsml[:, 0:32], gsml[:, 32:64]
        P.op("vector", lambda g: g.tensor_tensor(
            v3(scc, 8, 32), v3(scn, 8, 32),
            biasbc[:].unsqueeze(1).broadcast_to([128, 8, 32]), ALU.add),
            waits=[f"a_sg_{tg}_{j}" for j in range(8)] + ["a_gb"], sig=vv("v_sc"), sem="ve")
        P.op("vector", lambda g: g.reduce_max(
            gm1.rearrange("p (t q) -> p t q", q=4), v3(scc, 32, 8), axis=AX.X),
            waits=[vv("v_sc")], sig=vv("v_gm1"), sem="ve")
        P.op("vector", lambda g: g.tensor_tensor(
            v3(mskd, 32, 8), v3(scc, 32, 8),
            gm1.unsqueeze(2).broadcast_to([128, 32, 8]), ALU.is_equal),
            waits=[vv("v_gm1")], sig=vv("v_eq1"), sem="ve")
        P.op("vector", lambda g: g.scalar_tensor_tensor(mskd[:], mskd[:], -1e30, scc[:],
                                                        ALU.mult, ALU.add),
             waits=[vv("v_eq1")], sig=vv("v_t1"), sem="ve")
        P.op("vector", lambda g: g.reduce_max(
            gm2.rearrange("p (t q) -> p t q", q=4), v3(mskd, 32, 8), axis=AX.X),
            waits=[vv("v_t1")], sig=vv("v_gm2"), sem="ve")
        P.op("vector", lambda g: g.tensor_tensor(gm1, gm1, gm2, ALU.add),
             waits=[vv("v_gm2")], sig=vv("v_gs"), sem="ve")
        P.op("vector", lambda g: g.reduce_max(
            rcol[:, 0:8], gm1.rearrange("p (t q) -> p t q", q=4), axis=AX.X),
            waits=[vv("v_gs")], sig=vv("v_g1"), sem="ve")
        P.op("vector", lambda g: g.tensor_tensor(
            gm2.rearrange("p (t q) -> p t q", q=4), gm1.rearrange("p (t q) -> p t q", q=4),
            rcol[:, 0:8].unsqueeze(2).broadcast_to([128, 8, 4]), ALU.is_equal),
            waits=[vv("v_g1")], sig=vv("v_eqg"), sem="ve")
        P.op("vector", lambda g: g.scalar_tensor_tensor(gm2, gm2, -1e30, gm1, ALU.mult, ALU.add),
             waits=[vv("v_eqg")], sig=vv("v_tg"), sem="ve")
        P.op("vector", lambda g: g.reduce_max(
            rcol[:, 8:16], gm2.rearrange("p (t q) -> p t q", q=4), axis=AX.X),
            waits=[vv("v_tg")], sig=vv("v_g2"), sem="ve")
        P.op("vector", lambda g: g.tensor_tensor(
            gm2.rearrange("p (t q) -> p t q", q=4), gm1.rearrange("p (t q) -> p t q", q=4),
            rcol[:, 8:16].unsqueeze(2).broadcast_to([128, 8, 4]), ALU.is_ge),
            waits=[vv("v_g2")], sig=vv("v_gmask"), sem="ve")
        P.op("vector", lambda g: g.tensor_tensor(
            v3(mskd, 32, 8), v3(scc, 32, 8),
            gm2.unsqueeze(2).broadcast_to([128, 32, 8]), ALU.mult),
            waits=[vv("v_gmask")], sig=vv("v_msk"), sem="ve")
        P.op("vector", lambda g: g.memset(selb[:], 0.0),
             waits=([f"v_wr_{tg - 1}"] if tg >= 1 else []), sig=vv("v_sel0"), sem="ve")
        prev = vv("v_msk")
        for it in range(8):
            P.op("vector", lambda g: g.reduce_max(rcol[:, 16:24], v3(mskd, 8, 32), axis=AX.X),
                 waits=[prev], sig=vv(f"v_m8_{it}"), sem="ve")
            P.op("vector", lambda g: g.tensor_tensor(
                v3(scc, 8, 32), v3(mskd, 8, 32),
                rcol[:, 16:24].unsqueeze(2).broadcast_to([128, 8, 32]), ALU.is_equal),
                waits=[vv(f"v_m8_{it}")], sig=vv(f"v_eq8_{it}"), sem="ve")
            P.op("vector", lambda g: g.tensor_tensor(selb[:], selb[:], scc[:], ALU.add),
                 waits=[vv(f"v_eq8_{it}"), vv("v_sel0")], sig=vv(f"v_sa_{it}"), sem="ve")
            P.op("vector", lambda g: g.scalar_tensor_tensor(mskd[:], scc[:], -1e30, mskd[:],
                                                            ALU.mult, ALU.add),
                 waits=[vv(f"v_sa_{it}")], sig=vv(f"v_mu_{it}"), sem="ve")
            prev = vv(f"v_mu_{it}")
        P.op("vector", lambda g: g.tensor_tensor(selb[:], selb[:], scn[:], ALU.mult),
             waits=[prev], sig=vv("v_w"), sem="ve")
        P.op("vector", lambda g: g.reduce_sum(rcol[:, 24:32], v3(selb, 8, 32), axis=AX.X),
             waits=[vv("v_w")], sig=vv("v_ws"), sem="ve")
        P.op("vector", lambda g: g.tensor_scalar_add(rcol[:, 24:32], rcol[:, 24:32], 1e-20),
             waits=[vv("v_ws")], sig=vv("v_pa"), sem="ve")
        P.op("vector", lambda g: g.reciprocal(rcol[:, 16:24], rcol[:, 24:32]),
             waits=[vv("v_pa")], sig=vv("v_pb"), sem="ve")
        P.op("vector", lambda g: g.tensor_scalar_mul(rcol[:, 16:24], rcol[:, 16:24], 2.5),
             waits=[vv("v_pb")], sig=vv("v_rs"), sem="ve")
        P.op("vector", lambda g, tg=tg: g.tensor_tensor(
            WR[:, tg * 256:(tg + 1) * 256].rearrange("p (t e) -> p t e", e=32),
            v3(selb, 8, 32),
            rcol[:, 16:24].unsqueeze(2).broadcast_to([128, 8, 32]), ALU.mult),
            waits=[vv("v_rs")], sig=f"v_wr_{tg}", sem="ve")

    if "t_wroute" in dbg_t:
        P.dma("sync", lambda g: g.dma_start(out=dbg_t["t_wroute"], in_=WR[:]),
              waits=["v_wr_3"], sig="tap_wr", sem="stp")
    emit_conv(NCONV)
    if stop == "phaseC":
        w = ["v_wr_3"]
        if "t_wroute" in dbg_t:
            w.append("tap_wr")
        return finish(w)
    P.flush()

    areset("r1")
    areset("r2", keep=r2_mark - _rbase["r2"])
    ACC = aalloc([128, 16 * 1024], F32, "r1")
    WGb = [aalloc([128, 4096], BF16, "r2") for i in range(2)]
    WUb = [aalloc([128, 4096], BF16, "r2") for i in range(2)]
    WDb = [aalloc([128, 4096], BF16, "r2") for i in range(2)]
    Hb = [aalloc([128, 2048], BF16, "r2") for i in range(2)]
    SGb = aalloc([128, 512], F32, "r2")

    n_tb = 4
    outr = out.rearrange("(n p) c -> p n c", p=128)
    for hf in range(2):
        aw = [f"ywr_{31}", f"ywr_{30}", yt_done]
        if hf == 1:
            aw.append("out_0")
        P.dma("sync", lambda g, hf=hf: g.dma_start(
            out=ACC[:].rearrange("p (n c) -> p n c", c=1024),
            in_=scr["y_scr"].rearrange("n p c -> p n c")[:, hf * 16:(hf + 1) * 16, :]),
            waits=aw, sig=f"accld_{hf}", sem="sac")
        for e in range(NE):
            pbuf = e % 2
            k = hf * NE + e
            ew = exp_ready(e)[:]
            if k == 0:
                ew.append("v_wr_3")
            if k >= 2:
                ew.append(f"pe_dn_{k - 2}_{n_tb - 1}_7")
            P.dma("sync", lambda g, e=e, pbuf=pbuf: g.dma_start(out=WGb[pbuf][:], in_=scr["eg_scr"][e]),
                  waits=ew, sig=f"wg_{hf}_{e}", sem=f"eb{e % 2}")
            P.dma("sync", lambda g, e=e, pbuf=pbuf: g.dma_start(out=WUb[pbuf][:], in_=scr["eu_scr"][e]),
                  waits=[], sig=f"wu_{hf}_{e}", sem=f"eb{e % 2}")
            P.dma("sync", lambda g, e=e, pbuf=pbuf: g.dma_start(out=WDb[pbuf][:], in_=scr["ed_scr"][e]),
                  waits=[], sig=f"wd_{hf}_{e}", sem=f"eb{e % 2}")
            for tb in range(n_tb):
                tbk = k * n_tb + tb
                for it in range(4):
                    ki = tbk * 4 + it
                    gbk, ubk = it % 2, 2 + it % 2
                    gw_ = [f"wd_{hf}_{e}", yt_done]
                    if ki >= 2:
                        gw_.append(f"v_h_{ki - 2}")
                    if ki == 0:
                        gw_ += ["v_s_31_0", "v_s_31_1", "a_yt_31_6", "a_yt_31_7"]
                    for gc in range(8):
                        P.op("tensor", lambda g, hf=hf, tb=tb, it=it, gc=gc, pbuf=pbuf, gbk=gbk: g.matmul(
                            pb[gbk][:], WGb[pbuf][:, gc * 512 + it * 128: gc * 512 + it * 128 + 128],
                            YT[:, gc * T + (hf * 4 + tb) * 512: gc * T + (hf * 4 + tb + 1) * 512],
                            start=(gc == 0), stop=(gc == 7)),
                            waits=(gw_ if gc == 0 else []),
                            sig=(f"pe_g_{ki}" if gc == 7 else None), sem="pe")
                    for gc in range(8):
                        P.op("tensor", lambda g, hf=hf, tb=tb, it=it, gc=gc, pbuf=pbuf, ubk=ubk: g.matmul(
                            pb[ubk][:], WUb[pbuf][:, gc * 512 + it * 128: gc * 512 + it * 128 + 128],
                            YT[:, gc * T + (hf * 4 + tb) * 512: gc * T + (hf * 4 + tb + 1) * 512],
                            start=(gc == 0), stop=(gc == 7)),
                            waits=[], sig=(f"pe_u_{ki}" if gc == 7 else None), sem="pe")
                    sw = [f"pe_g_{ki}"]
                    if ki >= 1:
                        sw.append(f"v_h_{ki - 1}")
                    P.op("scalar", lambda g, gbk=gbk: g.activation(SGb[:], pb[gbk][:], AF.Silu),
                         waits=sw, sig=f"a_sil_{ki}", sem="ae")
                    hw = [f"a_sil_{ki}", f"pe_u_{ki}"]
                    if tbk >= 2 and it == 0:
                        pk2 = tbk - 2
                        hw.append(f"pe_dn_{pk2 // n_tb}_{pk2 % n_tb}_7")
                    P.op("vector", lambda g, tb=tb, it=it, ubk=ubk: g.tensor_tensor(
                        Hb[tb % 2][:, it * 512:(it + 1) * 512], SGb[:], pb[ubk][:], ALU.mult),
                        waits=hw, sig=f"v_h_{ki}", sem="ve")
                for ts_ in range(4):
                    for oh in range(2):
                        nd = ts_ * 2 + oh
                        dbk = 4 + nd % 4
                        ndk = tbk * 8 + nd
                        dw = [f"v_h_{tbk * 4 + 3}", f"wd_{hf}_{e}"]
                        if ndk >= 4:
                            dw.append(f"v_dn_{ndk - 4}")
                        if ndk < 4:
                            dw += ["a_sg_3_3", "a_sg_3_7", "a_gb"]
                        for ic in range(4):
                            P.op("tensor", lambda g, tb=tb, ts_=ts_, oh=oh, ic=ic, pbuf=pbuf, dbk=dbk: g.matmul(
                                pb[dbk][:], Hb[tb % 2][:, ic * 512 + ts_ * 128: ic * 512 + ts_ * 128 + 128],
                                WDb[pbuf][:, ic * 1024 + oh * 512: ic * 1024 + oh * 512 + 512],
                                start=(ic == 0), stop=(ic == 3)),
                                waits=(dw if ic == 0 else []),
                                sig=(f"pe_dn_{k}_{tb}_{nd}" if ic == 3 else None), sem="pe")
                        vw = [f"pe_dn_{k}_{tb}_{nd}"]
                        if e == 0 and tb == 0:
                            vw.append(f"accld_{hf}")
                        if e >= 1:
                            vw.append(f"v_dn_{ndk - n_tb * 8}")
                        if e >= 2:
                            vw.append(f"v_wr_{(hf * 16 + tb * 4 + ts_) // 8}")
                        gtile = tb * 4 + ts_
                        a_sl = lambda gtile=gtile, oh=oh: ACC[:, gtile * 1024 + oh * 512: gtile * 1024 + (oh + 1) * 512]
                        if e < 2:
                            P.op("vector", lambda g, a_sl=a_sl, dbk=dbk: g.scalar_tensor_tensor(
                                a_sl(), pb[dbk][:], 1.0, a_sl(), ALU.mult, ALU.add),
                                waits=vw, sig=f"v_dn_{ndk}", sem="ve")
                        else:
                            P.op("vector", lambda g, a_sl=a_sl, dbk=dbk, hf=hf, e=e, gtile=gtile: g.scalar_tensor_tensor(
                                a_sl(), pb[dbk][:],
                                WR[:, (hf * 16 + gtile) * 32 + (e - 2): (hf * 16 + gtile) * 32 + (e - 1)],
                                a_sl(), ALU.mult, ALU.add),
                                waits=vw, sig=f"v_dn_{ndk}", sem="ve")
        last_dn = f"v_dn_{((hf * NE + NE - 1) * n_tb + n_tb - 1) * 8 + 7}"
        for tt in range(16):
            t = hf * 16 + tt
            a0 = ACC[:, tt * 1024: tt * 1024 + 512]
            a1 = ACC[:, tt * 1024 + 512: tt * 1024 + 1024]
            s0w = [last_dn]
            if tt >= 1:
                s0w += [f"a2_sq_{t - 1}_1", f"v2_y_{t - 1}"]
            P.op("vector", lambda g, a0=a0: g.tensor_scalar(
                SGb[:], a0, 0.0, None, ALU.add, ALU.add, accum_out=lncol[:, 0:1]),
                waits=s0w, sig=f"v2_s0_{t}", sem="ve")
            P.op("vector", lambda g, a1=a1: g.tensor_scalar(
                SGb[:], a1, 0.0, None, ALU.add, ALU.add, accum_out=lncol[:, 1:2]),
                waits=[f"v2_s0_{t}"], sig=f"v2_s1_{t}", sem="ve")
            P.op("vector", lambda g: g.tensor_tensor(lncol[:, 2:3], lncol[:, 0:1], lncol[:, 1:2], ALU.add),
                 waits=[f"v2_s0_{t}", f"v2_s1_{t}"], sig=f"v2_la_{t}", sem="ve")
            P.op("vector", lambda g: g.tensor_scalar_mul(lncol[:, 2:3], lncol[:, 2:3], 1.0 / 1024.0),
                 waits=[f"v2_la_{t}"], sig=f"v2_mean_{t}", sem="ve")
            for oh in range(2):
                P.op("scalar", lambda g, tt=tt, oh=oh: g.activation(
                    SGb[:],
                    ACC[:, tt * 1024 + oh * 512: tt * 1024 + (oh + 1) * 512],
                    AF.Square, accum_out=lncol[:, 5 + oh: 6 + oh]),
                    waits=[f"v2_s1_{t}"] + ([f"a2_sq_{t}_0"] if oh == 1 else []),
                    sig=f"a2_sq_{t}_{oh}", sem="ae")
            P.op("vector", lambda g: g.tensor_tensor(lncol[:, 4:5], lncol[:, 5:6], lncol[:, 6:7], ALU.add),
                 waits=[f"a2_sq_{t}_0", f"a2_sq_{t}_1"], sig=f"v2_lb_{t}", sem="ve")
            P.op("vector", lambda g: g.tensor_tensor(lncol[:, 3:4], lncol[:, 2:3], lncol[:, 2:3], ALU.mult),
                 waits=[f"v2_mean_{t}"], sig=f"v2_lc_{t}", sem="ve")
            P.op("vector", lambda g: g.tensor_scalar(lncol[:, 4:5], lncol[:, 4:5], 1.0 / 1024.0,
                                                     lncol[:, 3:4], ALU.mult, ALU.subtract),
                 waits=[f"v2_lb_{t}", f"v2_lc_{t}"], sig=f"v2_var_{t}", sem="ve")
            P.op("scalar", lambda g: g.activation(lncol[:, 4:5], lncol[:, 4:5], AF.Ln, bias=epsc[:, 0:1]),
                 waits=[f"v2_var_{t}", "v_eps"], sig=f"a2_lnv_{t}", sem="ae")
            P.op("scalar", lambda g: g.activation(lncol[:, 7:8], lncol[:, 4:5], AF.Exp, scale=-0.5),
                 waits=[f"a2_lnv_{t}"], sig=f"a2_rstd_{t}", sem="ae")
            P.op("vector", lambda g, tt=tt: g.tensor_scalar(
                ACC[:, tt * 1024:(tt + 1) * 1024], ACC[:, tt * 1024:(tt + 1) * 1024],
                lncol[:, 2:3], lncol[:, 7:8], ALU.subtract, ALU.mult),
                waits=[f"a2_rstd_{t}"], sig=f"v2_y_{t}", sem="ve")
        P.dma("sync", lambda g, hf=hf: g.dma_start(
            out=outr[:, hf * 16:(hf + 1) * 16, :],
            in_=ACC[:].rearrange("p (n c) -> p n c", c=1024)),
            waits=[f"v2_y_{hf * 16 + tt2}" for tt2 in range(16)], sig=f"out_{hf}", sem="sout")

    fw = ["out_0", "out_1"]
    if P.counts["stp"] > 0:
        P.alias("taps_all", "stp")
        fw.append("taps_all")
    P.op("sync", None, waits=fw)
    P.flush()

    stack.close()
    return nc, dbg_t


def host_prep(ins):
    f32 = np.float32
    w_qkv = np.asarray(ins["L0_attn_w_qkv"], f32)
    wq = w_qkv[0:1024].reshape(16, 64, 1024)
    wk = w_qkv[1024:2048].reshape(16, 64, 1024)
    wv = w_qkv[2048:3072].reshape(16, 64, 1024)
    blocks = []
    for p in range(NP):
        h0, h1 = 2 * p, 2 * p + 1
        cols = np.concatenate([wq[h0].T, wq[h1].T, wk[h0].T, wk[h1].T,
                               wv[h0].T, wv[h1].T], axis=1)
        blocks.append(cols)
    wqkv_src = np.stack(blocks).reshape(NP, 8, 128, 384).transpose(0, 2, 1, 3).reshape(NP, 128, 3072)

    woutT = np.asarray(ins["L0_attn_w_out"], f32).T
    wout_src = woutT.reshape(8, 128, 1024).transpose(1, 0, 2).reshape(128, 8192)

    sgT = np.asarray(ins["L0_ffn_shared_gate"], f32).T
    suT = np.asarray(ins["L0_ffn_shared_up"], f32).T
    sdT = np.asarray(ins["L0_ffn_shared_down"], f32).T
    egT = np.transpose(np.asarray(ins["L0_ffn_expert_gate"], f32), (0, 2, 1))
    euT = np.transpose(np.asarray(ins["L0_ffn_expert_up"], f32), (0, 2, 1))
    edT = np.transpose(np.asarray(ins["L0_ffn_expert_down"], f32), (0, 2, 1))
    eg_full = np.concatenate([sgT[None, :, 0:512], sgT[None, :, 512:1024], egT], 0)
    eu_full = np.concatenate([suT[None, :, 0:512], suT[None, :, 512:1024], euT], 0)
    ed_full = np.concatenate([sdT[None, 0:512, :], sdT[None, 512:1024, :], edT], 0)
    eg_src = eg_full.reshape(NE, 8, 128, 512).transpose(0, 2, 1, 3).reshape(NE, 128, 4096)
    eu_src = eu_full.reshape(NE, 8, 128, 512).transpose(0, 2, 1, 3).reshape(NE, 128, 4096)
    ed_src = ed_full.reshape(NE, 4, 128, 1024).transpose(0, 2, 1, 3).reshape(NE, 128, 4096)

    gw_src = np.asarray(ins["L0_ffn_gate_w"], f32).T.reshape(8, 128, 32).transpose(1, 0, 2).reshape(128, 256)

    omega = np.asarray(ins["L0_attn_omega"], f32)
    omcat = np.transpose(omega, (1, 2, 0, 3)).reshape(16, 64, 64)
    omega_src = np.zeros((128, 1024), f32)
    for p in range(NP):
        omega_src[0:64, p * 128: p * 128 + 64] = omcat[2 * p]
        omega_src[64:128, p * 128 + 64: p * 128 + 128] = omcat[2 * p + 1]

    b_qkv = np.asarray(ins["L0_attn_b_qkv"], f32)
    bq = b_qkv[0:1024].reshape(16, 64)
    bk = b_qkv[1024:2048].reshape(16, 64)
    bv = b_qkv[2048:3072].reshape(16, 64)
    cols = []
    for p in range(NP):
        cols.append(np.concatenate([bq[2 * p], bq[2 * p + 1]]))
        cols.append(np.concatenate([bk[2 * p], bk[2 * p + 1]]))
        cols.append(np.concatenate([bv[2 * p], bv[2 * p + 1]]))
    bqkv_pp = np.stack(cols, axis=1)

    qn = np.asarray(ins["L0_attn_quad_nodes"], f32)
    qw = np.asarray(ins["L0_attn_quad_weights"], f32)
    pat = np.concatenate([np.repeat(qn, 32), np.repeat(qn, 32)])
    patw = np.concatenate([np.repeat(qw, 32), np.repeat(qw, 32)])
    quad_pp = np.stack([pat, patw], axis=1).astype(f32)

    bf = ml_dtypes.bfloat16
    blkdiag = np.zeros((128, 128), f32)
    blkdiag[0:64, 0:64] = 1.0
    blkdiag[64:128, 64:128] = 1.0
    common = dict(
        wqkv_src=np.ascontiguousarray(wqkv_src),
        wout_src=np.ascontiguousarray(wout_src),
        eg_src=np.ascontiguousarray(eg_src),
        eu_src=np.ascontiguousarray(eu_src),
        ed_src=np.ascontiguousarray(ed_src),
        gw_src=np.ascontiguousarray(gw_src),
        omega_src=omega_src,
        bqkv_pp=np.ascontiguousarray(bqkv_pp),
        bout_src=np.asarray(ins["L0_attn_b_out"], f32).reshape(1, 1024),
        quad_pp=quad_pp,
        gbias_src=np.asarray(ins["L0_ffn_gate_bias"], f32).reshape(1, 32),
        ident_bf=np.eye(128, dtype=bf),
        maskrep_bf=np.tile(np.triu(np.ones((128, 128), np.float32)), (1, 2)).astype(bf),
        blkdiag_bf=blkdiag.astype(bf),
        ones1_bf=np.ones((1, 128), bf),
        ones1_f=np.ones((1, 128), f32),
    )
    return common


def kernel(**ins):
    common = host_prep(ins)
    x_all = np.asarray(ins["L0_attn_x"], np.float32)
    nc, _ = build()
    in_maps = [dict(common, x=np.ascontiguousarray(x_all[b])) for b in range(8)]
    res = run_bass_kernel_spmd(nc, in_maps, core_ids=list(range(8)))
    return np.stack([res.results[b]["out"] for b in range(8)], axis=0)


# version 04
# speedup vs baseline: 1.2375x; 1.2375x over previous
import numpy as np
import ml_dtypes
import concourse.bass as bass
import concourse.mybir as mybir
from concourse.bass_utils import run_bass_kernel_spmd

F32 = mybir.dt.float32
BF16 = mybir.dt.bfloat16
AF = mybir.ActivationFunctionType
ALU = mybir.AluOpType
AX = mybir.AxisListType

T = 4096
NP = 8
NCH = 32
NE = 34
CONV_CHUNK = 256

ENGINES = ("sync", "tensor", "vector", "scalar", "gpsimd")


class Prog:

    def __init__(self, nc, sems):
        self.nc = nc
        self.sems = sems
        self.counts = {k: 0 for k in sems}
        self.ev = {}
        self.pending = {e: [] for e in ENGINES}
        self.maxwait = {e: {} for e in ENGINES}

    def op(self, eng, fn, waits=(), sig=None, sem=None, amt=1):
        for w in waits:
            assert w in self.ev, f"wait on undefined event {w}"
        if sig is not None:
            assert sem is not None
            self.counts[sem] += amt
            assert sig not in self.ev, f"event {sig} redefined"
            self.ev[sig] = (sem, self.counts[sem])
        self.pending[eng].append((fn, tuple(waits), sig, sem, amt))

    def dma(self, eng, fn, waits=(), sig=None, sem=None):
        assert sem is not None
        self.op(eng, fn, waits, sig, sem, amt=16)

    def alias(self, name, sem):
        assert name not in self.ev
        self.ev[name] = (sem, self.counts[sem])

    def flush(self):
        nc = self.nc
        if not any(self.pending.values()):
            return
        with nc.Block() as block:
            for eng_name in ENGINES:
                ops = self.pending[eng_name]
                if not ops:
                    continue
                mw = self.maxwait[eng_name]

                def emit(eng, ops=ops, mw=mw):
                    for fn, waits, sig, sem, amt in ops:
                        for w in waits:
                            wsem, wval = self.ev[w]
                            if wval <= mw.get(wsem, 0):
                                continue
                            mw[wsem] = wval
                            eng.wait_ge(self.sems[wsem], wval)
                        instr = fn(eng) if fn is not None else None
                        if sig is not None:
                            instr.then_inc(self.sems[sem], amt)
                getattr(block, eng_name)(emit)
        self.pending = {e: [] for e in ENGINES}


def build(dbg=(), stop=None):
    nc = bass.Bass("TRN2", target_bir_lowering=False)
    d = {}

    def din(name, shape, dt=F32):
        d[name] = nc.dram_tensor(name, shape, dt, kind="ExternalInput").ap()
    din("x", [T, 1024])
    din("wqkv_src", [NP, 128, 3072])
    din("wout_src", [128, 8192])
    din("eg_src", [NE, 128, 4096])
    din("eu_src", [NE, 128, 4096])
    din("ed_src", [NE, 128, 4096])
    din("gw_src", [128, 256])
    din("omega_src", [128, 1024])
    din("bqkv_pp", [128, 24])
    din("bout_src", [1, 1024])
    din("quad_pp", [128, 2])
    din("gbias_src", [1, 32])
    din("ident_bf", [128, 128], BF16)
    din("maskrep_bf", [128, 256], BF16)
    din("blkdiag_bf", [128, 128], BF16)
    din("ones1_bf", [1, 128], BF16)
    din("ones1_f", [1, 128])

    out = nc.dram_tensor("out", [T, 1024], F32, kind="ExternalOutput").ap()
    dbg_t = {}
    dbg_spec = {"t_attnT": ([128, NP * T], BF16), "t_y": ([32, 128, 1024], F32),
                "t_wroute": ([128, 1024], F32), "t_qf": ([128, T], BF16),
                "t_kf": ([128, T], BF16), "t_vt": ([128, T], BF16)}
    for nm in dbg:
        shape, dt = dbg_spec[nm]
        dbg_t[nm] = nc.dram_tensor(nm, shape, dt, kind="ExternalOutput").ap()

    scr = {}
    def dscr(name, shape, dt=BF16):
        scr[name] = nc.dram_tensor(name, shape, dt, kind="Internal").ap()
    dscr("y_scr", [32, 128, 1024], F32)

    from contextlib import ExitStack
    stack = ExitStack()
    e_ = stack.enter_context

    sems = {k: e_(nc.semaphore(k)) for k in (
        "pe", "ve", "ae", "gp", "si", "sx0", "sx1", "cv0", "cv1", "gq0", "gq1",
        "wp", "swo", "yw0", "yw1", "eb0", "eb1", "sac", "sout", "stp")}
    P = Prog(nc, sems)

    sbg = {}
    def alloc(name, shape, dt):
        g = nc.sbuf_tensor(name, shape, dt)
        h = g.__enter__()
        sbg[name] = g
        return h

    U8 = mybir.dt.uint8
    RKB = 64 * 1024
    arena = alloc("arena", [128, 3 * RKB], U8)
    _rbase = {"r1": 0, "r2": RKB, "r3": 2 * RKB}
    _bump = dict(_rbase)

    def aalloc(shape, dt, region):
        assert shape[0] == 128
        esz = 4 if dt == F32 else 2
        nbytes = shape[1] * esz
        off = (_bump[region] + 511) // 512 * 512
        assert off + nbytes <= _rbase[region] + RKB, (region, off, nbytes)
        _bump[region] = off + nbytes
        return arena[:, off: off + nbytes].bitcast(dt)

    def areset(region, keep=0):
        _bump[region] = _rbase[region] + keep

    ident = alloc("ident", [128, 128], BF16)
    maskr = alloc("maskr", [128, 256], BF16)
    blkd = alloc("blkd", [128, 128], BF16)
    ones1b = alloc("ones1b", [1, 128], BF16)
    ones1f = alloc("ones1f", [1, 128], F32)
    bqkv = alloc("bqkv", [128, 24], F32)
    quad = alloc("quad", [128, 2], F32)
    qs2s = alloc("qs2s", [128, 1], F32)
    qnegs = alloc("qnegs", [128, 1], F32)
    qspp = alloc("qspp", [128, 1], F32)
    omg = alloc("omg", [128, 1024], BF16)
    gwb = alloc("gwb", [128, 256], BF16)
    boutb = alloc("boutb", [1, 1024], BF16)
    gbias = alloc("gbias", [1, 32], F32)
    lncol = alloc("lncol", [128, 8], F32)
    epsc = alloc("epsc", [128, 1], F32)

    pb = [e_(nc.psum_tensor(f"pb{i}", [128, 512], F32)) for i in range(8)]
    pbb = [b[:].bitcast(BF16) for b in pb]

    XT = aalloc([128, 8 * T], BF16, "r1")
    ATT = aalloc([128, NP * T], BF16, "r2")
    wpair = aalloc([128, 3072], BF16, "r3")
    r3_mark = _bump["r3"]
    xld = [aalloc([128, 1024], F32, "r3") for i in range(2)]
    xbf = aalloc([128, 1024], BF16, "r3")

    for nm, dst in (("ident_bf", ident), ("maskrep_bf", maskr), ("blkdiag_bf", blkd),
                    ("ones1_bf", ones1b), ("ones1_f", ones1f), ("bqkv_pp", bqkv),
                    ("quad_pp", quad), ("gbias_src", gbias)):
        P.dma("sync", lambda g, nm=nm, dst=dst: g.dma_start(out=dst[:], in_=d[nm]),
              sig=f"ld_{nm}", sem="si")

    P.alias("init_done", "si")
    P.op("vector", lambda g: g.memset(epsc[:], 1e-5), sig="v_eps", sem="ve")
    P.op("vector", lambda g: g.tensor_scalar_max(qs2s[:], quad[:, 0:1], 0.0),
         waits=["init_done"], sig="v_q1", sem="ve")
    P.op("scalar", lambda g: g.activation(qs2s[:], qs2s[:], AF.Sqrt, scale=2.0),
         waits=["v_q1"], sig="a_q1", sem="ae")
    P.op("vector", lambda g: g.tensor_scalar_mul(qnegs[:], quad[:, 0:1], -1.0),
         waits=["init_done"], sig="v_q2", sem="ve")
    P.op("vector", lambda g: g.tensor_scalar_max(qspp[:], quad[:, 1:2], 0.0),
         waits=["init_done"], sig="v_q3", sem="ve")
    P.op("scalar", lambda g: g.activation(qspp[:], qspp[:], AF.Sqrt), waits=["v_q3"], sig="a_q2", sem="ae")
    P.op("scalar", lambda g: g.activation(qspp[:], qspp[:], AF.Sqrt), waits=["a_q2"], sig="a_q3", sem="ae")
    P.op("vector", lambda g: g.tensor_scalar_mul(qspp[:], qspp[:], 32.0 ** -0.5),
         waits=["a_q3"], sig="v_q4", sem="ve")

    P.dma("gpsimd", lambda g: g.dma_start(out=omg[:], in_=d["omega_src"]),
          sig="g_omg_raw", sem="cv0")
    P.dma("gpsimd", lambda g: g.dma_start(out=gwb[:], in_=d["gw_src"]),
          sig="g_gw_raw", sem="cv0")
    P.dma("gpsimd", lambda g: g.dma_start(out=boutb[:], in_=d["bout_src"]),
          sig="g_bout_raw", sem="cv0")
    P.alias("g_omg", "cv0")
    P.alias("g_gw", "cv0")
    P.alias("g_bout", "cv0")

    def finish(waits):
        P.op("sync", None, waits=list(waits))
        P.flush()
        stack.close()
        return nc, dbg_t

    xr = d["x"].rearrange("(n p) c -> n p c", p=128)
    for t in range(32):
        lw = [f"v_xbf_{t - 2}"] if t >= 2 else []
        P.dma("sync", lambda g, t=t: g.dma_start(out=xld[t % 2][:], in_=xr[t]),
              waits=lw, sig=f"xld_{t}", sem=f"sx{t % 2}")
        bw = [f"xld_{t}"] + ([f"pe_x_{t - 1}_7"] if t >= 1 else [])
        P.op("vector", lambda g, t=t: g.tensor_copy(xbf[:], xld[t % 2][:]),
             waits=bw, sig=f"v_xbf_{t}", sem="ve")
        for gc in range(8):
            bk = 6 + (gc % 2)
            tw = [f"v_xbf_{t}", "init_done"]
            if (t, gc) > (0, 1):
                pt, pg = (t, gc - 2) if gc >= 2 else (t - 1, gc + 6)
                tw.append(f"a_x_{pt}_{pg}")
            P.op("tensor", lambda g, t=t, gc=gc, bk=bk: g.transpose(
                pbb[bk][:, 0:128], xbf[:, gc * 128:(gc + 1) * 128], ident[:]),
                waits=tw, sig=f"pe_x_{t}_{gc}", sem="pe")
            P.op("scalar", lambda g, t=t, gc=gc, bk=bk: g.copy(
                XT[:, gc * T + t * 128: gc * T + (t + 1) * 128], pbb[bk][:, 0:128]),
                waits=[f"pe_x_{t}_{gc}"], sig=f"a_x_{t}_{gc}", sem="ae")

    if stop == "phase0":
        return finish(["a_x_31_7"])

    areset("r3", keep=r3_mark - _rbase["r3"])
    qkf = [aalloc([128, T], BF16, "r3") for i in range(2)]
    vT = aalloc([128, T], BF16, "r3")
    qkts = [aalloc([128, 512], BF16, "r3") for i in range(2)]
    sqs = aalloc([128, 512], BF16, "r3")
    nrm2g = [aalloc([128, 512], F32, "r3") for i in range(2)]
    rinvbc = [aalloc([128, 512], F32, "r3") for i in range(2)]
    znT = aalloc([128, 512], BF16, "r3")
    poly = aalloc([128, 512], F32, "r3")
    Vt = aalloc([128, NCH * 130], BF16, "r3")
    Ktok = [aalloc([128, 128], BF16, "r3") for i in range(2)]
    upd = aalloc([128, 2 * 260], F32, "r3")
    stbf = aalloc([128, NCH * 65], BF16, "r3")
    stf = aalloc([128, 65], F32, "r3")
    dcol = aalloc([128, 128], F32, "r3")
    smk = [aalloc([128, 256], BF16, "r3") for i in range(2)]
    otok = [aalloc([128, 128], BF16, "r3") for i in range(2)]

    P.op("gpsimd", lambda g: g.memset(
        Vt[:].rearrange("p (ch j w) -> p ch j w", ch=NCH, j=2)[:, :, :, 64:65], 1.0),
        sig="g_vones", sem="gp")

    for p in range(NP):
        pw = []
        if p >= 1:
            pw.append(f"pe_vmm_{p - 1}_7")
        P.dma("gpsimd", lambda g, p=p: g.dma_start(out=wpair[:], in_=d["wqkv_src"][p]),
              waits=pw, sig=f"wp_{p}", sem="wp")

        for tb in range(8):
            for qk in range(2):
                fi = (p * 8 + tb) * 2 + qk
                mmw = [f"wp_{p}", "a_x_31_7"]
                if fi >= 2:
                    mmw.append(f"a_qke_{fi - 2}")
                if tb == 0 and qk == 0 and p >= 1:
                    mmw.append(f"v_smk_{p * NCH - 1}")
                for gc in range(8):
                    P.op("tensor", lambda g, p=p, tb=tb, qk=qk, gc=gc: g.matmul(
                        pb[qk][:], wpair[:, gc * 384 + qk * 128: gc * 384 + qk * 128 + 128],
                        XT[:, gc * T + tb * 512: gc * T + (tb + 1) * 512],
                        start=(gc == 0), stop=(gc == 7)),
                        waits=(mmw if gc == 0 else []),
                        sig=(f"pe_qk_{fi}" if gc == 7 else None), sem="pe")
            vw = [f"wp_{p}", "a_x_31_7"]
            vi = p * 8 + tb
            if vi >= 2:
                vw.append(f"a_ve_{vi - 2}")
            if tb == 0 and p >= 1:
                vw.append(f"v_ot_{p * NCH - 1}")
            vbk = 2 + (tb % 2)
            for gc in range(8):
                P.op("tensor", lambda g, p=p, tb=tb, gc=gc, vbk=vbk: g.matmul(
                    pb[vbk][:], wpair[:, gc * 384 + 256: gc * 384 + 384],
                    XT[:, gc * T + tb * 512: gc * T + (tb + 1) * 512],
                    start=(gc == 0), stop=(gc == 7)),
                    waits=(vw if gc == 0 else []),
                    sig=(f"pe_vmm_{p}_{tb}" if gc == 7 else None), sem="pe")
            P.op("scalar", lambda g, p=p, tb=tb, vbk=vbk: g.activation(
                vT[:, tb * 512:(tb + 1) * 512], pb[vbk][:], AF.Identity,
                bias=bqkv[:, 3 * p + 2: 3 * p + 3]),
                waits=[f"pe_vmm_{p}_{tb}", "init_done"], sig=f"a_ve_{vi}", sem="ae")

            for qk in range(2):
                fi = (p * 8 + tb) * 2 + qk
                ew = [f"pe_qk_{fi}", "init_done"]
                if fi >= 2:
                    ew.append(f"v_zn_{fi - 2}")
                P.op("scalar", lambda g, p=p, qk=qk: g.activation(
                    qkts[qk][:], pb[qk][:], AF.Identity, bias=bqkv[:, 3 * p + qk: 3 * p + qk + 1]),
                    waits=ew, sig=f"a_qke_{fi}", sem="ae")
                fb = fi % 2
                sw = [f"a_qke_{fi}"]
                if fi >= 1:
                    sw.append(f"pe_n_{fi - 1}")
                P.op("scalar", lambda g, qk=qk: g.activation(sqs[:], qkts[qk][:], AF.Square),
                     waits=sw, sig=f"a_sq_{fi}", sem="ae")
                nw = [f"a_sq_{fi}", "init_done"]
                if fi >= 1:
                    nw.append(f"v_ng_{fi - 1}")
                P.op("tensor", lambda g: g.matmul(pb[4][:], blkd[:], sqs[:], start=True, stop=True),
                     waits=nw, sig=f"pe_n_{fi}", sem="pe")
                gw_ = [f"pe_n_{fi}"]
                if fi >= 2:
                    gw_.append(f"a_prf_{fi - 2}")
                P.op("vector", lambda g, fb=fb: g.tensor_scalar_max(nrm2g[fb][:], pb[4][:], 1e-24),
                     waits=gw_, sig=f"v_ng_{fi}", sem="ve")
                P.op("scalar", lambda g, fb=fb: g.activation(nrm2g[fb][:], nrm2g[fb][:], AF.Ln),
                     waits=[f"v_ng_{fi}"], sig=f"a_ln_{fi}", sem="ae")
                rw = [f"a_ln_{fi}"]
                if fi >= 2:
                    rw.append(f"v_f_{fi - 2}")
                P.op("scalar", lambda g, fb=fb: g.activation(rinvbc[fb][:], nrm2g[fb][:], AF.Exp, scale=-0.5),
                     waits=rw, sig=f"a_rinv_{fi}", sem="ae")
                zw = [f"a_rinv_{fi}"]
                if fi >= 1:
                    zw.append(f"pe_pr_{fi - 1}")
                P.op("vector", lambda g, qk=qk, fb=fb: g.tensor_tensor(znT[:], qkts[qk][:], rinvbc[fb][:], ALU.mult),
                     waits=zw, sig=f"v_zn_{fi}", sem="ve")
                prw = [f"v_zn_{fi}", "g_omg", "g_gw"]
                if fi >= 1:
                    prw.append(f"a_poly_{fi - 1}")
                P.op("tensor", lambda g, p=p: g.matmul(
                    pb[5][:], omg[:, p * 128:(p + 1) * 128], znT[:], start=True, stop=True),
                    waits=prw, sig=f"pe_pr_{fi}", sem="pe")
                eaw = [f"pe_pr_{fi}", "a_q1", "v_q2"]
                P.op("vector", lambda g, fb=fb: g.tensor_scalar(nrm2g[fb][:], pb[5][:], qs2s[:, 0:1],
                                                         qnegs[:, 0:1], ALU.mult, ALU.add),
                     waits=eaw, sig=f"v_ea_{fi}", sem="ve")
                P.op("vector", lambda g, fb=fb: g.tensor_scalar(nrm2g[fb][:], nrm2g[fb][:], 20.0, -20.0,
                                                         ALU.min, ALU.max),
                     waits=[f"v_ea_{fi}"], sig=f"v_cl_{fi}", sem="ve")
                P.op("scalar", lambda g, fb=fb: g.activation(rinvbc[fb][:], nrm2g[fb][:], AF.Exp),
                     waits=[f"v_cl_{fi}"], sig=f"a_prf_{fi}", sem="ae")
                pyw = [f"v_ea_{fi}", "v_q4"]
                if fi >= 1:
                    pyw.append(f"v_f_{fi - 1}")
                P.op("scalar", lambda g: g.activation(poly[:], pb[5][:], AF.Square, scale=qspp[:, 0:1]),
                     waits=pyw, sig=f"a_poly_{fi}", sem="ae")
                fw_ = [f"a_prf_{fi}", f"a_poly_{fi}"]
                if tb == 0 and p >= 1:
                    fw_.append(f"pe_ctx_{p * NCH - 1}")
                P.op("vector", lambda g, qk=qk, tb=tb, fb=fb: g.tensor_tensor(
                    qkf[qk][:, tb * 512:(tb + 1) * 512], rinvbc[fb][:], poly[:], ALU.mult),
                    waits=fw_, sig=f"v_f_{fi}", sem="ve")

        if p == 0 and stop == "feat0":
            return finish([f"v_f_{16 - 1}", f"a_ve_{7}"])
        pair_f = f"v_f_{(p + 1) * 16 - 1}"
        P.op("vector", lambda g: g.memset(stf[:], 0.0),
             waits=([f"pe_ctx_{p * NCH - 1}"] if p >= 1 else []), sig=f"v_st0_{p}", sem="ve")
        for c4 in range(8):
            for j in range(4):
                ch = c4 * 4 + j
                k = p * NCH + ch
                vbk2 = 2 + (ch % 2)
                tw = ["g_vones", pair_f, f"a_ve_{p * 8 + 7}"]
                if k >= 2:
                    tw.append(f"a_vte_{k - 2}")
                P.op("tensor", lambda g, ch=ch, vbk2=vbk2: g.transpose(
                    pbb[vbk2][:, 0:128], vT[:, ch * 128:(ch + 1) * 128], ident[:]),
                    waits=tw, sig=f"pe_vtr_{p}_{ch}", sem="pe")
                P.op("scalar", lambda g, ch=ch, vbk2=vbk2: g.copy(
                    Vt[:, ch * 130:(ch + 1) * 130].rearrange("p (j w) -> p j w", j=2)[:, :, 0:64],
                    pbb[vbk2][:, 0:128].rearrange("p (j w) -> p j w", j=2)),
                    waits=[f"pe_vtr_{p}_{ch}"], sig=f"a_vte_{k}", sem="ae")
                kbk = 6 + (ch % 2)
                ktw = [f"a_kte_{k - 2}"] if k >= 2 else []
                P.op("tensor", lambda g, ch=ch, kbk=kbk: g.transpose(
                    pbb[kbk][:, 0:128], qkf[1][:, ch * 128:(ch + 1) * 128], ident[:]),
                    waits=ktw, sig=f"pe_ktr_{p}_{ch}", sem="pe")
                kew = [f"pe_ktr_{p}_{ch}"]
                if k >= 2:
                    gp_, gc_ = (k - 2) // NCH, (k - 2) % NCH
                    kew.append(f"pe_upd2_{gp_}_{gc_ // 4}_{gc_ % 4}")
                P.op("scalar", lambda g, ch=ch, kbk=kbk: g.copy(Ktok[ch % 2][:], pbb[kbk][:, 0:128]),
                     waits=kew, sig=f"a_kte_{k}", sem="ae")
                uw = [f"a_kte_{k}", f"a_vte_{k}"]
                if c4 >= 2 and j == 0:
                    uw.append(f"a_ue_{p}_{c4 - 2}")
                P.op("tensor", lambda g, ch=ch, j=j: g.matmul(
                    pb[4][0:64, j * 65:(j + 1) * 65], Ktok[ch % 2][:, 0:64],
                    Vt[:, ch * 130: ch * 130 + 65], start=True, stop=True),
                    waits=uw, sem="pe")
                P.op("tensor", lambda g, ch=ch, j=j: g.matmul(
                    pb[4][64:128, j * 65:(j + 1) * 65], Ktok[ch % 2][:, 64:128],
                    Vt[:, ch * 130 + 65: ch * 130 + 130], start=True, stop=True,
                    tile_position=(0, 64)),
                    waits=[], sig=f"pe_upd2_{p}_{c4}_{j}", sem="pe")
            ew = [f"pe_upd2_{p}_{c4}_3"]
            if c4 >= 2:
                ew.append(f"v_sta_{p}_{(c4 - 2) * 4 + 3}")
            elif p >= 1:
                ew.append(f"v_sta_{p - 1}_{(c4 + 6) * 4 + 3}")
            P.op("scalar", lambda g, c4=c4: g.copy(upd[:, (c4 % 2) * 260:(c4 % 2) * 260 + 260],
                                                   pb[4][:, 0:260]),
                 waits=ew, sig=f"a_ue_{p}_{c4}", sem="ae")
            for j2 in range(4):
                ch2 = c4 * 4 + j2
                cw = [f"v_st0_{p}"] if ch2 == 0 else [f"v_sta_{p}_{ch2 - 1}"]
                if p >= 1 and ch2 <= 1:
                    cw.append(f"pe_ctx_{p * NCH - 1}")
                P.op("vector", lambda g, ch2=ch2: g.tensor_copy(stbf[:, ch2 * 65:(ch2 + 1) * 65], stf[:]),
                     waits=cw, sig=f"v_stb_{p}_{ch2}", sem="ve")
                P.op("vector", lambda g, ch2=ch2, c4=c4, j2=j2: g.tensor_tensor(
                    stf[:], stf[:], upd[:, (c4 % 2) * 260 + j2 * 65:
                                         (c4 % 2) * 260 + j2 * 65 + 65], ALU.add),
                    waits=[f"a_ue_{p}_{c4}", f"v_stb_{p}_{ch2}"], sig=f"v_sta_{p}_{ch2}", sem="ve")

        if p == 0 and stop == "stageA0":
            return finish([f"v_sta_0_31", f"a_ue_0_7", f"a_vte_{NCH - 1}"])
        for ch in range(NCH):
            k = p * NCH + ch
            sbk = ch % 2
            sw = [f"v_stb_{p}_{ch}", pair_f]
            if k >= 2:
                sw.append(f"v_smk_{k - 2}")
            P.op("tensor", lambda g, ch=ch, sbk=sbk: g.matmul(
                pb[sbk][:, 0:128], qkf[1][0:64, ch * 128:(ch + 1) * 128],
                qkf[0][0:64, ch * 128:(ch + 1) * 128], start=True, stop=True),
                waits=sw, sem="pe")
            P.op("tensor", lambda g, sbk=sbk: g.matmul(
                pb[sbk][0:1, 500:501], ident[:, 0:1], ident[:, 0:1],
                start=True, stop=True), waits=[], sem="pe")
            P.op("tensor", lambda g, ch=ch, sbk=sbk: g.matmul(
                pb[sbk][:, 128:256], qkf[1][64:128, ch * 128:(ch + 1) * 128],
                qkf[0][64:128, ch * 128:(ch + 1) * 128], start=True, stop=True),
                waits=[], sig=f"pe_S_{k}", sem="pe")
            if p == 0 and ch == 0 and stop == "sbS":
                return finish([f"pe_S_{k}"])
            P.op("vector", lambda g, ch=ch, sbk=sbk: g.tensor_tensor(
                smk[ch % 2][:], pb[sbk][:, 0:256], maskr[:], ALU.mult),
                waits=[f"pe_S_{k}", "init_done"], sig=f"v_smk_{k}", sem="ve")
            if p == 0 and ch == 0 and stop == "sbM":
                return finish([f"v_smk_{k}"])
            cbk = 2 + (ch % 2)
            cw = [f"v_smk_{k}", f"a_vte_{p * NCH + NCH - 1}"]
            if k >= 2:
                cw.append(f"v_ot_{k - 2}")
            P.op("tensor", lambda g, ch=ch, cbk=cbk: g.matmul(
                pb[cbk][:, 0:65], smk[ch % 2][:, 0:128], Vt[:, ch * 130: ch * 130 + 65],
                start=True, stop=False), waits=cw, sem="pe")
            P.op("tensor", lambda g, ch=ch, cbk=cbk: g.matmul(
                pb[cbk][:, 0:65], qkf[0][0:64, ch * 128:(ch + 1) * 128],
                stbf[0:64, ch * 65:(ch + 1) * 65], start=False, stop=True),
                waits=[], sem="pe")
            P.op("tensor", lambda g, ch=ch, cbk=cbk: g.matmul(
                pb[cbk][:, 65:130], smk[ch % 2][:, 128:256], Vt[:, ch * 130 + 65: ch * 130 + 130],
                start=True, stop=False), waits=[], sem="pe")
            P.op("tensor", lambda g, ch=ch, cbk=cbk: g.matmul(
                pb[cbk][:, 65:130], qkf[0][64:128, ch * 128:(ch + 1) * 128],
                stbf[64:128, ch * 65:(ch + 1) * 65], start=False, stop=True),
                waits=[], sig=f"pe_ctx_{k}", sem="pe")
            if p == 0 and ch == 0 and stop == "sbC":
                return finish([f"pe_ctx_{k}"])
            dw = [f"pe_ctx_{k}"]
            if k >= 2:
                dw.append(f"pe_otr_{k - 2}")
            for h01 in range(2):
                base = h01 * 65
                c0 = (ch * 2 + h01) % 64
                P.op("vector", lambda g, cbk=cbk, base=base, c0=c0: g.tensor_scalar_max(
                    dcol[:, c0:c0 + 1], pb[cbk][:, base + 64: base + 65], 1e-6),
                    waits=(dw if h01 == 0 else []),
                    sig=f"v_dx_{k}_{h01}", sem="ve")
                P.op("vector", lambda g, c0=c0: g.reciprocal(dcol[:, 64 + c0:65 + c0], dcol[:, c0:c0 + 1]),
                     waits=[f"v_dx_{k}_{h01}"], sig=f"v_dr_{k}_{h01}", sem="ve")
                P.op("vector", lambda g, ch=ch, cbk=cbk, base=base, h01=h01, c0=c0: g.tensor_scalar_mul(
                    otok[ch % 2][:, h01 * 64:(h01 + 1) * 64], pb[cbk][:, base: base + 64],
                    dcol[:, 64 + c0:65 + c0]),
                    waits=[f"v_dr_{k}_{h01}"], sig=f"v_om_{k}_{h01}", sem="ve")
            P.alias(f"v_ot_{k}", "ve")
            if p == 0 and ch == 0 and stop == "sbD":
                return finish([f"v_ot_{k}"])
            obk = 6 + (ch % 2)
            ow = [f"v_ot_{k}"]
            if k >= 2:
                ow.append(f"a_ot_{k - 2}")
            P.op("tensor", lambda g, ch=ch, obk=obk: g.transpose(
                pbb[obk][:, 0:128], otok[ch % 2][:], ident[:]),
                waits=ow, sig=f"pe_otr_{k}", sem="pe")
            P.op("scalar", lambda g, p=p, ch=ch, obk=obk: g.copy(
                ATT[:, p * T + ch * 128: p * T + (ch + 1) * 128], pbb[obk][:, 0:128]),
                waits=[f"pe_otr_{k}"], sig=f"a_ot_{k}", sem="ae")
            if p == 0 and stop == f"sb{ch}":
                return finish([f"a_ot_{k}"])

        if stop == f"pair{p}":
            return finish([f"a_ot_{(p + 1) * NCH - 1}"])

    att_done = f"a_ot_{NP * NCH - 1}"
    tap_r3, tap_r2 = [], []
    if "t_qf" in dbg_t:
        P.dma("sync", lambda g: g.dma_start(out=dbg_t["t_qf"], in_=qkf[0][:]),
              waits=[f"v_f_{NP * 16 - 1}"], sig="tap_qf", sem="stp")
        tap_r3.append("tap_qf")
    if "t_kf" in dbg_t:
        P.dma("sync", lambda g: g.dma_start(out=dbg_t["t_kf"], in_=qkf[1][:]),
              waits=[f"v_f_{NP * 16 - 1}"], sig="tap_kf", sem="stp")
        tap_r3.append("tap_kf")
    if "t_vt" in dbg_t:
        P.dma("sync", lambda g: g.dma_start(out=dbg_t["t_vt"], in_=vT[:]),
              waits=[f"a_ve_{NP * 8 - 1}"], sig="tap_vt", sem="stp")
        tap_r3.append("tap_vt")
    if "t_attnT" in dbg_t:
        P.dma("sync", lambda g: g.dma_start(out=dbg_t["t_attnT"], in_=ATT[:]),
              waits=[att_done], sig="tap_att", sem="stp")
        tap_r2.append("tap_att")
    if tap_r3 or tap_r2:
        P.alias("tapsA_done", "stp")
        tap_r3 = ["tapsA_done"] if tap_r3 else []
        tap_r2 = ["tapsA_done"] if tap_r2 else []
    P.flush()

    areset("r1")
    areset("r3")
    woutb = aalloc([128, 8192], BF16, "r1")
    x2 = [aalloc([128, 1024], F32, "r1") for i in range(2)]
    s_sb = aalloc([128, 1024], F32, "r1")
    sq2 = aalloc([128, 1024], F32, "r1")
    y32 = [aalloc([128, 1024], F32, "r1") for i in range(2)]
    ybf = aalloc([128, 1024], BF16, "r1")
    YT = aalloc([128, 8 * T], BF16, "r3")

    P.dma("gpsimd", lambda g: g.dma_start(out=woutb[:], in_=d["wout_src"]),
          waits=[att_done], sig="ld_wout", sem="swo")

    for t in range(32):
        lw = [f"v_y_{t - 2}"] if t >= 2 else [att_done]
        P.dma("sync", lambda g, t=t: g.dma_start(out=x2[t % 2][:], in_=xr[t]),
              waits=lw, sig=f"x2_{t}", sem=f"sx{t % 2}")
        for oh in range(2):
            mw = ["ld_wout", att_done, "g_bout", "init_done"]
            if t >= 2:
                mw.append(f"v_s_{t - 2}_{oh}")
            for gc in range(8):
                P.op("tensor", lambda g, t=t, oh=oh, gc=gc: g.matmul(
                    pb[oh][:], ATT[:, gc * T + t * 128: gc * T + (t + 1) * 128],
                    woutb[:, gc * 1024 + oh * 512: gc * 1024 + (oh + 1) * 512],
                    start=(gc == 0), stop=False),
                    waits=(mw if gc == 0 else []), sem="pe")
            P.op("tensor", lambda g, oh=oh: g.matmul(
                pb[oh][:], ones1b[:], boutb[0:1, oh * 512:(oh + 1) * 512],
                start=False, stop=True),
                waits=[], sig=f"pe_op_{t}_{oh}", sem="pe")
        for oh in range(2):
            P.op("vector", lambda g, t=t, oh=oh: g.scalar_tensor_tensor(
                s_sb[:, oh * 512:(oh + 1) * 512], pb[oh][:], 1.0,
                x2[t % 2][:, oh * 512:(oh + 1) * 512], ALU.mult, ALU.add,
                accum_out=lncol[:, oh: oh + 1]),
                waits=[f"pe_op_{t}_{oh}", f"x2_{t}"]
                + ([f"a_sqs_{t - 1}_{oh}", f"v_y_{t - 1}"] if t >= 1 else []),
                sig=f"v_s_{t}_{oh}", sem="ve")
        P.op("vector", lambda g: g.tensor_tensor(lncol[:, 2:3], lncol[:, 0:1], lncol[:, 1:2], ALU.add),
             waits=[f"v_s_{t}_0", f"v_s_{t}_1"], sig=f"v_la_{t}", sem="ve")
        P.op("vector", lambda g: g.tensor_scalar_mul(lncol[:, 2:3], lncol[:, 2:3], 1.0 / 1024.0),
             waits=[f"v_la_{t}"], sig=f"v_mean_{t}", sem="ve")
        for oh in range(2):
            P.op("scalar", lambda g, oh=oh: g.activation(
                sq2[:, oh * 512:(oh + 1) * 512], s_sb[:, oh * 512:(oh + 1) * 512], AF.Square,
                accum_out=lncol[:, 5 + oh: 6 + oh]),
                waits=[f"v_s_{t}_{oh}"], sig=f"a_sqs_{t}_{oh}", sem="ae")
        P.op("vector", lambda g: g.tensor_tensor(lncol[:, 4:5], lncol[:, 5:6], lncol[:, 6:7], ALU.add),
             waits=[f"a_sqs_{t}_0", f"a_sqs_{t}_1"], sig=f"v_lb_{t}", sem="ve")
        P.op("vector", lambda g: g.tensor_tensor(lncol[:, 3:4], lncol[:, 2:3], lncol[:, 2:3], ALU.mult),
             waits=[f"v_mean_{t}"], sig=f"v_lc_{t}", sem="ve")
        P.op("vector", lambda g: g.tensor_scalar(lncol[:, 4:5], lncol[:, 4:5], 1.0 / 1024.0,
                                                 lncol[:, 3:4], ALU.mult, ALU.subtract),
             waits=[f"v_lb_{t}", f"v_lc_{t}"], sig=f"v_var_{t}", sem="ve")
        P.op("scalar", lambda g: g.activation(lncol[:, 4:5], lncol[:, 4:5], AF.Ln, bias=epsc[:, 0:1]),
             waits=[f"v_var_{t}", "v_eps"], sig=f"a_lnv_{t}", sem="ae")
        P.op("scalar", lambda g: g.activation(lncol[:, 7:8], lncol[:, 4:5], AF.Exp, scale=-0.5),
             waits=[f"a_lnv_{t}"], sig=f"a_rstd_{t}", sem="ae")
        P.op("vector", lambda g, t=t: g.tensor_scalar(
            y32[t % 2][:], s_sb[:], lncol[:, 2:3], lncol[:, 7:8], ALU.subtract, ALU.mult),
            waits=[f"a_rstd_{t}"] + ([f"ywr_{t - 2}"] if t >= 2 else []), sig=f"v_y_{t}", sem="ve")
        bw = [f"v_y_{t}"] + ([f"pe_yt_{t - 1}_7"] if t >= 1 else [])
        P.op("vector", lambda g, t=t: g.tensor_copy(ybf[:], y32[t % 2][:]),
             waits=bw, sig=f"v_ybf_{t}", sem="ve")
        P.dma("sync", lambda g, t=t: g.dma_start(out=scr["y_scr"][t], in_=y32[t % 2][:]),
              waits=[f"v_y_{t}"], sig=f"ywr_{t}", sem=f"yw{t % 2}")
        for gc in range(8):
            bk = 2 + (gc % 2)
            tw = [f"v_ybf_{t}"]
            if (t, gc) > (0, 1):
                pt, pg = (t, gc - 2) if gc >= 2 else (t - 1, gc + 6)
                tw.append(f"a_yt_{pt}_{pg}")
            elif t == 0:
                tw.append(f"v_ot_{NP * NCH - 1}")
                tw += tap_r3
            P.op("tensor", lambda g, t=t, gc=gc, bk=bk: g.transpose(
                pbb[bk][:, 0:128], ybf[:, gc * 128:(gc + 1) * 128], ident[:]),
                waits=tw, sig=f"pe_yt_{t}_{gc}", sem="pe")
            P.op("scalar", lambda g, t=t, gc=gc, bk=bk: g.copy(
                YT[:, gc * T + t * 128: gc * T + (t + 1) * 128], pbb[bk][:, 0:128]),
                waits=[f"pe_yt_{t}_{gc}"], sig=f"a_yt_{t}_{gc}", sem="ae")

    yt_done = "a_yt_31_7"
    if "t_y" in dbg_t:
        P.dma("sync", lambda g: g.dma_start(out=dbg_t["t_y"], in_=scr["y_scr"][:]),
              waits=[f"ywr_{31}", f"ywr_{30}"], sig="tap_y", sem="stp")
    if stop == "phaseB":
        w = [yt_done, "ywr_31", "ywr_30"]
        if "t_y" in dbg_t:
            w.append("tap_y")
        return finish(w)
    P.flush()

    areset("r2")
    WR = aalloc([128, 1024], F32, "r2")
    r2_mark = _bump["r2"]
    scn = aalloc([128, 256], F32, "r2")
    scc = aalloc([128, 256], F32, "r2")
    mskd = aalloc([128, 256], F32, "r2")
    selb = aalloc([128, 256], F32, "r2")
    gsml = aalloc([128, 64], F32, "r2")
    rcol = aalloc([128, 32], F32, "r2")
    biasbc = aalloc([128, 32], F32, "r2")

    P.op("tensor", lambda g: g.matmul(pb[4][:, 0:32], ones1f[:], gbias[:], start=True, stop=True),
         waits=["init_done", yt_done, f"v_ng_{NP * 16 - 1}"] + tap_r2, sig="pe_gb", sem="pe")
    P.op("scalar", lambda g: g.copy(biasbc[:], pb[4][:, 0:32]), waits=["pe_gb"], sig="a_gb", sem="ae")

    def v3(x, i, j):
        return x[:].rearrange("p (a b) -> p a b", a=i, b=j)

    for tg in range(4):
        for j in range(8):
            tile = tg * 8 + j
            mw = [yt_done, "g_gw"]
            if tg == 0 and j == 0:
                mw.append(f"a_poly_{NP * 16 - 1}")
            if tg >= 1:
                mw.append(f"v_wr_{tg - 1}")
            for gc in range(8):
                P.op("tensor", lambda g, tile=tile, j=j, gc=gc: g.matmul(
                    pb[5 + j // 4][:, (j % 4) * 128:(j % 4) * 128 + 32],
                    YT[:, gc * T + tile * 128: gc * T + (tile + 1) * 128],
                    gwb[:, gc * 32:(gc + 1) * 32],
                    start=(gc == 0), stop=(gc == 7)),
                    waits=(mw if gc == 0 else []),
                    sig=(f"pe_lg_{tile}" if gc == 7 else None), sem="pe")
        for j in range(8):
            P.op("scalar", lambda g, j=j: g.activation(
                scn[:, j * 32:(j + 1) * 32], pb[5 + j // 4][:, (j % 4) * 128:(j % 4) * 128 + 32],
                AF.Sigmoid),
                waits=[f"pe_lg_{tg * 8 + (j // 4) * 4 + 3}"], sig=f"a_sg_{tg}_{j}", sem="ae")
        vv = lambda nm: f"{nm}_{tg}"
        gm1, gm2 = gsml[:, 0:32], gsml[:, 32:64]
        P.op("vector", lambda g: g.tensor_tensor(
            v3(scc, 8, 32), v3(scn, 8, 32),
            biasbc[:].unsqueeze(1).broadcast_to([128, 8, 32]), ALU.add),
            waits=[f"a_sg_{tg}_{j}" for j in range(8)] + ["a_gb"], sig=vv("v_sc"), sem="ve")
        P.op("vector", lambda g: g.reduce_max(
            gm1.rearrange("p (t q) -> p t q", q=4), v3(scc, 32, 8), axis=AX.X),
            waits=[vv("v_sc")], sig=vv("v_gm1"), sem="ve")
        P.op("vector", lambda g: g.tensor_tensor(
            v3(mskd, 32, 8), v3(scc, 32, 8),
            gm1.unsqueeze(2).broadcast_to([128, 32, 8]), ALU.is_equal),
            waits=[vv("v_gm1")], sig=vv("v_eq1"), sem="ve")
        P.op("vector", lambda g: g.scalar_tensor_tensor(mskd[:], mskd[:], -1e30, scc[:],
                                                        ALU.mult, ALU.add),
             waits=[vv("v_eq1")], sig=vv("v_t1"), sem="ve")
        P.op("vector", lambda g: g.reduce_max(
            gm2.rearrange("p (t q) -> p t q", q=4), v3(mskd, 32, 8), axis=AX.X),
            waits=[vv("v_t1")], sig=vv("v_gm2"), sem="ve")
        P.op("vector", lambda g: g.tensor_tensor(gm1, gm1, gm2, ALU.add),
             waits=[vv("v_gm2")], sig=vv("v_gs"), sem="ve")
        P.op("vector", lambda g: g.reduce_max(
            rcol[:, 0:8], gm1.rearrange("p (t q) -> p t q", q=4), axis=AX.X),
            waits=[vv("v_gs")], sig=vv("v_g1"), sem="ve")
        P.op("vector", lambda g: g.tensor_tensor(
            gm2.rearrange("p (t q) -> p t q", q=4), gm1.rearrange("p (t q) -> p t q", q=4),
            rcol[:, 0:8].unsqueeze(2).broadcast_to([128, 8, 4]), ALU.is_equal),
            waits=[vv("v_g1")], sig=vv("v_eqg"), sem="ve")
        P.op("vector", lambda g: g.scalar_tensor_tensor(gm2, gm2, -1e30, gm1, ALU.mult, ALU.add),
             waits=[vv("v_eqg")], sig=vv("v_tg"), sem="ve")
        P.op("vector", lambda g: g.reduce_max(
            rcol[:, 8:16], gm2.rearrange("p (t q) -> p t q", q=4), axis=AX.X),
            waits=[vv("v_tg")], sig=vv("v_g2"), sem="ve")
        P.op("vector", lambda g: g.tensor_tensor(
            gm2.rearrange("p (t q) -> p t q", q=4), gm1.rearrange("p (t q) -> p t q", q=4),
            rcol[:, 8:16].unsqueeze(2).broadcast_to([128, 8, 4]), ALU.is_ge),
            waits=[vv("v_g2")], sig=vv("v_gmask"), sem="ve")
        P.op("vector", lambda g: g.tensor_tensor(
            v3(mskd, 32, 8), v3(scc, 32, 8),
            gm2.unsqueeze(2).broadcast_to([128, 32, 8]), ALU.mult),
            waits=[vv("v_gmask")], sig=vv("v_msk"), sem="ve")
        P.op("vector", lambda g: g.memset(selb[:], 0.0),
             waits=([f"v_wr_{tg - 1}"] if tg >= 1 else []), sig=vv("v_sel0"), sem="ve")
        prev = vv("v_msk")
        for it in range(8):
            P.op("vector", lambda g: g.reduce_max(rcol[:, 16:24], v3(mskd, 8, 32), axis=AX.X),
                 waits=[prev], sig=vv(f"v_m8_{it}"), sem="ve")
            P.op("vector", lambda g: g.tensor_tensor(
                v3(scc, 8, 32), v3(mskd, 8, 32),
                rcol[:, 16:24].unsqueeze(2).broadcast_to([128, 8, 32]), ALU.is_equal),
                waits=[vv(f"v_m8_{it}")], sig=vv(f"v_eq8_{it}"), sem="ve")
            P.op("vector", lambda g: g.tensor_tensor(selb[:], selb[:], scc[:], ALU.add),
                 waits=[vv(f"v_eq8_{it}"), vv("v_sel0")], sig=vv(f"v_sa_{it}"), sem="ve")
            P.op("vector", lambda g: g.scalar_tensor_tensor(mskd[:], scc[:], -1e30, mskd[:],
                                                            ALU.mult, ALU.add),
                 waits=[vv(f"v_sa_{it}")], sig=vv(f"v_mu_{it}"), sem="ve")
            prev = vv(f"v_mu_{it}")
        P.op("vector", lambda g: g.tensor_tensor(selb[:], selb[:], scn[:], ALU.mult),
             waits=[prev], sig=vv("v_w"), sem="ve")
        P.op("vector", lambda g: g.reduce_sum(rcol[:, 24:32], v3(selb, 8, 32), axis=AX.X),
             waits=[vv("v_w")], sig=vv("v_ws"), sem="ve")
        P.op("vector", lambda g: g.tensor_scalar_add(rcol[:, 24:32], rcol[:, 24:32], 1e-20),
             waits=[vv("v_ws")], sig=vv("v_pa"), sem="ve")
        P.op("vector", lambda g: g.reciprocal(rcol[:, 16:24], rcol[:, 24:32]),
             waits=[vv("v_pa")], sig=vv("v_pb"), sem="ve")
        P.op("vector", lambda g: g.tensor_scalar_mul(rcol[:, 16:24], rcol[:, 16:24], 2.5),
             waits=[vv("v_pb")], sig=vv("v_rs"), sem="ve")
        P.op("vector", lambda g, tg=tg: g.tensor_tensor(
            WR[:, tg * 256:(tg + 1) * 256].rearrange("p (t e) -> p t e", e=32),
            v3(selb, 8, 32),
            rcol[:, 16:24].unsqueeze(2).broadcast_to([128, 8, 32]), ALU.mult),
            waits=[vv("v_rs")], sig=f"v_wr_{tg}", sem="ve")

    if "t_wroute" in dbg_t:
        P.dma("sync", lambda g: g.dma_start(out=dbg_t["t_wroute"], in_=WR[:]),
              waits=["v_wr_3"], sig="tap_wr", sem="stp")
    if stop == "phaseC":
        w = ["v_wr_3"]
        if "t_wroute" in dbg_t:
            w.append("tap_wr")
        return finish(w)
    P.flush()

    areset("r1")
    areset("r2", keep=r2_mark - _rbase["r2"])
    ACC = aalloc([128, 16 * 1024], F32, "r1")
    WGb = [aalloc([128, 4096], BF16, "r2") for i in range(2)]
    WUb = [aalloc([128, 4096], BF16, "r2") for i in range(2)]
    WDb = [aalloc([128, 4096], BF16, "r2") for i in range(2)]
    Hb = [aalloc([128, 2048], BF16, "r2") for i in range(2)]
    SGb = aalloc([128, 512], F32, "r2")

    n_tb = 4
    outr = out.rearrange("(n p) c -> p n c", p=128)
    for hf in range(2):
        aw = [f"ywr_{31}", f"ywr_{30}", yt_done]
        if hf == 1:
            aw.append("out_0")
        P.dma("sync", lambda g, hf=hf: g.dma_start(
            out=ACC[:].rearrange("p (n c) -> p n c", c=1024),
            in_=scr["y_scr"].rearrange("n p c -> p n c")[:, hf * 16:(hf + 1) * 16, :]),
            waits=aw, sig=f"accld_{hf}", sem="sac")
        for e in range(NE):
            pbuf = e % 2
            k = hf * NE + e
            ew = []
            if k == 0:
                ew.append("v_wr_3")
            if k >= 2:
                ew.append(f"pe_dn_{k - 2}_{n_tb - 1}_7")
            P.dma("gpsimd", lambda g, e=e, pbuf=pbuf: g.dma_start(out=WGb[pbuf][:], in_=d["eg_src"][e]),
                  waits=ew, sig=f"wg_{hf}_{e}", sem=f"eb{e % 2}")
            P.dma("gpsimd", lambda g, e=e, pbuf=pbuf: g.dma_start(out=WUb[pbuf][:], in_=d["eu_src"][e]),
                  waits=[], sig=f"wu_{hf}_{e}", sem=f"eb{e % 2}")
            P.dma("gpsimd", lambda g, e=e, pbuf=pbuf: g.dma_start(out=WDb[pbuf][:], in_=d["ed_src"][e]),
                  waits=[], sig=f"wd_{hf}_{e}", sem=f"eb{e % 2}")
            for tb in range(n_tb):
                tbk = k * n_tb + tb
                for it in range(4):
                    ki = tbk * 4 + it
                    gbk, ubk = it % 2, 2 + it % 2
                    gw_ = [f"wd_{hf}_{e}", yt_done]
                    if ki >= 2:
                        gw_.append(f"v_h_{ki - 2}")
                    if ki == 0:
                        gw_ += ["v_s_31_0", "v_s_31_1", "a_yt_31_6", "a_yt_31_7"]
                    for gc in range(8):
                        P.op("tensor", lambda g, hf=hf, tb=tb, it=it, gc=gc, pbuf=pbuf, gbk=gbk: g.matmul(
                            pb[gbk][:], WGb[pbuf][:, gc * 512 + it * 128: gc * 512 + it * 128 + 128],
                            YT[:, gc * T + (hf * 4 + tb) * 512: gc * T + (hf * 4 + tb + 1) * 512],
                            start=(gc == 0), stop=(gc == 7)),
                            waits=(gw_ if gc == 0 else []),
                            sig=(f"pe_g_{ki}" if gc == 7 else None), sem="pe")
                    for gc in range(8):
                        P.op("tensor", lambda g, hf=hf, tb=tb, it=it, gc=gc, pbuf=pbuf, ubk=ubk: g.matmul(
                            pb[ubk][:], WUb[pbuf][:, gc * 512 + it * 128: gc * 512 + it * 128 + 128],
                            YT[:, gc * T + (hf * 4 + tb) * 512: gc * T + (hf * 4 + tb + 1) * 512],
                            start=(gc == 0), stop=(gc == 7)),
                            waits=[], sig=(f"pe_u_{ki}" if gc == 7 else None), sem="pe")
                    sw = [f"pe_g_{ki}"]
                    if ki >= 1:
                        sw.append(f"v_h_{ki - 1}")
                    P.op("scalar", lambda g, gbk=gbk: g.activation(SGb[:], pb[gbk][:], AF.Silu),
                         waits=sw, sig=f"a_sil_{ki}", sem="ae")
                    hw = [f"a_sil_{ki}", f"pe_u_{ki}"]
                    if tbk >= 2 and it == 0:
                        pk2 = tbk - 2
                        hw.append(f"pe_dn_{pk2 // n_tb}_{pk2 % n_tb}_7")
                    P.op("vector", lambda g, tb=tb, it=it, ubk=ubk: g.tensor_tensor(
                        Hb[tb % 2][:, it * 512:(it + 1) * 512], SGb[:], pb[ubk][:], ALU.mult),
                        waits=hw, sig=f"v_h_{ki}", sem="ve")
                for ts_ in range(4):
                    for oh in range(2):
                        nd = ts_ * 2 + oh
                        dbk = 4 + nd % 4
                        ndk = tbk * 8 + nd
                        dw = [f"v_h_{tbk * 4 + 3}", f"wd_{hf}_{e}"]
                        if ndk >= 4:
                            dw.append(f"v_dn_{ndk - 4}")
                        if ndk < 4:
                            dw += ["a_sg_3_3", "a_sg_3_7", "a_gb"]
                        for ic in range(4):
                            P.op("tensor", lambda g, tb=tb, ts_=ts_, oh=oh, ic=ic, pbuf=pbuf, dbk=dbk: g.matmul(
                                pb[dbk][:], Hb[tb % 2][:, ic * 512 + ts_ * 128: ic * 512 + ts_ * 128 + 128],
                                WDb[pbuf][:, ic * 1024 + oh * 512: ic * 1024 + oh * 512 + 512],
                                start=(ic == 0), stop=(ic == 3)),
                                waits=(dw if ic == 0 else []),
                                sig=(f"pe_dn_{k}_{tb}_{nd}" if ic == 3 else None), sem="pe")
                        vw = [f"pe_dn_{k}_{tb}_{nd}"]
                        if e == 0 and tb == 0:
                            vw.append(f"accld_{hf}")
                        if e >= 1:
                            vw.append(f"v_dn_{ndk - n_tb * 8}")
                        if e >= 2:
                            vw.append(f"v_wr_{(hf * 16 + tb * 4 + ts_) // 8}")
                        gtile = tb * 4 + ts_
                        a_sl = lambda gtile=gtile, oh=oh: ACC[:, gtile * 1024 + oh * 512: gtile * 1024 + (oh + 1) * 512]
                        if e < 2:
                            P.op("vector", lambda g, a_sl=a_sl, dbk=dbk: g.scalar_tensor_tensor(
                                a_sl(), pb[dbk][:], 1.0, a_sl(), ALU.mult, ALU.add),
                                waits=vw, sig=f"v_dn_{ndk}", sem="ve")
                        else:
                            P.op("vector", lambda g, a_sl=a_sl, dbk=dbk, hf=hf, e=e, gtile=gtile: g.scalar_tensor_tensor(
                                a_sl(), pb[dbk][:],
                                WR[:, (hf * 16 + gtile) * 32 + (e - 2): (hf * 16 + gtile) * 32 + (e - 1)],
                                a_sl(), ALU.mult, ALU.add),
                                waits=vw, sig=f"v_dn_{ndk}", sem="ve")
        last_dn = f"v_dn_{((hf * NE + NE - 1) * n_tb + n_tb - 1) * 8 + 7}"
        for tt in range(16):
            t = hf * 16 + tt
            a0 = ACC[:, tt * 1024: tt * 1024 + 512]
            a1 = ACC[:, tt * 1024 + 512: tt * 1024 + 1024]
            s0w = [last_dn]
            if tt >= 1:
                s0w += [f"a2_sq_{t - 1}_1", f"v2_y_{t - 1}"]
            P.op("vector", lambda g, a0=a0: g.tensor_scalar(
                SGb[:], a0, 0.0, None, ALU.add, ALU.add, accum_out=lncol[:, 0:1]),
                waits=s0w, sig=f"v2_s0_{t}", sem="ve")
            P.op("vector", lambda g, a1=a1: g.tensor_scalar(
                SGb[:], a1, 0.0, None, ALU.add, ALU.add, accum_out=lncol[:, 1:2]),
                waits=[f"v2_s0_{t}"], sig=f"v2_s1_{t}", sem="ve")
            P.op("vector", lambda g: g.tensor_tensor(lncol[:, 2:3], lncol[:, 0:1], lncol[:, 1:2], ALU.add),
                 waits=[f"v2_s0_{t}", f"v2_s1_{t}"], sig=f"v2_la_{t}", sem="ve")
            P.op("vector", lambda g: g.tensor_scalar_mul(lncol[:, 2:3], lncol[:, 2:3], 1.0 / 1024.0),
                 waits=[f"v2_la_{t}"], sig=f"v2_mean_{t}", sem="ve")
            for oh in range(2):
                P.op("scalar", lambda g, tt=tt, oh=oh: g.activation(
                    SGb[:],
                    ACC[:, tt * 1024 + oh * 512: tt * 1024 + (oh + 1) * 512],
                    AF.Square, accum_out=lncol[:, 5 + oh: 6 + oh]),
                    waits=[f"v2_s1_{t}"] + ([f"a2_sq_{t}_0"] if oh == 1 else []),
                    sig=f"a2_sq_{t}_{oh}", sem="ae")
            P.op("vector", lambda g: g.tensor_tensor(lncol[:, 4:5], lncol[:, 5:6], lncol[:, 6:7], ALU.add),
                 waits=[f"a2_sq_{t}_0", f"a2_sq_{t}_1"], sig=f"v2_lb_{t}", sem="ve")
            P.op("vector", lambda g: g.tensor_tensor(lncol[:, 3:4], lncol[:, 2:3], lncol[:, 2:3], ALU.mult),
                 waits=[f"v2_mean_{t}"], sig=f"v2_lc_{t}", sem="ve")
            P.op("vector", lambda g: g.tensor_scalar(lncol[:, 4:5], lncol[:, 4:5], 1.0 / 1024.0,
                                                     lncol[:, 3:4], ALU.mult, ALU.subtract),
                 waits=[f"v2_lb_{t}", f"v2_lc_{t}"], sig=f"v2_var_{t}", sem="ve")
            P.op("scalar", lambda g: g.activation(lncol[:, 4:5], lncol[:, 4:5], AF.Ln, bias=epsc[:, 0:1]),
                 waits=[f"v2_var_{t}", "v_eps"], sig=f"a2_lnv_{t}", sem="ae")
            P.op("scalar", lambda g: g.activation(lncol[:, 7:8], lncol[:, 4:5], AF.Exp, scale=-0.5),
                 waits=[f"a2_lnv_{t}"], sig=f"a2_rstd_{t}", sem="ae")
            P.op("vector", lambda g, tt=tt: g.tensor_scalar(
                ACC[:, tt * 1024:(tt + 1) * 1024], ACC[:, tt * 1024:(tt + 1) * 1024],
                lncol[:, 2:3], lncol[:, 7:8], ALU.subtract, ALU.mult),
                waits=[f"a2_rstd_{t}"], sig=f"v2_y_{t}", sem="ve")
        P.dma("sync", lambda g, hf=hf: g.dma_start(
            out=outr[:, hf * 16:(hf + 1) * 16, :],
            in_=ACC[:].rearrange("p (n c) -> p n c", c=1024)),
            waits=[f"v2_y_{hf * 16 + tt2}" for tt2 in range(16)], sig=f"out_{hf}", sem="sout")

    fw = ["out_0", "out_1"]
    if P.counts["stp"] > 0:
        P.alias("taps_all", "stp")
        fw.append("taps_all")
    P.op("sync", None, waits=fw)
    P.flush()

    stack.close()
    return nc, dbg_t


def host_prep(ins):
    f32 = np.float32
    w_qkv = np.asarray(ins["L0_attn_w_qkv"], f32)
    wq = w_qkv[0:1024].reshape(16, 64, 1024)
    wk = w_qkv[1024:2048].reshape(16, 64, 1024)
    wv = w_qkv[2048:3072].reshape(16, 64, 1024)
    blocks = []
    for p in range(NP):
        h0, h1 = 2 * p, 2 * p + 1
        cols = np.concatenate([wq[h0].T, wq[h1].T, wk[h0].T, wk[h1].T,
                               wv[h0].T, wv[h1].T], axis=1)
        blocks.append(cols)
    wqkv_src = np.stack(blocks).reshape(NP, 8, 128, 384).transpose(0, 2, 1, 3).reshape(NP, 128, 3072)

    woutT = np.asarray(ins["L0_attn_w_out"], f32).T
    wout_src = woutT.reshape(8, 128, 1024).transpose(1, 0, 2).reshape(128, 8192)

    sgT = np.asarray(ins["L0_ffn_shared_gate"], f32).T
    suT = np.asarray(ins["L0_ffn_shared_up"], f32).T
    sdT = np.asarray(ins["L0_ffn_shared_down"], f32).T
    egT = np.transpose(np.asarray(ins["L0_ffn_expert_gate"], f32), (0, 2, 1))
    euT = np.transpose(np.asarray(ins["L0_ffn_expert_up"], f32), (0, 2, 1))
    edT = np.transpose(np.asarray(ins["L0_ffn_expert_down"], f32), (0, 2, 1))
    eg_full = np.concatenate([sgT[None, :, 0:512], sgT[None, :, 512:1024], egT], 0)
    eu_full = np.concatenate([suT[None, :, 0:512], suT[None, :, 512:1024], euT], 0)
    ed_full = np.concatenate([sdT[None, 0:512, :], sdT[None, 512:1024, :], edT], 0)
    eg_src = eg_full.reshape(NE, 8, 128, 512).transpose(0, 2, 1, 3).reshape(NE, 128, 4096)
    eu_src = eu_full.reshape(NE, 8, 128, 512).transpose(0, 2, 1, 3).reshape(NE, 128, 4096)
    ed_src = ed_full.reshape(NE, 4, 128, 1024).transpose(0, 2, 1, 3).reshape(NE, 128, 4096)

    gw_src = np.asarray(ins["L0_ffn_gate_w"], f32).T.reshape(8, 128, 32).transpose(1, 0, 2).reshape(128, 256)

    omega = np.asarray(ins["L0_attn_omega"], f32)
    omcat = np.transpose(omega, (1, 2, 0, 3)).reshape(16, 64, 64)
    omega_src = np.zeros((128, 1024), f32)
    for p in range(NP):
        omega_src[0:64, p * 128: p * 128 + 64] = omcat[2 * p]
        omega_src[64:128, p * 128 + 64: p * 128 + 128] = omcat[2 * p + 1]

    b_qkv = np.asarray(ins["L0_attn_b_qkv"], f32)
    bq = b_qkv[0:1024].reshape(16, 64)
    bk = b_qkv[1024:2048].reshape(16, 64)
    bv = b_qkv[2048:3072].reshape(16, 64)
    cols = []
    for p in range(NP):
        cols.append(np.concatenate([bq[2 * p], bq[2 * p + 1]]))
        cols.append(np.concatenate([bk[2 * p], bk[2 * p + 1]]))
        cols.append(np.concatenate([bv[2 * p], bv[2 * p + 1]]))
    bqkv_pp = np.stack(cols, axis=1)

    qn = np.asarray(ins["L0_attn_quad_nodes"], f32)
    qw = np.asarray(ins["L0_attn_quad_weights"], f32)
    pat = np.concatenate([np.repeat(qn, 32), np.repeat(qn, 32)])
    patw = np.concatenate([np.repeat(qw, 32), np.repeat(qw, 32)])
    quad_pp = np.stack([pat, patw], axis=1).astype(f32)

    bf = ml_dtypes.bfloat16
    blkdiag = np.zeros((128, 128), f32)
    blkdiag[0:64, 0:64] = 1.0
    blkdiag[64:128, 64:128] = 1.0
    common = dict(
        wqkv_src=np.ascontiguousarray(wqkv_src),
        wout_src=np.ascontiguousarray(wout_src),
        eg_src=np.ascontiguousarray(eg_src),
        eu_src=np.ascontiguousarray(eu_src),
        ed_src=np.ascontiguousarray(ed_src),
        gw_src=np.ascontiguousarray(gw_src),
        omega_src=omega_src,
        bqkv_pp=np.ascontiguousarray(bqkv_pp),
        bout_src=np.asarray(ins["L0_attn_b_out"], f32).reshape(1, 1024),
        quad_pp=quad_pp,
        gbias_src=np.asarray(ins["L0_ffn_gate_bias"], f32).reshape(1, 32),
        ident_bf=np.eye(128, dtype=bf),
        maskrep_bf=np.tile(np.triu(np.ones((128, 128), np.float32)), (1, 2)).astype(bf),
        blkdiag_bf=blkdiag.astype(bf),
        ones1_bf=np.ones((1, 128), bf),
        ones1_f=np.ones((1, 128), f32),
    )
    return common


def kernel(**ins):
    common = host_prep(ins)
    x_all = np.asarray(ins["L0_attn_x"], np.float32)
    nc, _ = build()
    in_maps = [dict(common, x=np.ascontiguousarray(x_all[b])) for b in range(8)]
    res = run_bass_kernel_spmd(nc, in_maps, core_ids=list(range(8)))
    return np.stack([res.results[b]["out"] for b in range(8)], axis=0)


# version 05
# speedup vs baseline: 1.3187x; 1.0656x over previous
import numpy as np
import ml_dtypes
import concourse.bass as bass
import concourse.mybir as mybir
from concourse.bass_utils import run_bass_kernel_spmd

F32 = mybir.dt.float32
BF16 = mybir.dt.bfloat16
AF = mybir.ActivationFunctionType
ALU = mybir.AluOpType
AX = mybir.AxisListType

T = 4096
NP = 8
NCH = 32
NE = 34
CONV_CHUNK = 256

ENGINES = ("sync", "tensor", "vector", "scalar", "gpsimd")


class Prog:

    def __init__(self, nc, sems):
        self.nc = nc
        self.sems = sems
        self.counts = {k: 0 for k in sems}
        self.ev = {}
        self.pending = {e: [] for e in ENGINES}
        self.maxwait = {e: {} for e in ENGINES}

    def op(self, eng, fn, waits=(), sig=None, sem=None, amt=1):
        for w in waits:
            assert w in self.ev, f"wait on undefined event {w}"
        if sig is not None:
            assert sem is not None
            self.counts[sem] += amt
            assert sig not in self.ev, f"event {sig} redefined"
            self.ev[sig] = (sem, self.counts[sem])
        self.pending[eng].append((fn, tuple(waits), sig, sem, amt))

    def dma(self, eng, fn, waits=(), sig=None, sem=None):
        assert sem is not None
        self.op(eng, fn, waits, sig, sem, amt=16)

    def alias(self, name, sem):
        assert name not in self.ev
        self.ev[name] = (sem, self.counts[sem])

    def flush(self):
        nc = self.nc
        if not any(self.pending.values()):
            return
        with nc.Block() as block:
            for eng_name in ENGINES:
                ops = self.pending[eng_name]
                if not ops:
                    continue
                mw = self.maxwait[eng_name]

                def emit(eng, ops=ops, mw=mw):
                    for fn, waits, sig, sem, amt in ops:
                        for w in waits:
                            wsem, wval = self.ev[w]
                            if wval <= mw.get(wsem, 0):
                                continue
                            mw[wsem] = wval
                            eng.wait_ge(self.sems[wsem], wval)
                        instr = fn(eng) if fn is not None else None
                        if sig is not None:
                            instr.then_inc(self.sems[sem], amt)
                getattr(block, eng_name)(emit)
        self.pending = {e: [] for e in ENGINES}


def build(dbg=(), stop=None):
    nc = bass.Bass("TRN2", target_bir_lowering=False)
    d = {}

    def din(name, shape, dt=F32):
        d[name] = nc.dram_tensor(name, shape, dt, kind="ExternalInput").ap()
    din("x", [T, 1024])
    din("wqkv_src", [NP, 128, 3072])
    din("wout_src", [128, 8192])
    din("eg_src", [NE, 128, 4096])
    din("eu_src", [NE, 128, 4096])
    din("ed_src", [NE, 128, 4096])
    din("gw_src", [128, 256])
    din("omega_src", [128, 1024])
    din("bqkv_pp", [128, 24])
    din("bout_src", [1, 1024])
    din("quad_pp", [128, 2])
    din("gbias_src", [1, 32])
    din("ident_bf", [128, 128], BF16)
    din("maskrep_bf", [128, 256], BF16)
    din("blkdiag_bf", [128, 128], BF16)
    din("ones1_bf", [1, 128], BF16)
    din("ones1_f", [1, 128])

    out = nc.dram_tensor("out", [T, 1024], F32, kind="ExternalOutput").ap()
    dbg_t = {}
    dbg_spec = {"t_attnT": ([128, NP * T], BF16), "t_y": ([32, 128, 1024], F32),
                "t_wroute": ([128, 1024], F32), "t_qf": ([128, T], BF16),
                "t_kf": ([128, T], BF16), "t_vt": ([128, T], BF16)}
    for nm in dbg:
        shape, dt = dbg_spec[nm]
        dbg_t[nm] = nc.dram_tensor(nm, shape, dt, kind="ExternalOutput").ap()

    scr = {}
    def dscr(name, shape, dt=BF16):
        scr[name] = nc.dram_tensor(name, shape, dt, kind="Internal").ap()
    dscr("y_scr", [32, 128, 1024], F32)

    from contextlib import ExitStack
    stack = ExitStack()
    e_ = stack.enter_context

    sems = {k: e_(nc.semaphore(k)) for k in (
        "pe", "ve", "ae", "gp", "si", "sx0", "sx1", "cv0", "cv1", "gq0", "gq1",
        "wp", "swo", "yw0", "yw1", "eb0", "eb1", "sac", "sout", "stp")}
    P = Prog(nc, sems)

    sbg = {}
    def alloc(name, shape, dt):
        g = nc.sbuf_tensor(name, shape, dt)
        h = g.__enter__()
        sbg[name] = g
        return h

    U8 = mybir.dt.uint8
    RKB = 64 * 1024
    arena = alloc("arena", [128, 3 * RKB], U8)
    _rbase = {"r1": 0, "r2": RKB, "r3": 2 * RKB}
    _bump = dict(_rbase)

    def aalloc(shape, dt, region):
        assert shape[0] == 128
        esz = 4 if dt == F32 else 2
        nbytes = shape[1] * esz
        off = (_bump[region] + 511) // 512 * 512
        assert off + nbytes <= _rbase[region] + RKB, (region, off, nbytes)
        _bump[region] = off + nbytes
        return arena[:, off: off + nbytes].bitcast(dt)

    def areset(region, keep=0):
        _bump[region] = _rbase[region] + keep

    ident = alloc("ident", [128, 128], BF16)
    maskr = alloc("maskr", [128, 256], BF16)
    blkd = alloc("blkd", [128, 128], BF16)
    ones1b = alloc("ones1b", [1, 128], BF16)
    ones1f = alloc("ones1f", [1, 128], F32)
    bqkv = alloc("bqkv", [128, 24], F32)
    quad = alloc("quad", [128, 2], F32)
    qs2s = alloc("qs2s", [128, 1], F32)
    qnegs = alloc("qnegs", [128, 1], F32)
    qspp = alloc("qspp", [128, 1], F32)
    omg = alloc("omg", [128, 1024], BF16)
    gwb = alloc("gwb", [128, 256], BF16)
    boutb = alloc("boutb", [1, 1024], BF16)
    gbias = alloc("gbias", [1, 32], F32)
    lncol = alloc("lncol", [128, 8], F32)
    epsc = alloc("epsc", [128, 1], F32)
    qkx2 = [alloc(f"qkx2_{i}", [128, 512], BF16) for i in range(2)]
    sqs2 = alloc("sqs2", [128, 512], BF16)
    znT2 = alloc("znT2", [128, 512], BF16)
    poly2 = alloc("poly2", [128, 512], F32)

    pb = [e_(nc.psum_tensor(f"pb{i}", [128, 512], F32)) for i in range(8)]
    pbb = [b[:].bitcast(BF16) for b in pb]

    XT = aalloc([128, 8 * T], BF16, "r1")
    ATT = aalloc([128, NP * T], BF16, "r2")
    wpair = aalloc([128, 3072], BF16, "r3")
    r3_mark = _bump["r3"]
    xld = [aalloc([128, 1024], F32, "r3") for i in range(2)]
    xbf = aalloc([128, 1024], BF16, "r3")

    for nm, dst in (("ident_bf", ident), ("maskrep_bf", maskr), ("blkdiag_bf", blkd),
                    ("ones1_bf", ones1b), ("ones1_f", ones1f), ("bqkv_pp", bqkv),
                    ("quad_pp", quad), ("gbias_src", gbias)):
        P.dma("sync", lambda g, nm=nm, dst=dst: g.dma_start(out=dst[:], in_=d[nm]),
              sig=f"ld_{nm}", sem="si")

    P.alias("init_done", "si")
    P.op("vector", lambda g: g.memset(epsc[:], 1e-5), sig="v_eps", sem="ve")
    P.op("vector", lambda g: g.tensor_scalar_max(qs2s[:], quad[:, 0:1], 0.0),
         waits=["init_done"], sig="v_q1", sem="ve")
    P.op("scalar", lambda g: g.activation(qs2s[:], qs2s[:], AF.Sqrt, scale=2.0),
         waits=["v_q1"], sig="a_q1", sem="ae")
    P.op("vector", lambda g: g.tensor_scalar_mul(qnegs[:], quad[:, 0:1], -1.0),
         waits=["init_done"], sig="v_q2", sem="ve")
    P.op("vector", lambda g: g.tensor_scalar_max(qspp[:], quad[:, 1:2], 0.0),
         waits=["init_done"], sig="v_q3", sem="ve")
    P.op("scalar", lambda g: g.activation(qspp[:], qspp[:], AF.Sqrt), waits=["v_q3"], sig="a_q2", sem="ae")
    P.op("scalar", lambda g: g.activation(qspp[:], qspp[:], AF.Sqrt), waits=["a_q2"], sig="a_q3", sem="ae")
    P.op("vector", lambda g: g.tensor_scalar_mul(qspp[:], qspp[:], 32.0 ** -0.5),
         waits=["a_q3"], sig="v_q4", sem="ve")

    P.dma("gpsimd", lambda g: g.dma_start(out=omg[:], in_=d["omega_src"]),
          sig="g_omg_raw", sem="cv0")
    P.dma("gpsimd", lambda g: g.dma_start(out=gwb[:], in_=d["gw_src"]),
          sig="g_gw_raw", sem="cv0")
    P.dma("gpsimd", lambda g: g.dma_start(out=boutb[:], in_=d["bout_src"]),
          sig="g_bout_raw", sem="cv0")
    P.alias("g_omg", "cv0")
    P.alias("g_gw", "cv0")
    P.alias("g_bout", "cv0")

    def finish(waits):
        P.op("sync", None, waits=list(waits))
        P.flush()
        stack.close()
        return nc, dbg_t

    xr = d["x"].rearrange("(n p) c -> n p c", p=128)
    for t in range(32):
        lw = [f"v_xbf_{t - 2}"] if t >= 2 else []
        P.dma("sync", lambda g, t=t: g.dma_start(out=xld[t % 2][:], in_=xr[t]),
              waits=lw, sig=f"xld_{t}", sem=f"sx{t % 2}")
        bw = [f"xld_{t}"] + ([f"pe_x_{t - 1}_7"] if t >= 1 else [])
        P.op("vector", lambda g, t=t: g.tensor_copy(xbf[:], xld[t % 2][:]),
             waits=bw, sig=f"v_xbf_{t}", sem="ve")
        for gc in range(8):
            bk = 6 + (gc % 2)
            tw = [f"v_xbf_{t}", "init_done"]
            if (t, gc) > (0, 1):
                pt, pg = (t, gc - 2) if gc >= 2 else (t - 1, gc + 6)
                tw.append(f"a_x_{pt}_{pg}")
            P.op("tensor", lambda g, t=t, gc=gc, bk=bk: g.transpose(
                pbb[bk][:, 0:128], xbf[:, gc * 128:(gc + 1) * 128], ident[:]),
                waits=tw, sig=f"pe_x_{t}_{gc}", sem="pe")
            P.op("scalar", lambda g, t=t, gc=gc, bk=bk: g.copy(
                XT[:, gc * T + t * 128: gc * T + (t + 1) * 128], pbb[bk][:, 0:128]),
                waits=[f"pe_x_{t}_{gc}"], sig=f"a_x_{t}_{gc}", sem="ae")

    if stop == "phase0":
        return finish(["a_x_31_7"])

    areset("r3", keep=r3_mark - _rbase["r3"])
    qkf = [aalloc([128, T], BF16, "r3") for i in range(2)]
    vT = aalloc([128, T], BF16, "r3")
    qkts = [aalloc([128, 512], BF16, "r3") for i in range(2)] + qkx2
    sqs = [aalloc([128, 512], BF16, "r3"), sqs2]

    nrm2g = [aalloc([128, 512], F32, "r3") for i in range(2)]
    rinvbc = [aalloc([128, 512], F32, "r3") for i in range(2)]
    znT = [aalloc([128, 512], BF16, "r3"), znT2]
    poly = [aalloc([128, 512], F32, "r3"), poly2]
    Vt = aalloc([128, NCH * 130], BF16, "r3")
    Ktok = [aalloc([128, 128], BF16, "r3") for i in range(2)]
    upd = aalloc([128, 2 * 260], F32, "r3")
    stbf = aalloc([128, NCH * 65], BF16, "r3")
    stf = aalloc([128, 65], F32, "r3")
    dcol = aalloc([128, 128], F32, "r3")
    smk = [aalloc([128, 256], BF16, "r3") for i in range(2)]
    otok = [aalloc([128, 128], BF16, "r3") for i in range(2)]

    P.op("gpsimd", lambda g: g.memset(
        Vt[:].rearrange("p (ch j w) -> p ch j w", ch=NCH, j=2)[:, :, :, 64:65], 1.0),
        sig="g_vones", sem="gp")

    for p in range(NP):
        pw = []
        if p >= 1:
            pw.append(f"pe_vmm_{p - 1}_7")
        P.dma("gpsimd", lambda g, p=p: g.dma_start(out=wpair[:], in_=d["wqkv_src"][p]),
              waits=pw, sig=f"wp_{p}", sem="wp")

        for tb in range(8):
            for qk in range(2):
                fi = (p * 8 + tb) * 2 + qk
                mmw = [f"wp_{p}", "a_x_31_7"]
                if fi >= 2:
                    mmw.append(f"a_qke_{fi - 2}")
                if tb == 0 and qk == 0 and p >= 1:
                    mmw.append(f"v_smk_{p * NCH - 1}")
                for gc in range(8):
                    P.op("tensor", lambda g, p=p, tb=tb, qk=qk, gc=gc: g.matmul(
                        pb[qk][:], wpair[:, gc * 384 + qk * 128: gc * 384 + qk * 128 + 128],
                        XT[:, gc * T + tb * 512: gc * T + (tb + 1) * 512],
                        start=(gc == 0), stop=(gc == 7)),
                        waits=(mmw if gc == 0 else []),
                        sig=(f"pe_qk_{fi}" if gc == 7 else None), sem="pe")
            vw = [f"wp_{p}", "a_x_31_7"]
            vi = p * 8 + tb
            if vi >= 2:
                vw.append(f"a_ve_{vi - 2}")
            if tb == 0 and p >= 1:
                vw.append(f"v_ot_{p * NCH - 1}")
            vbk = 2 + (tb % 2)
            for gc in range(8):
                P.op("tensor", lambda g, p=p, tb=tb, gc=gc, vbk=vbk: g.matmul(
                    pb[vbk][:], wpair[:, gc * 384 + 256: gc * 384 + 384],
                    XT[:, gc * T + tb * 512: gc * T + (tb + 1) * 512],
                    start=(gc == 0), stop=(gc == 7)),
                    waits=(vw if gc == 0 else []),
                    sig=(f"pe_vmm_{p}_{tb}" if gc == 7 else None), sem="pe")
            P.op("scalar", lambda g, p=p, tb=tb, vbk=vbk: g.activation(
                vT[:, tb * 512:(tb + 1) * 512], pb[vbk][:], AF.Identity,
                bias=bqkv[:, 3 * p + 2: 3 * p + 3]),
                waits=[f"pe_vmm_{p}_{tb}", "init_done"], sig=f"a_ve_{vi}", sem="ae")

            for qk in range(2):
                fi = (p * 8 + tb) * 2 + qk
                ew = [f"pe_qk_{fi}", "init_done"]
                if fi >= 4:
                    ew.append(f"v_zn_{fi - 4}")
                P.op("scalar", lambda g, p=p, qk=qk, fi=fi: g.activation(
                    qkts[fi % 4][:], pb[qk][:], AF.Identity, bias=bqkv[:, 3 * p + qk: 3 * p + qk + 1]),
                    waits=ew, sig=f"a_qke_{fi}", sem="ae")
                fb = fi % 2
                nb = 4 + 2 * fb
                prb = 5 + 2 * fb
                sw = [f"a_qke_{fi}"]
                if fi >= 2:
                    sw.append(f"pe_n_{fi - 2}")
                P.op("scalar", lambda g, qk=qk, fi=fi, fb=fb: g.activation(sqs[fb][:], qkts[fi % 4][:], AF.Square),
                     waits=sw, sig=f"a_sq_{fi}", sem="ae")
                nw = [f"a_sq_{fi}", "init_done"]
                if fi >= 2:
                    nw.append(f"v_ng_{fi - 2}")
                if fi % 16 <= 1 and p >= 1:
                    nw.append(f"a_ot_{p * NCH - 1}")
                P.op("tensor", lambda g, fb=fb, nb=nb: g.matmul(pb[nb][:], blkd[:], sqs[fb][:], start=True, stop=True),
                     waits=nw, sig=f"pe_n_{fi}", sem="pe")
                gw_ = [f"pe_n_{fi}"]
                if fi >= 2:
                    gw_.append(f"a_prf_{fi - 2}")
                P.op("vector", lambda g, fb=fb, nb=nb: g.tensor_scalar_max(nrm2g[fb][:], pb[nb][:], 1e-24),
                     waits=gw_, sig=f"v_ng_{fi}", sem="ve")
                P.op("scalar", lambda g, fb=fb: g.activation(nrm2g[fb][:], nrm2g[fb][:], AF.Ln),
                     waits=[f"v_ng_{fi}"], sig=f"a_ln_{fi}", sem="ae")
                rw = [f"a_ln_{fi}"]
                if fi >= 2:
                    rw.append(f"v_f_{fi - 2}")
                P.op("scalar", lambda g, fb=fb: g.activation(rinvbc[fb][:], nrm2g[fb][:], AF.Exp, scale=-0.5),
                     waits=rw, sig=f"a_rinv_{fi}", sem="ae")
                zw = [f"a_rinv_{fi}"]
                if fi >= 2:
                    zw.append(f"pe_pr_{fi - 2}")
                P.op("vector", lambda g, qk=qk, fi=fi, fb=fb: g.tensor_tensor(znT[fb][:], qkts[fi % 4][:], rinvbc[fb][:], ALU.mult),
                     waits=zw, sig=f"v_zn_{fi}", sem="ve")
                prw = [f"v_zn_{fi}", "g_omg", "g_gw"]
                if fi >= 2:
                    prw.append(f"a_poly_{fi - 2}")
                if fi % 16 <= 1 and p >= 1:
                    prw.append(f"a_ot_{p * NCH - 1}")
                P.op("tensor", lambda g, p=p, fb=fb, prb=prb: g.matmul(
                    pb[prb][:], omg[:, p * 128:(p + 1) * 128], znT[fb][:], start=True, stop=True),
                    waits=prw, sig=f"pe_pr_{fi}", sem="pe")
                eaw = [f"pe_pr_{fi}", "a_q1", "v_q2"]
                P.op("vector", lambda g, fb=fb, prb=prb: g.tensor_scalar(nrm2g[fb][:], pb[prb][:], qs2s[:, 0:1],
                                                         qnegs[:, 0:1], ALU.mult, ALU.add),
                     waits=eaw, sig=f"v_ea_{fi}", sem="ve")
                P.op("vector", lambda g, fb=fb: g.tensor_scalar(nrm2g[fb][:], nrm2g[fb][:], 20.0, -20.0,
                                                         ALU.min, ALU.max),
                     waits=[f"v_ea_{fi}"], sig=f"v_cl_{fi}", sem="ve")
                P.op("scalar", lambda g, fb=fb: g.activation(rinvbc[fb][:], nrm2g[fb][:], AF.Exp),
                     waits=[f"v_cl_{fi}"], sig=f"a_prf_{fi}", sem="ae")
                pyw = [f"v_ea_{fi}", "v_q4"]
                if fi >= 2:
                    pyw.append(f"v_f_{fi - 2}")
                P.op("scalar", lambda g, fb=fb, prb=prb: g.activation(poly[fb][:], pb[prb][:], AF.Square, scale=qspp[:, 0:1]),
                     waits=pyw, sig=f"a_poly_{fi}", sem="ae")
                fw_ = [f"a_prf_{fi}", f"a_poly_{fi}"]
                if tb == 0 and p >= 1:
                    fw_.append(f"pe_ctx_{p * NCH - 1}")
                P.op("vector", lambda g, qk=qk, tb=tb, fb=fb: g.tensor_tensor(
                    qkf[qk][:, tb * 512:(tb + 1) * 512], rinvbc[fb][:], poly[fb][:], ALU.mult),
                    waits=fw_, sig=f"v_f_{fi}", sem="ve")

        if p == 0 and stop == "feat0":
            return finish([f"v_f_{16 - 1}", f"a_ve_{7}"])
        pair_f = f"v_f_{(p + 1) * 16 - 1}"
        P.op("vector", lambda g: g.memset(stf[:], 0.0),
             waits=([f"pe_ctx_{p * NCH - 1}"] if p >= 1 else []), sig=f"v_st0_{p}", sem="ve")
        for c4 in range(8):
            for j in range(4):
                ch = c4 * 4 + j
                k = p * NCH + ch
                vbk2 = 2 + (ch % 2)
                tw = ["g_vones", pair_f, f"a_ve_{p * 8 + 7}"]
                if k >= 2:
                    tw.append(f"a_vte_{k - 2}")
                P.op("tensor", lambda g, ch=ch, vbk2=vbk2: g.transpose(
                    pbb[vbk2][:, 0:128], vT[:, ch * 128:(ch + 1) * 128], ident[:]),
                    waits=tw, sig=f"pe_vtr_{p}_{ch}", sem="pe")
                P.op("scalar", lambda g, ch=ch, vbk2=vbk2: g.copy(
                    Vt[:, ch * 130:(ch + 1) * 130].rearrange("p (j w) -> p j w", j=2)[:, :, 0:64],
                    pbb[vbk2][:, 0:128].rearrange("p (j w) -> p j w", j=2)),
                    waits=[f"pe_vtr_{p}_{ch}"], sig=f"a_vte_{k}", sem="ae")
                kbk = 6 + (ch % 2)
                ktw = [f"a_kte_{k - 2}"] if k >= 2 else []
                P.op("tensor", lambda g, ch=ch, kbk=kbk: g.transpose(
                    pbb[kbk][:, 0:128], qkf[1][:, ch * 128:(ch + 1) * 128], ident[:]),
                    waits=ktw, sig=f"pe_ktr_{p}_{ch}", sem="pe")
                kew = [f"pe_ktr_{p}_{ch}"]
                if k >= 2:
                    gp_, gc_ = (k - 2) // NCH, (k - 2) % NCH
                    kew.append(f"pe_upd2_{gp_}_{gc_ // 4}_{gc_ % 4}")
                P.op("scalar", lambda g, ch=ch, kbk=kbk: g.copy(Ktok[ch % 2][:], pbb[kbk][:, 0:128]),
                     waits=kew, sig=f"a_kte_{k}", sem="ae")
                uw = [f"a_kte_{k}", f"a_vte_{k}"]
                if c4 >= 2 and j == 0:
                    uw.append(f"a_ue_{p}_{c4 - 2}")
                P.op("tensor", lambda g, ch=ch, j=j: g.matmul(
                    pb[4][0:64, j * 65:(j + 1) * 65], Ktok[ch % 2][:, 0:64],
                    Vt[:, ch * 130: ch * 130 + 65], start=True, stop=True),
                    waits=uw, sem="pe")
                P.op("tensor", lambda g, ch=ch, j=j: g.matmul(
                    pb[4][64:128, j * 65:(j + 1) * 65], Ktok[ch % 2][:, 64:128],
                    Vt[:, ch * 130 + 65: ch * 130 + 130], start=True, stop=True,
                    tile_position=(0, 64)),
                    waits=[], sig=f"pe_upd2_{p}_{c4}_{j}", sem="pe")
            ew = [f"pe_upd2_{p}_{c4}_3"]
            if c4 >= 2:
                ew.append(f"v_sta_{p}_{(c4 - 2) * 4 + 3}")
            elif p >= 1:
                ew.append(f"v_sta_{p - 1}_{(c4 + 6) * 4 + 3}")
            P.op("scalar", lambda g, c4=c4: g.copy(upd[:, (c4 % 2) * 260:(c4 % 2) * 260 + 260],
                                                   pb[4][:, 0:260]),
                 waits=ew, sig=f"a_ue_{p}_{c4}", sem="ae")
            for j2 in range(4):
                ch2 = c4 * 4 + j2
                cw = [f"v_st0_{p}"] if ch2 == 0 else [f"v_sta_{p}_{ch2 - 1}"]
                if p >= 1 and ch2 <= 1:
                    cw.append(f"pe_ctx_{p * NCH - 1}")
                P.op("vector", lambda g, ch2=ch2: g.tensor_copy(stbf[:, ch2 * 65:(ch2 + 1) * 65], stf[:]),
                     waits=cw, sig=f"v_stb_{p}_{ch2}", sem="ve")
                P.op("vector", lambda g, ch2=ch2, c4=c4, j2=j2: g.tensor_tensor(
                    stf[:], stf[:], upd[:, (c4 % 2) * 260 + j2 * 65:
                                         (c4 % 2) * 260 + j2 * 65 + 65], ALU.add),
                    waits=[f"a_ue_{p}_{c4}", f"v_stb_{p}_{ch2}"], sig=f"v_sta_{p}_{ch2}", sem="ve")

        if p == 0 and stop == "stageA0":
            return finish([f"v_sta_0_31", f"a_ue_0_7", f"a_vte_{NCH - 1}"])
        for ch in range(NCH):
            k = p * NCH + ch
            sbk = ch % 2
            sw = [f"v_stb_{p}_{ch}", pair_f]
            if k >= 2:
                sw.append(f"v_smk_{k - 2}")
            P.op("tensor", lambda g, ch=ch, sbk=sbk: g.matmul(
                pb[sbk][:, 0:128], qkf[1][0:64, ch * 128:(ch + 1) * 128],
                qkf[0][0:64, ch * 128:(ch + 1) * 128], start=True, stop=True),
                waits=sw, sem="pe")
            P.op("tensor", lambda g, sbk=sbk: g.matmul(
                pb[sbk][0:1, 500:501], ident[:, 0:1], ident[:, 0:1],
                start=True, stop=True), waits=[], sem="pe")
            P.op("tensor", lambda g, ch=ch, sbk=sbk: g.matmul(
                pb[sbk][:, 128:256], qkf[1][64:128, ch * 128:(ch + 1) * 128],
                qkf[0][64:128, ch * 128:(ch + 1) * 128], start=True, stop=True),
                waits=[], sig=f"pe_S_{k}", sem="pe")
            if p == 0 and ch == 0 and stop == "sbS":
                return finish([f"pe_S_{k}"])
            P.op("vector", lambda g, ch=ch, sbk=sbk: g.tensor_tensor(
                smk[ch % 2][:], pb[sbk][:, 0:256], maskr[:], ALU.mult),
                waits=[f"pe_S_{k}", "init_done"], sig=f"v_smk_{k}", sem="ve")
            if p == 0 and ch == 0 and stop == "sbM":
                return finish([f"v_smk_{k}"])
            cbk = 2 + (ch % 2)
            cw = [f"v_smk_{k}", f"a_vte_{p * NCH + NCH - 1}"]
            if k >= 2:
                cw.append(f"v_ot_{k - 2}")
            P.op("tensor", lambda g, ch=ch, cbk=cbk: g.matmul(
                pb[cbk][:, 0:65], smk[ch % 2][:, 0:128], Vt[:, ch * 130: ch * 130 + 65],
                start=True, stop=False), waits=cw, sem="pe")
            P.op("tensor", lambda g, ch=ch, cbk=cbk: g.matmul(
                pb[cbk][:, 0:65], qkf[0][0:64, ch * 128:(ch + 1) * 128],
                stbf[0:64, ch * 65:(ch + 1) * 65], start=False, stop=True),
                waits=[], sem="pe")
            P.op("tensor", lambda g, ch=ch, cbk=cbk: g.matmul(
                pb[cbk][:, 65:130], smk[ch % 2][:, 128:256], Vt[:, ch * 130 + 65: ch * 130 + 130],
                start=True, stop=False), waits=[], sem="pe")
            P.op("tensor", lambda g, ch=ch, cbk=cbk: g.matmul(
                pb[cbk][:, 65:130], qkf[0][64:128, ch * 128:(ch + 1) * 128],
                stbf[64:128, ch * 65:(ch + 1) * 65], start=False, stop=True),
                waits=[], sig=f"pe_ctx_{k}", sem="pe")
            if p == 0 and ch == 0 and stop == "sbC":
                return finish([f"pe_ctx_{k}"])
            def emit_otr(kk, chch):
                obk = 6 + (chch % 2)
                ow = [f"v_ot_{kk}"]
                if kk >= 2:
                    ow.append(f"a_ot_{kk - 2}")
                P.op("tensor", lambda g, chch=chch, obk=obk: g.transpose(
                    pbb[obk][:, 0:128], otok[chch % 2][:], ident[:]),
                    waits=ow, sig=f"pe_otr_{kk}", sem="pe")
                P.op("scalar", lambda g, p=p, chch=chch, obk=obk: g.copy(
                    ATT[:, p * T + chch * 128: p * T + (chch + 1) * 128], pbb[obk][:, 0:128]),
                    waits=[f"pe_otr_{kk}"], sig=f"a_ot_{kk}", sem="ae")
            if ch >= 1:
                emit_otr(k - 1, ch - 1)
            dw = [f"pe_ctx_{k}"]
            if k >= 2:
                dw.append(f"pe_otr_{k - 2}")
            for h01 in range(2):
                base = h01 * 65
                c0 = (ch * 2 + h01) % 64
                P.op("vector", lambda g, cbk=cbk, base=base, c0=c0: g.tensor_scalar_max(
                    dcol[:, c0:c0 + 1], pb[cbk][:, base + 64: base + 65], 1e-6),
                    waits=(dw if h01 == 0 else []),
                    sig=f"v_dx_{k}_{h01}", sem="ve")
                P.op("vector", lambda g, c0=c0: g.reciprocal(dcol[:, 64 + c0:65 + c0], dcol[:, c0:c0 + 1]),
                     waits=[f"v_dx_{k}_{h01}"], sig=f"v_dr_{k}_{h01}", sem="ve")
                P.op("vector", lambda g, ch=ch, cbk=cbk, base=base, h01=h01, c0=c0: g.tensor_scalar_mul(
                    otok[ch % 2][:, h01 * 64:(h01 + 1) * 64], pb[cbk][:, base: base + 64],
                    dcol[:, 64 + c0:65 + c0]),
                    waits=[f"v_dr_{k}_{h01}"], sig=f"v_om_{k}_{h01}", sem="ve")
            P.alias(f"v_ot_{k}", "ve")
            if ch == NCH - 1:
                emit_otr(k, ch)

        if stop == f"pair{p}":
            return finish([f"v_ot_{(p + 1) * NCH - 1}"])

    att_done = f"a_ot_{NP * NCH - 1}"
    tap_r3, tap_r2 = [], []
    if "t_qf" in dbg_t:
        P.dma("sync", lambda g: g.dma_start(out=dbg_t["t_qf"], in_=qkf[0][:]),
              waits=[f"v_f_{NP * 16 - 1}"], sig="tap_qf", sem="stp")
        tap_r3.append("tap_qf")
    if "t_kf" in dbg_t:
        P.dma("sync", lambda g: g.dma_start(out=dbg_t["t_kf"], in_=qkf[1][:]),
              waits=[f"v_f_{NP * 16 - 1}"], sig="tap_kf", sem="stp")
        tap_r3.append("tap_kf")
    if "t_vt" in dbg_t:
        P.dma("sync", lambda g: g.dma_start(out=dbg_t["t_vt"], in_=vT[:]),
              waits=[f"a_ve_{NP * 8 - 1}"], sig="tap_vt", sem="stp")
        tap_r3.append("tap_vt")
    if "t_attnT" in dbg_t:
        P.dma("sync", lambda g: g.dma_start(out=dbg_t["t_attnT"], in_=ATT[:]),
              waits=[att_done], sig="tap_att", sem="stp")
        tap_r2.append("tap_att")
    if tap_r3 or tap_r2:
        P.alias("tapsA_done", "stp")
        tap_r3 = ["tapsA_done"] if tap_r3 else []
        tap_r2 = ["tapsA_done"] if tap_r2 else []
    P.flush()

    areset("r1")
    areset("r3")
    woutb = aalloc([128, 8192], BF16, "r1")
    x2 = [aalloc([128, 1024], F32, "r1") for i in range(2)]
    s_sb = aalloc([128, 1024], F32, "r1")
    sq2 = aalloc([128, 1024], F32, "r1")
    y32 = [aalloc([128, 1024], F32, "r1") for i in range(2)]
    ybf = aalloc([128, 1024], BF16, "r1")
    YT = aalloc([128, 8 * T], BF16, "r3")

    P.dma("gpsimd", lambda g: g.dma_start(out=woutb[:], in_=d["wout_src"]),
          waits=[att_done], sig="ld_wout", sem="swo")

    for t in range(32):
        lw = [f"v_y_{t - 2}"] if t >= 2 else [att_done]
        P.dma("sync", lambda g, t=t: g.dma_start(out=x2[t % 2][:], in_=xr[t]),
              waits=lw, sig=f"x2_{t}", sem=f"sx{t % 2}")
        for oh in range(2):
            mw = ["ld_wout", att_done, "g_bout", "init_done"]
            if t >= 2:
                mw.append(f"v_s_{t - 2}_{oh}")
            for gc in range(8):
                P.op("tensor", lambda g, t=t, oh=oh, gc=gc: g.matmul(
                    pb[oh][:], ATT[:, gc * T + t * 128: gc * T + (t + 1) * 128],
                    woutb[:, gc * 1024 + oh * 512: gc * 1024 + (oh + 1) * 512],
                    start=(gc == 0), stop=False),
                    waits=(mw if gc == 0 else []), sem="pe")
            P.op("tensor", lambda g, oh=oh: g.matmul(
                pb[oh][:], ones1b[:], boutb[0:1, oh * 512:(oh + 1) * 512],
                start=False, stop=True),
                waits=[], sig=f"pe_op_{t}_{oh}", sem="pe")
        for oh in range(2):
            P.op("vector", lambda g, t=t, oh=oh: g.scalar_tensor_tensor(
                s_sb[:, oh * 512:(oh + 1) * 512], pb[oh][:], 1.0,
                x2[t % 2][:, oh * 512:(oh + 1) * 512], ALU.mult, ALU.add,
                accum_out=lncol[:, oh: oh + 1]),
                waits=[f"pe_op_{t}_{oh}", f"x2_{t}"]
                + ([f"a_sqs_{t - 1}_{oh}", f"v_y_{t - 1}"] if t >= 1 else []),
                sig=f"v_s_{t}_{oh}", sem="ve")
        P.op("vector", lambda g: g.tensor_tensor(lncol[:, 2:3], lncol[:, 0:1], lncol[:, 1:2], ALU.add),
             waits=[f"v_s_{t}_0", f"v_s_{t}_1"], sig=f"v_la_{t}", sem="ve")
        P.op("vector", lambda g: g.tensor_scalar_mul(lncol[:, 2:3], lncol[:, 2:3], 1.0 / 1024.0),
             waits=[f"v_la_{t}"], sig=f"v_mean_{t}", sem="ve")
        for oh in range(2):
            P.op("scalar", lambda g, oh=oh: g.activation(
                sq2[:, oh * 512:(oh + 1) * 512], s_sb[:, oh * 512:(oh + 1) * 512], AF.Square,
                accum_out=lncol[:, 5 + oh: 6 + oh]),
                waits=[f"v_s_{t}_{oh}"], sig=f"a_sqs_{t}_{oh}", sem="ae")
        P.op("vector", lambda g: g.tensor_tensor(lncol[:, 4:5], lncol[:, 5:6], lncol[:, 6:7], ALU.add),
             waits=[f"a_sqs_{t}_0", f"a_sqs_{t}_1"], sig=f"v_lb_{t}", sem="ve")
        P.op("vector", lambda g: g.tensor_tensor(lncol[:, 3:4], lncol[:, 2:3], lncol[:, 2:3], ALU.mult),
             waits=[f"v_mean_{t}"], sig=f"v_lc_{t}", sem="ve")
        P.op("vector", lambda g: g.tensor_scalar(lncol[:, 4:5], lncol[:, 4:5], 1.0 / 1024.0,
                                                 lncol[:, 3:4], ALU.mult, ALU.subtract),
             waits=[f"v_lb_{t}", f"v_lc_{t}"], sig=f"v_var_{t}", sem="ve")
        P.op("scalar", lambda g: g.activation(lncol[:, 4:5], lncol[:, 4:5], AF.Ln, bias=epsc[:, 0:1]),
             waits=[f"v_var_{t}", "v_eps"], sig=f"a_lnv_{t}", sem="ae")
        P.op("scalar", lambda g: g.activation(lncol[:, 7:8], lncol[:, 4:5], AF.Exp, scale=-0.5),
             waits=[f"a_lnv_{t}"], sig=f"a_rstd_{t}", sem="ae")
        P.op("vector", lambda g, t=t: g.tensor_scalar(
            y32[t % 2][:], s_sb[:], lncol[:, 2:3], lncol[:, 7:8], ALU.subtract, ALU.mult),
            waits=[f"a_rstd_{t}"] + ([f"ywr_{t - 2}"] if t >= 2 else []), sig=f"v_y_{t}", sem="ve")
        bw = [f"v_y_{t}"] + ([f"pe_yt_{t - 1}_7"] if t >= 1 else [])
        P.op("vector", lambda g, t=t: g.tensor_copy(ybf[:], y32[t % 2][:]),
             waits=bw, sig=f"v_ybf_{t}", sem="ve")
        P.dma("sync", lambda g, t=t: g.dma_start(out=scr["y_scr"][t], in_=y32[t % 2][:]),
              waits=[f"v_y_{t}"], sig=f"ywr_{t}", sem=f"yw{t % 2}")
        for gc in range(8):
            bk = 2 + (gc % 2)
            tw = [f"v_ybf_{t}"]
            if (t, gc) > (0, 1):
                pt, pg = (t, gc - 2) if gc >= 2 else (t - 1, gc + 6)
                tw.append(f"a_yt_{pt}_{pg}")
            elif t == 0:
                tw.append(f"v_ot_{NP * NCH - 1}")
                tw += tap_r3
            P.op("tensor", lambda g, t=t, gc=gc, bk=bk: g.transpose(
                pbb[bk][:, 0:128], ybf[:, gc * 128:(gc + 1) * 128], ident[:]),
                waits=tw, sig=f"pe_yt_{t}_{gc}", sem="pe")
            P.op("scalar", lambda g, t=t, gc=gc, bk=bk: g.copy(
                YT[:, gc * T + t * 128: gc * T + (t + 1) * 128], pbb[bk][:, 0:128]),
                waits=[f"pe_yt_{t}_{gc}"], sig=f"a_yt_{t}_{gc}", sem="ae")

    yt_done = "a_yt_31_7"
    if "t_y" in dbg_t:
        P.dma("sync", lambda g: g.dma_start(out=dbg_t["t_y"], in_=scr["y_scr"][:]),
              waits=[f"ywr_{31}", f"ywr_{30}"], sig="tap_y", sem="stp")
    if stop == "phaseB":
        w = [yt_done, "ywr_31", "ywr_30"]
        if "t_y" in dbg_t:
            w.append("tap_y")
        return finish(w)
    P.flush()

    areset("r2")
    WR = aalloc([128, 1024], F32, "r2")
    r2_mark = _bump["r2"]
    scn = aalloc([128, 256], F32, "r2")
    scc = aalloc([128, 256], F32, "r2")
    mskd = aalloc([128, 256], F32, "r2")
    selb = aalloc([128, 256], F32, "r2")
    gsml = aalloc([128, 64], F32, "r2")
    rcol = aalloc([128, 32], F32, "r2")
    biasbc = aalloc([128, 32], F32, "r2")

    P.op("tensor", lambda g: g.matmul(pb[4][:, 0:32], ones1f[:], gbias[:], start=True, stop=True),
         waits=["init_done", yt_done, f"v_ng_{NP * 16 - 1}"] + tap_r2, sig="pe_gb", sem="pe")
    P.op("scalar", lambda g: g.copy(biasbc[:], pb[4][:, 0:32]), waits=["pe_gb"], sig="a_gb", sem="ae")

    def v3(x, i, j):
        return x[:].rearrange("p (a b) -> p a b", a=i, b=j)

    for tg in range(4):
        for j in range(8):
            tile = tg * 8 + j
            mw = [yt_done, "g_gw"]
            if tg == 0 and j == 0:
                mw.append(f"a_poly_{NP * 16 - 1}")
            if tg >= 1:
                mw.append(f"v_wr_{tg - 1}")
            for gc in range(8):
                P.op("tensor", lambda g, tile=tile, j=j, gc=gc: g.matmul(
                    pb[5 + j // 4][:, (j % 4) * 128:(j % 4) * 128 + 32],
                    YT[:, gc * T + tile * 128: gc * T + (tile + 1) * 128],
                    gwb[:, gc * 32:(gc + 1) * 32],
                    start=(gc == 0), stop=(gc == 7)),
                    waits=(mw if gc == 0 else []),
                    sig=(f"pe_lg_{tile}" if gc == 7 else None), sem="pe")
        for j in range(8):
            P.op("scalar", lambda g, j=j: g.activation(
                scn[:, j * 32:(j + 1) * 32], pb[5 + j // 4][:, (j % 4) * 128:(j % 4) * 128 + 32],
                AF.Sigmoid),
                waits=[f"pe_lg_{tg * 8 + (j // 4) * 4 + 3}"], sig=f"a_sg_{tg}_{j}", sem="ae")
        vv = lambda nm: f"{nm}_{tg}"
        gm1, gm2 = gsml[:, 0:32], gsml[:, 32:64]
        P.op("vector", lambda g: g.tensor_tensor(
            v3(scc, 8, 32), v3(scn, 8, 32),
            biasbc[:].unsqueeze(1).broadcast_to([128, 8, 32]), ALU.add),
            waits=[f"a_sg_{tg}_{j}" for j in range(8)] + ["a_gb"], sig=vv("v_sc"), sem="ve")
        P.op("vector", lambda g: g.reduce_max(
            gm1.rearrange("p (t q) -> p t q", q=4), v3(scc, 32, 8), axis=AX.X),
            waits=[vv("v_sc")], sig=vv("v_gm1"), sem="ve")
        P.op("vector", lambda g: g.tensor_tensor(
            v3(mskd, 32, 8), v3(scc, 32, 8),
            gm1.unsqueeze(2).broadcast_to([128, 32, 8]), ALU.is_equal),
            waits=[vv("v_gm1")], sig=vv("v_eq1"), sem="ve")
        P.op("vector", lambda g: g.scalar_tensor_tensor(mskd[:], mskd[:], -1e30, scc[:],
                                                        ALU.mult, ALU.add),
             waits=[vv("v_eq1")], sig=vv("v_t1"), sem="ve")
        P.op("vector", lambda g: g.reduce_max(
            gm2.rearrange("p (t q) -> p t q", q=4), v3(mskd, 32, 8), axis=AX.X),
            waits=[vv("v_t1")], sig=vv("v_gm2"), sem="ve")
        P.op("vector", lambda g: g.tensor_tensor(gm1, gm1, gm2, ALU.add),
             waits=[vv("v_gm2")], sig=vv("v_gs"), sem="ve")
        P.op("vector", lambda g: g.reduce_max(
            rcol[:, 0:8], gm1.rearrange("p (t q) -> p t q", q=4), axis=AX.X),
            waits=[vv("v_gs")], sig=vv("v_g1"), sem="ve")
        P.op("vector", lambda g: g.tensor_tensor(
            gm2.rearrange("p (t q) -> p t q", q=4), gm1.rearrange("p (t q) -> p t q", q=4),
            rcol[:, 0:8].unsqueeze(2).broadcast_to([128, 8, 4]), ALU.is_equal),
            waits=[vv("v_g1")], sig=vv("v_eqg"), sem="ve")
        P.op("vector", lambda g: g.scalar_tensor_tensor(gm2, gm2, -1e30, gm1, ALU.mult, ALU.add),
             waits=[vv("v_eqg")], sig=vv("v_tg"), sem="ve")
        P.op("vector", lambda g: g.reduce_max(
            rcol[:, 8:16], gm2.rearrange("p (t q) -> p t q", q=4), axis=AX.X),
            waits=[vv("v_tg")], sig=vv("v_g2"), sem="ve")
        P.op("vector", lambda g: g.tensor_tensor(
            gm2.rearrange("p (t q) -> p t q", q=4), gm1.rearrange("p (t q) -> p t q", q=4),
            rcol[:, 8:16].unsqueeze(2).broadcast_to([128, 8, 4]), ALU.is_ge),
            waits=[vv("v_g2")], sig=vv("v_gmask"), sem="ve")
        P.op("vector", lambda g: g.tensor_tensor(
            v3(mskd, 32, 8), v3(scc, 32, 8),
            gm2.unsqueeze(2).broadcast_to([128, 32, 8]), ALU.mult),
            waits=[vv("v_gmask")], sig=vv("v_msk"), sem="ve")
        P.op("vector", lambda g: g.memset(selb[:], 0.0),
             waits=([f"v_wr_{tg - 1}"] if tg >= 1 else []), sig=vv("v_sel0"), sem="ve")
        prev = vv("v_msk")
        for it in range(8):
            P.op("vector", lambda g: g.reduce_max(rcol[:, 16:24], v3(mskd, 8, 32), axis=AX.X),
                 waits=[prev], sig=vv(f"v_m8_{it}"), sem="ve")
            P.op("vector", lambda g: g.tensor_tensor(
                v3(scc, 8, 32), v3(mskd, 8, 32),
                rcol[:, 16:24].unsqueeze(2).broadcast_to([128, 8, 32]), ALU.is_equal),
                waits=[vv(f"v_m8_{it}")], sig=vv(f"v_eq8_{it}"), sem="ve")
            P.op("vector", lambda g: g.tensor_tensor(selb[:], selb[:], scc[:], ALU.add),
                 waits=[vv(f"v_eq8_{it}"), vv("v_sel0")], sig=vv(f"v_sa_{it}"), sem="ve")
            P.op("vector", lambda g: g.scalar_tensor_tensor(mskd[:], scc[:], -1e30, mskd[:],
                                                            ALU.mult, ALU.add),
                 waits=[vv(f"v_sa_{it}")], sig=vv(f"v_mu_{it}"), sem="ve")
            prev = vv(f"v_mu_{it}")
        P.op("vector", lambda g: g.tensor_tensor(selb[:], selb[:], scn[:], ALU.mult),
             waits=[prev], sig=vv("v_w"), sem="ve")
        P.op("vector", lambda g: g.reduce_sum(rcol[:, 24:32], v3(selb, 8, 32), axis=AX.X),
             waits=[vv("v_w")], sig=vv("v_ws"), sem="ve")
        P.op("vector", lambda g: g.tensor_scalar_add(rcol[:, 24:32], rcol[:, 24:32], 1e-20),
             waits=[vv("v_ws")], sig=vv("v_pa"), sem="ve")
        P.op("vector", lambda g: g.reciprocal(rcol[:, 16:24], rcol[:, 24:32]),
             waits=[vv("v_pa")], sig=vv("v_pb"), sem="ve")
        P.op("vector", lambda g: g.tensor_scalar_mul(rcol[:, 16:24], rcol[:, 16:24], 2.5),
             waits=[vv("v_pb")], sig=vv("v_rs"), sem="ve")
        P.op("vector", lambda g, tg=tg: g.tensor_tensor(
            WR[:, tg * 256:(tg + 1) * 256].rearrange("p (t e) -> p t e", e=32),
            v3(selb, 8, 32),
            rcol[:, 16:24].unsqueeze(2).broadcast_to([128, 8, 32]), ALU.mult),
            waits=[vv("v_rs")], sig=f"v_wr_{tg}", sem="ve")

    if "t_wroute" in dbg_t:
        P.dma("sync", lambda g: g.dma_start(out=dbg_t["t_wroute"], in_=WR[:]),
              waits=["v_wr_3"], sig="tap_wr", sem="stp")
    if stop == "phaseC":
        w = ["v_wr_3"]
        if "t_wroute" in dbg_t:
            w.append("tap_wr")
        return finish(w)
    P.flush()

    areset("r1")
    areset("r2", keep=r2_mark - _rbase["r2"])
    ACC = aalloc([128, 16 * 1024], F32, "r1")
    WGb = [aalloc([128, 4096], BF16, "r2") for i in range(2)]
    WUb = [aalloc([128, 4096], BF16, "r2") for i in range(2)]
    WDb = [aalloc([128, 4096], BF16, "r2") for i in range(2)]
    Hb = [aalloc([128, 2048], BF16, "r2") for i in range(2)]
    SGb = aalloc([128, 512], F32, "r2")

    n_tb = 4
    outr = out.rearrange("(n p) c -> p n c", p=128)
    for hf in range(2):
        aw = [f"ywr_{31}", f"ywr_{30}", yt_done]
        if hf == 1:
            aw.append("out_0")
        P.dma("sync", lambda g, hf=hf: g.dma_start(
            out=ACC[:].rearrange("p (n c) -> p n c", c=1024),
            in_=scr["y_scr"].rearrange("n p c -> p n c")[:, hf * 16:(hf + 1) * 16, :]),
            waits=aw, sig=f"accld_{hf}", sem="sac")
        for e in range(NE):
            pbuf = e % 2
            k = hf * NE + e
            ew = []
            if k == 0:
                ew.append("v_wr_3")
            if k >= 2:
                ew.append(f"pe_dn_{k - 2}_{n_tb - 1}_7")
            P.dma("gpsimd", lambda g, e=e, pbuf=pbuf: g.dma_start(out=WGb[pbuf][:], in_=d["eg_src"][e]),
                  waits=ew, sig=f"wg_{hf}_{e}", sem=f"eb{e % 2}")
            P.dma("gpsimd", lambda g, e=e, pbuf=pbuf: g.dma_start(out=WUb[pbuf][:], in_=d["eu_src"][e]),
                  waits=[], sig=f"wu_{hf}_{e}", sem=f"eb{e % 2}")
            P.dma("gpsimd", lambda g, e=e, pbuf=pbuf: g.dma_start(out=WDb[pbuf][:], in_=d["ed_src"][e]),
                  waits=[], sig=f"wd_{hf}_{e}", sem=f"eb{e % 2}")
            for tb in range(n_tb):
                tbk = k * n_tb + tb
                for it in range(4):
                    ki = tbk * 4 + it
                    gbk, ubk = it % 2, 2 + it % 2
                    gw_ = [f"wd_{hf}_{e}", yt_done]
                    if ki >= 2:
                        gw_.append(f"v_h_{ki - 2}")
                    if ki == 0:
                        gw_ += ["v_s_31_0", "v_s_31_1", "a_yt_31_6", "a_yt_31_7"]
                    for gc in range(8):
                        P.op("tensor", lambda g, hf=hf, tb=tb, it=it, gc=gc, pbuf=pbuf, gbk=gbk: g.matmul(
                            pb[gbk][:], WGb[pbuf][:, gc * 512 + it * 128: gc * 512 + it * 128 + 128],
                            YT[:, gc * T + (hf * 4 + tb) * 512: gc * T + (hf * 4 + tb + 1) * 512],
                            start=(gc == 0), stop=(gc == 7)),
                            waits=(gw_ if gc == 0 else []),
                            sig=(f"pe_g_{ki}" if gc == 7 else None), sem="pe")
                    for gc in range(8):
                        P.op("tensor", lambda g, hf=hf, tb=tb, it=it, gc=gc, pbuf=pbuf, ubk=ubk: g.matmul(
                            pb[ubk][:], WUb[pbuf][:, gc * 512 + it * 128: gc * 512 + it * 128 + 128],
                            YT[:, gc * T + (hf * 4 + tb) * 512: gc * T + (hf * 4 + tb + 1) * 512],
                            start=(gc == 0), stop=(gc == 7)),
                            waits=[], sig=(f"pe_u_{ki}" if gc == 7 else None), sem="pe")
                    sw = [f"pe_g_{ki}"]
                    if ki >= 1:
                        sw.append(f"v_h_{ki - 1}")
                    P.op("scalar", lambda g, gbk=gbk: g.activation(SGb[:], pb[gbk][:], AF.Silu),
                         waits=sw, sig=f"a_sil_{ki}", sem="ae")
                    hw = [f"a_sil_{ki}", f"pe_u_{ki}"]
                    if tbk >= 2 and it == 0:
                        pk2 = tbk - 2
                        hw.append(f"pe_dn_{pk2 // n_tb}_{pk2 % n_tb}_7")
                    P.op("vector", lambda g, tb=tb, it=it, ubk=ubk: g.tensor_tensor(
                        Hb[tb % 2][:, it * 512:(it + 1) * 512], SGb[:], pb[ubk][:], ALU.mult),
                        waits=hw, sig=f"v_h_{ki}", sem="ve")
                for ts_ in range(4):
                    for oh in range(2):
                        nd = ts_ * 2 + oh
                        dbk = 4 + nd % 4
                        ndk = tbk * 8 + nd
                        dw = [f"v_h_{tbk * 4 + 3}", f"wd_{hf}_{e}"]
                        if ndk >= 4:
                            dw.append(f"v_dn_{ndk - 4}")
                        if ndk < 4:
                            dw += ["a_sg_3_3", "a_sg_3_7", "a_gb"]
                        for ic in range(4):
                            P.op("tensor", lambda g, tb=tb, ts_=ts_, oh=oh, ic=ic, pbuf=pbuf, dbk=dbk: g.matmul(
                                pb[dbk][:], Hb[tb % 2][:, ic * 512 + ts_ * 128: ic * 512 + ts_ * 128 + 128],
                                WDb[pbuf][:, ic * 1024 + oh * 512: ic * 1024 + oh * 512 + 512],
                                start=(ic == 0), stop=(ic == 3)),
                                waits=(dw if ic == 0 else []),
                                sig=(f"pe_dn_{k}_{tb}_{nd}" if ic == 3 else None), sem="pe")
                        vw = [f"pe_dn_{k}_{tb}_{nd}"]
                        if e == 0 and tb == 0:
                            vw.append(f"accld_{hf}")
                        if e >= 1:
                            vw.append(f"v_dn_{ndk - n_tb * 8}")
                        if e >= 2:
                            vw.append(f"v_wr_{(hf * 16 + tb * 4 + ts_) // 8}")
                        gtile = tb * 4 + ts_
                        a_sl = lambda gtile=gtile, oh=oh: ACC[:, gtile * 1024 + oh * 512: gtile * 1024 + (oh + 1) * 512]
                        if e < 2:
                            P.op("vector", lambda g, a_sl=a_sl, dbk=dbk: g.scalar_tensor_tensor(
                                a_sl(), pb[dbk][:], 1.0, a_sl(), ALU.mult, ALU.add),
                                waits=vw, sig=f"v_dn_{ndk}", sem="ve")
                        else:
                            P.op("vector", lambda g, a_sl=a_sl, dbk=dbk, hf=hf, e=e, gtile=gtile: g.scalar_tensor_tensor(
                                a_sl(), pb[dbk][:],
                                WR[:, (hf * 16 + gtile) * 32 + (e - 2): (hf * 16 + gtile) * 32 + (e - 1)],
                                a_sl(), ALU.mult, ALU.add),
                                waits=vw, sig=f"v_dn_{ndk}", sem="ve")
        last_dn = f"v_dn_{((hf * NE + NE - 1) * n_tb + n_tb - 1) * 8 + 7}"
        for tt in range(16):
            t = hf * 16 + tt
            a0 = ACC[:, tt * 1024: tt * 1024 + 512]
            a1 = ACC[:, tt * 1024 + 512: tt * 1024 + 1024]
            s0w = [last_dn]
            if tt >= 1:
                s0w += [f"a2_sq_{t - 1}_1", f"v2_y_{t - 1}"]
            P.op("vector", lambda g, a0=a0: g.tensor_scalar(
                SGb[:], a0, 0.0, None, ALU.add, ALU.add, accum_out=lncol[:, 0:1]),
                waits=s0w, sig=f"v2_s0_{t}", sem="ve")
            P.op("vector", lambda g, a1=a1: g.tensor_scalar(
                SGb[:], a1, 0.0, None, ALU.add, ALU.add, accum_out=lncol[:, 1:2]),
                waits=[f"v2_s0_{t}"], sig=f"v2_s1_{t}", sem="ve")
            P.op("vector", lambda g: g.tensor_tensor(lncol[:, 2:3], lncol[:, 0:1], lncol[:, 1:2], ALU.add),
                 waits=[f"v2_s0_{t}", f"v2_s1_{t}"], sig=f"v2_la_{t}", sem="ve")
            P.op("vector", lambda g: g.tensor_scalar_mul(lncol[:, 2:3], lncol[:, 2:3], 1.0 / 1024.0),
                 waits=[f"v2_la_{t}"], sig=f"v2_mean_{t}", sem="ve")
            for oh in range(2):
                P.op("scalar", lambda g, tt=tt, oh=oh: g.activation(
                    SGb[:],
                    ACC[:, tt * 1024 + oh * 512: tt * 1024 + (oh + 1) * 512],
                    AF.Square, accum_out=lncol[:, 5 + oh: 6 + oh]),
                    waits=[f"v2_s1_{t}"] + ([f"a2_sq_{t}_0"] if oh == 1 else []),
                    sig=f"a2_sq_{t}_{oh}", sem="ae")
            P.op("vector", lambda g: g.tensor_tensor(lncol[:, 4:5], lncol[:, 5:6], lncol[:, 6:7], ALU.add),
                 waits=[f"a2_sq_{t}_0", f"a2_sq_{t}_1"], sig=f"v2_lb_{t}", sem="ve")
            P.op("vector", lambda g: g.tensor_tensor(lncol[:, 3:4], lncol[:, 2:3], lncol[:, 2:3], ALU.mult),
                 waits=[f"v2_mean_{t}"], sig=f"v2_lc_{t}", sem="ve")
            P.op("vector", lambda g: g.tensor_scalar(lncol[:, 4:5], lncol[:, 4:5], 1.0 / 1024.0,
                                                     lncol[:, 3:4], ALU.mult, ALU.subtract),
                 waits=[f"v2_lb_{t}", f"v2_lc_{t}"], sig=f"v2_var_{t}", sem="ve")
            P.op("scalar", lambda g: g.activation(lncol[:, 4:5], lncol[:, 4:5], AF.Ln, bias=epsc[:, 0:1]),
                 waits=[f"v2_var_{t}", "v_eps"], sig=f"a2_lnv_{t}", sem="ae")
            P.op("scalar", lambda g: g.activation(lncol[:, 7:8], lncol[:, 4:5], AF.Exp, scale=-0.5),
                 waits=[f"a2_lnv_{t}"], sig=f"a2_rstd_{t}", sem="ae")
            P.op("vector", lambda g, tt=tt: g.tensor_scalar(
                ACC[:, tt * 1024:(tt + 1) * 1024], ACC[:, tt * 1024:(tt + 1) * 1024],
                lncol[:, 2:3], lncol[:, 7:8], ALU.subtract, ALU.mult),
                waits=[f"a2_rstd_{t}"], sig=f"v2_y_{t}", sem="ve")
        P.dma("sync", lambda g, hf=hf: g.dma_start(
            out=outr[:, hf * 16:(hf + 1) * 16, :],
            in_=ACC[:].rearrange("p (n c) -> p n c", c=1024)),
            waits=[f"v2_y_{hf * 16 + tt2}" for tt2 in range(16)], sig=f"out_{hf}", sem="sout")

    fw = ["out_0", "out_1"]
    if P.counts["stp"] > 0:
        P.alias("taps_all", "stp")
        fw.append("taps_all")
    P.op("sync", None, waits=fw)
    P.flush()

    stack.close()
    return nc, dbg_t


def host_prep(ins):
    f32 = np.float32
    w_qkv = np.asarray(ins["L0_attn_w_qkv"], f32)
    wq = w_qkv[0:1024].reshape(16, 64, 1024)
    wk = w_qkv[1024:2048].reshape(16, 64, 1024)
    wv = w_qkv[2048:3072].reshape(16, 64, 1024)
    blocks = []
    for p in range(NP):
        h0, h1 = 2 * p, 2 * p + 1
        cols = np.concatenate([wq[h0].T, wq[h1].T, wk[h0].T, wk[h1].T,
                               wv[h0].T, wv[h1].T], axis=1)
        blocks.append(cols)
    wqkv_src = np.stack(blocks).reshape(NP, 8, 128, 384).transpose(0, 2, 1, 3).reshape(NP, 128, 3072)

    woutT = np.asarray(ins["L0_attn_w_out"], f32).T
    wout_src = woutT.reshape(8, 128, 1024).transpose(1, 0, 2).reshape(128, 8192)

    sgT = np.asarray(ins["L0_ffn_shared_gate"], f32).T
    suT = np.asarray(ins["L0_ffn_shared_up"], f32).T
    sdT = np.asarray(ins["L0_ffn_shared_down"], f32).T
    egT = np.transpose(np.asarray(ins["L0_ffn_expert_gate"], f32), (0, 2, 1))
    euT = np.transpose(np.asarray(ins["L0_ffn_expert_up"], f32), (0, 2, 1))
    edT = np.transpose(np.asarray(ins["L0_ffn_expert_down"], f32), (0, 2, 1))
    eg_full = np.concatenate([sgT[None, :, 0:512], sgT[None, :, 512:1024], egT], 0)
    eu_full = np.concatenate([suT[None, :, 0:512], suT[None, :, 512:1024], euT], 0)
    ed_full = np.concatenate([sdT[None, 0:512, :], sdT[None, 512:1024, :], edT], 0)
    eg_src = eg_full.reshape(NE, 8, 128, 512).transpose(0, 2, 1, 3).reshape(NE, 128, 4096)
    eu_src = eu_full.reshape(NE, 8, 128, 512).transpose(0, 2, 1, 3).reshape(NE, 128, 4096)
    ed_src = ed_full.reshape(NE, 4, 128, 1024).transpose(0, 2, 1, 3).reshape(NE, 128, 4096)

    gw_src = np.asarray(ins["L0_ffn_gate_w"], f32).T.reshape(8, 128, 32).transpose(1, 0, 2).reshape(128, 256)

    omega = np.asarray(ins["L0_attn_omega"], f32)
    omcat = np.transpose(omega, (1, 2, 0, 3)).reshape(16, 64, 64)
    omega_src = np.zeros((128, 1024), f32)
    for p in range(NP):
        omega_src[0:64, p * 128: p * 128 + 64] = omcat[2 * p]
        omega_src[64:128, p * 128 + 64: p * 128 + 128] = omcat[2 * p + 1]

    b_qkv = np.asarray(ins["L0_attn_b_qkv"], f32)
    bq = b_qkv[0:1024].reshape(16, 64)
    bk = b_qkv[1024:2048].reshape(16, 64)
    bv = b_qkv[2048:3072].reshape(16, 64)
    cols = []
    for p in range(NP):
        cols.append(np.concatenate([bq[2 * p], bq[2 * p + 1]]))
        cols.append(np.concatenate([bk[2 * p], bk[2 * p + 1]]))
        cols.append(np.concatenate([bv[2 * p], bv[2 * p + 1]]))
    bqkv_pp = np.stack(cols, axis=1)

    qn = np.asarray(ins["L0_attn_quad_nodes"], f32)
    qw = np.asarray(ins["L0_attn_quad_weights"], f32)
    pat = np.concatenate([np.repeat(qn, 32), np.repeat(qn, 32)])
    patw = np.concatenate([np.repeat(qw, 32), np.repeat(qw, 32)])
    quad_pp = np.stack([pat, patw], axis=1).astype(f32)

    bf = ml_dtypes.bfloat16
    blkdiag = np.zeros((128, 128), f32)
    blkdiag[0:64, 0:64] = 1.0
    blkdiag[64:128, 64:128] = 1.0
    common = dict(
        wqkv_src=np.ascontiguousarray(wqkv_src),
        wout_src=np.ascontiguousarray(wout_src),
        eg_src=np.ascontiguousarray(eg_src),
        eu_src=np.ascontiguousarray(eu_src),
        ed_src=np.ascontiguousarray(ed_src),
        gw_src=np.ascontiguousarray(gw_src),
        omega_src=omega_src,
        bqkv_pp=np.ascontiguousarray(bqkv_pp),
        bout_src=np.asarray(ins["L0_attn_b_out"], f32).reshape(1, 1024),
        quad_pp=quad_pp,
        gbias_src=np.asarray(ins["L0_ffn_gate_bias"], f32).reshape(1, 32),
        ident_bf=np.eye(128, dtype=bf),
        maskrep_bf=np.tile(np.triu(np.ones((128, 128), np.float32)), (1, 2)).astype(bf),
        blkdiag_bf=blkdiag.astype(bf),
        ones1_bf=np.ones((1, 128), bf),
        ones1_f=np.ones((1, 128), f32),
    )
    return common


def kernel(**ins):
    common = host_prep(ins)
    x_all = np.asarray(ins["L0_attn_x"], np.float32)
    nc, _ = build()
    in_maps = [dict(common, x=np.ascontiguousarray(x_all[b])) for b in range(8)]
    res = run_bass_kernel_spmd(nc, in_maps, core_ids=list(range(8)))
    return np.stack([res.results[b]["out"] for b in range(8)], axis=0)


# version 06
# speedup vs baseline: 1.3531x; 1.0261x over previous
import numpy as np
import ml_dtypes
import concourse.bass as bass
import concourse.mybir as mybir
from concourse.bass_utils import run_bass_kernel_spmd

F32 = mybir.dt.float32
BF16 = mybir.dt.bfloat16
AF = mybir.ActivationFunctionType
ALU = mybir.AluOpType
AX = mybir.AxisListType

T = 4096
NP = 8
NCH = 32
NE = 34
CONV_CHUNK = 256

ENGINES = ("sync", "tensor", "vector", "scalar", "gpsimd")


class Prog:

    def __init__(self, nc, sems):
        self.nc = nc
        self.sems = sems
        self.counts = {k: 0 for k in sems}
        self.ev = {}
        self.pending = {e: [] for e in ENGINES}
        self.maxwait = {e: {} for e in ENGINES}

    def op(self, eng, fn, waits=(), sig=None, sem=None, amt=1):
        for w in waits:
            assert w in self.ev, f"wait on undefined event {w}"
        if sig is not None:
            assert sem is not None
            self.counts[sem] += amt
            assert sig not in self.ev, f"event {sig} redefined"
            self.ev[sig] = (sem, self.counts[sem])
        self.pending[eng].append((fn, tuple(waits), sig, sem, amt))

    def dma(self, eng, fn, waits=(), sig=None, sem=None):
        assert sem is not None
        self.op(eng, fn, waits, sig, sem, amt=16)

    def alias(self, name, sem):
        assert name not in self.ev
        self.ev[name] = (sem, self.counts[sem])

    def flush(self):
        nc = self.nc
        if not any(self.pending.values()):
            return
        with nc.Block() as block:
            for eng_name in ENGINES:
                ops = self.pending[eng_name]
                if not ops:
                    continue
                mw = self.maxwait[eng_name]

                def emit(eng, ops=ops, mw=mw):
                    for fn, waits, sig, sem, amt in ops:
                        for w in waits:
                            wsem, wval = self.ev[w]
                            if wval <= mw.get(wsem, 0):
                                continue
                            mw[wsem] = wval
                            eng.wait_ge(self.sems[wsem], wval)
                        instr = fn(eng) if fn is not None else None
                        if sig is not None:
                            instr.then_inc(self.sems[sem], amt)
                getattr(block, eng_name)(emit)
        self.pending = {e: [] for e in ENGINES}


def build(dbg=(), stop=None):
    nc = bass.Bass("TRN2", target_bir_lowering=False)
    d = {}

    def din(name, shape, dt=F32):
        d[name] = nc.dram_tensor(name, shape, dt, kind="ExternalInput").ap()
    din("x", [T, 1024])
    din("wqkv_src", [NP, 128, 3072])
    din("wout_src", [128, 8192])
    din("eg_src", [NE, 128, 4096])
    din("eu_src", [NE, 128, 4096])
    din("ed_src", [NE, 128, 4096])
    din("gw_src", [128, 256])
    din("omega_src", [128, 1024])
    din("bqkv_pp", [128, 24])
    din("bout_src", [1, 1024])
    din("quad_pp", [128, 2])
    din("gbias_src", [1, 32])
    din("ident_bf", [128, 128], BF16)
    din("maskrep_bf", [128, 256], BF16)
    din("blkdiag_bf", [128, 128], BF16)
    din("ones1_bf", [1, 128], BF16)
    din("ones1_f", [1, 128])

    out = nc.dram_tensor("out", [T, 1024], F32, kind="ExternalOutput").ap()
    dbg_t = {}
    dbg_spec = {"t_attnT": ([128, NP * T], BF16), "t_y": ([32, 128, 1024], F32),
                "t_wroute": ([128, 1024], F32), "t_qf": ([128, T], BF16),
                "t_kf": ([128, T], BF16), "t_vt": ([128, T], BF16)}
    for nm in dbg:
        shape, dt = dbg_spec[nm]
        dbg_t[nm] = nc.dram_tensor(nm, shape, dt, kind="ExternalOutput").ap()

    scr = {}
    def dscr(name, shape, dt=BF16):
        scr[name] = nc.dram_tensor(name, shape, dt, kind="Internal").ap()
    dscr("y_scr", [32, 128, 1024], F32)

    from contextlib import ExitStack
    stack = ExitStack()
    e_ = stack.enter_context

    sems = {k: e_(nc.semaphore(k)) for k in (
        "pe", "ve", "ae", "gp", "si", "sx0", "sx1", "cv0", "cv1", "gq0", "gq1",
        "wp", "swo", "yw0", "yw1", "eb0", "eb1", "sac", "sout", "stp")}
    P = Prog(nc, sems)

    sbg = {}
    def alloc(name, shape, dt):
        g = nc.sbuf_tensor(name, shape, dt)
        h = g.__enter__()
        sbg[name] = g
        return h

    U8 = mybir.dt.uint8
    RKB = 64 * 1024
    arena = alloc("arena", [128, 3 * RKB], U8)
    _rbase = {"r1": 0, "r2": RKB, "r3": 2 * RKB}
    _bump = dict(_rbase)

    def aalloc(shape, dt, region):
        assert shape[0] == 128
        esz = 4 if dt == F32 else 2
        nbytes = shape[1] * esz
        off = (_bump[region] + 511) // 512 * 512
        assert off + nbytes <= _rbase[region] + RKB, (region, off, nbytes)
        _bump[region] = off + nbytes
        return arena[:, off: off + nbytes].bitcast(dt)

    def areset(region, keep=0):
        _bump[region] = _rbase[region] + keep

    ident = alloc("ident", [128, 128], BF16)
    maskr = alloc("maskr", [128, 256], BF16)
    blkd = alloc("blkd", [128, 128], BF16)
    ones1b = alloc("ones1b", [1, 128], BF16)
    ones1f = alloc("ones1f", [1, 128], F32)
    bqkv = alloc("bqkv", [128, 24], F32)
    quad = alloc("quad", [128, 2], F32)
    qs2s = alloc("qs2s", [128, 1], F32)
    qnegs = alloc("qnegs", [128, 1], F32)
    qspp = alloc("qspp", [128, 1], F32)
    omg = alloc("omg", [128, 1024], BF16)
    gwb = alloc("gwb", [128, 256], BF16)
    boutb = alloc("boutb", [1, 1024], BF16)
    gbias = alloc("gbias", [1, 32], F32)
    lncol = alloc("lncol", [128, 8], F32)
    epsc = alloc("epsc", [128, 1], F32)
    qkx2 = [alloc(f"qkx2_{i}", [128, 512], BF16) for i in range(2)]
    sqs2 = alloc("sqs2", [128, 512], BF16)
    znT2 = alloc("znT2", [128, 512], BF16)
    poly2 = alloc("poly2", [128, 512], F32)

    pb = [e_(nc.psum_tensor(f"pb{i}", [128, 512], F32)) for i in range(8)]
    pbb = [b[:].bitcast(BF16) for b in pb]

    XT = aalloc([128, 8 * T], BF16, "r1")
    ATT = aalloc([128, NP * T], BF16, "r2")
    wpair = aalloc([128, 3072], BF16, "r3")
    r3_mark = _bump["r3"]
    xld = [aalloc([128, 1024], F32, "r3") for i in range(2)]
    xbf = aalloc([128, 1024], BF16, "r3")

    for nm, dst in (("ident_bf", ident), ("maskrep_bf", maskr), ("blkdiag_bf", blkd),
                    ("ones1_bf", ones1b), ("ones1_f", ones1f), ("bqkv_pp", bqkv),
                    ("quad_pp", quad), ("gbias_src", gbias)):
        P.dma("sync", lambda g, nm=nm, dst=dst: g.dma_start(out=dst[:], in_=d[nm]),
              sig=f"ld_{nm}", sem="si")

    P.alias("init_done", "si")
    P.op("vector", lambda g: g.memset(epsc[:], 1e-5), sig="v_eps", sem="ve")
    P.op("vector", lambda g: g.tensor_scalar_max(qs2s[:], quad[:, 0:1], 0.0),
         waits=["init_done"], sig="v_q1", sem="ve")
    P.op("scalar", lambda g: g.activation(qs2s[:], qs2s[:], AF.Sqrt, scale=2.0),
         waits=["v_q1"], sig="a_q1", sem="ae")
    P.op("vector", lambda g: g.tensor_scalar_mul(qnegs[:], quad[:, 0:1], -1.0),
         waits=["init_done"], sig="v_q2", sem="ve")
    P.op("vector", lambda g: g.tensor_scalar_max(qspp[:], quad[:, 1:2], 0.0),
         waits=["init_done"], sig="v_q3", sem="ve")
    P.op("scalar", lambda g: g.activation(qspp[:], qspp[:], AF.Sqrt), waits=["v_q3"], sig="a_q2", sem="ae")
    P.op("scalar", lambda g: g.activation(qspp[:], qspp[:], AF.Sqrt), waits=["a_q2"], sig="a_q3", sem="ae")
    P.op("vector", lambda g: g.tensor_scalar_mul(qspp[:], qspp[:], 32.0 ** -0.5),
         waits=["a_q3"], sig="v_q4", sem="ve")

    P.dma("gpsimd", lambda g: g.dma_start(out=omg[:], in_=d["omega_src"]),
          sig="g_omg_raw", sem="cv0")
    P.dma("gpsimd", lambda g: g.dma_start(out=gwb[:], in_=d["gw_src"]),
          sig="g_gw_raw", sem="cv0")
    P.dma("gpsimd", lambda g: g.dma_start(out=boutb[:], in_=d["bout_src"]),
          sig="g_bout_raw", sem="cv0")
    P.alias("g_omg", "cv0")
    P.alias("g_gw", "cv0")
    P.alias("g_bout", "cv0")

    def finish(waits):
        P.op("sync", None, waits=list(waits))
        P.flush()
        stack.close()
        return nc, dbg_t

    xr = d["x"].rearrange("(n p) c -> n p c", p=128)
    for t in range(32):
        lw = [f"v_xbf_{t - 2}"] if t >= 2 else []
        P.dma("sync", lambda g, t=t: g.dma_start(out=xld[t % 2][:], in_=xr[t]),
              waits=lw, sig=f"xld_{t}", sem=f"sx{t % 2}")
        bw = [f"xld_{t}"] + ([f"pe_x_{t - 1}_7"] if t >= 1 else [])
        P.op("vector", lambda g, t=t: g.tensor_copy(xbf[:], xld[t % 2][:]),
             waits=bw, sig=f"v_xbf_{t}", sem="ve")
        for gc in range(8):
            bk = 6 + (gc % 2)
            tw = [f"v_xbf_{t}", "init_done"]
            if (t, gc) > (0, 1):
                pt, pg = (t, gc - 2) if gc >= 2 else (t - 1, gc + 6)
                tw.append(f"a_x_{pt}_{pg}")
            P.op("tensor", lambda g, t=t, gc=gc, bk=bk: g.transpose(
                pbb[bk][:, 0:128], xbf[:, gc * 128:(gc + 1) * 128], ident[:]),
                waits=tw, sig=f"pe_x_{t}_{gc}", sem="pe")
            P.op("scalar", lambda g, t=t, gc=gc, bk=bk: g.copy(
                XT[:, gc * T + t * 128: gc * T + (t + 1) * 128], pbb[bk][:, 0:128]),
                waits=[f"pe_x_{t}_{gc}"], sig=f"a_x_{t}_{gc}", sem="ae")

    if stop == "phase0":
        return finish(["a_x_31_7"])

    areset("r3", keep=r3_mark - _rbase["r3"])
    qkf = [aalloc([128, T], BF16, "r3") for i in range(2)]
    vT = aalloc([128, T], BF16, "r3")
    qkts = [aalloc([128, 512], BF16, "r3") for i in range(2)] + qkx2
    sqs = [aalloc([128, 512], BF16, "r3"), sqs2]

    nrm2g = [aalloc([128, 512], F32, "r3") for i in range(2)]
    rinvbc = [aalloc([128, 512], F32, "r3") for i in range(2)]
    znT = [aalloc([128, 512], BF16, "r3"), znT2]
    poly = [aalloc([128, 512], F32, "r3"), poly2]
    Vt = aalloc([128, NCH * 130], BF16, "r3")
    Ktok = [aalloc([128, 128], BF16, "r3") for i in range(2)]
    upd = aalloc([128, 2 * 260], F32, "r3")
    stbf = aalloc([128, NCH * 65], BF16, "r3")
    stf = aalloc([128, 65], F32, "r3")
    dcol = aalloc([128, 128], F32, "r3")
    smk = [aalloc([128, 256], BF16, "r3") for i in range(2)]
    otok = [aalloc([128, 128], BF16, "r3") for i in range(2)]

    P.op("gpsimd", lambda g: g.memset(
        Vt[:].rearrange("p (ch j w) -> p ch j w", ch=NCH, j=2)[:, :, :, 64:65], 1.0),
        sig="g_vones", sem="gp")

    for p in range(NP):
        pw = []
        if p >= 1:
            pw.append(f"pe_vmm_{p - 1}_7")
        P.dma("gpsimd", lambda g, p=p: g.dma_start(out=wpair[:], in_=d["wqkv_src"][p]),
              waits=pw, sig=f"wp_{p}", sem="wp")

        for tb in range(8):
            for qk in range(2):
                fi = (p * 8 + tb) * 2 + qk
                mmw = [f"wp_{p}", "a_x_31_7"]
                if fi >= 2:
                    mmw.append(f"a_qke_{fi - 2}")
                if tb == 0 and qk == 0 and p >= 1:
                    mmw.append(f"v_smk_{p * NCH - 1}")
                for gc in range(8):
                    P.op("tensor", lambda g, p=p, tb=tb, qk=qk, gc=gc: g.matmul(
                        pb[qk][:], wpair[:, gc * 384 + qk * 128: gc * 384 + qk * 128 + 128],
                        XT[:, gc * T + tb * 512: gc * T + (tb + 1) * 512],
                        start=(gc == 0), stop=(gc == 7)),
                        waits=(mmw if gc == 0 else []),
                        sig=(f"pe_qk_{fi}" if gc == 7 else None), sem="pe")
            vw = [f"wp_{p}", "a_x_31_7"]
            vi = p * 8 + tb
            if vi >= 2:
                vw.append(f"a_ve_{vi - 2}")
            if tb == 0 and p >= 1:
                vw.append(f"v_ot_{p * NCH - 1}")
            vbk = 2 + (tb % 2)
            for gc in range(8):
                P.op("tensor", lambda g, p=p, tb=tb, gc=gc, vbk=vbk: g.matmul(
                    pb[vbk][:], wpair[:, gc * 384 + 256: gc * 384 + 384],
                    XT[:, gc * T + tb * 512: gc * T + (tb + 1) * 512],
                    start=(gc == 0), stop=(gc == 7)),
                    waits=(vw if gc == 0 else []),
                    sig=(f"pe_vmm_{p}_{tb}" if gc == 7 else None), sem="pe")
            P.op("scalar", lambda g, p=p, tb=tb, vbk=vbk: g.activation(
                vT[:, tb * 512:(tb + 1) * 512], pb[vbk][:], AF.Identity,
                bias=bqkv[:, 3 * p + 2: 3 * p + 3]),
                waits=[f"pe_vmm_{p}_{tb}", "init_done"], sig=f"a_ve_{vi}", sem="ae")

            def feat_stage1(fi, qk):
                fb = fi % 2
                nb = 4 + 2 * fb
                ew = [f"pe_qk_{fi}", "init_done"]
                if fi >= 4:
                    ew.append(f"v_zn_{fi - 4}")
                P.op("scalar", lambda g, p=p, qk=qk, fi=fi: g.activation(
                    qkts[fi % 4][:], pb[qk][:], AF.Identity, bias=bqkv[:, 3 * p + qk: 3 * p + qk + 1]),
                    waits=ew, sig=f"a_qke_{fi}", sem="ae")
                sw = [f"a_qke_{fi}"]
                if fi >= 2:
                    sw.append(f"pe_n_{fi - 2}")
                P.op("scalar", lambda g, fi=fi, fb=fb: g.activation(sqs[fb][:], qkts[fi % 4][:], AF.Square),
                     waits=sw, sig=f"a_sq_{fi}", sem="ae")
                nw = [f"a_sq_{fi}", "init_done"]
                if fi >= 2:
                    nw.append(f"v_ng_{fi - 2}")
                if fi % 16 <= 1 and p >= 1:
                    nw.append(f"a_ot_{p * NCH - 1}")
                P.op("tensor", lambda g, fb=fb, nb=nb: g.matmul(pb[nb][:], blkd[:], sqs[fb][:], start=True, stop=True),
                     waits=nw, sig=f"pe_n_{fi}", sem="pe")
                gw_ = [f"pe_n_{fi}"]
                if fi >= 2:
                    gw_.append(f"a_prf_{fi - 2}")
                P.op("vector", lambda g, fb=fb, nb=nb: g.tensor_scalar_max(nrm2g[fb][:], pb[nb][:], 1e-24),
                     waits=gw_, sig=f"v_ng_{fi}", sem="ve")
                P.op("scalar", lambda g, fb=fb: g.activation(nrm2g[fb][:], nrm2g[fb][:], AF.Ln),
                     waits=[f"v_ng_{fi}"], sig=f"a_ln_{fi}", sem="ae")
                rw = [f"a_ln_{fi}"]
                if fi >= 2:
                    rw.append(f"v_f_{fi - 2}")
                P.op("scalar", lambda g, fb=fb: g.activation(rinvbc[fb][:], nrm2g[fb][:], AF.Exp, scale=-0.5),
                     waits=rw, sig=f"a_rinv_{fi}", sem="ae")

            def feat_stage2(fi, qk):
                fb = fi % 2
                prb = 5 + 2 * fb
                tb2 = (fi % 16) // 2
                zw = [f"a_rinv_{fi}"]
                if fi >= 2:
                    zw.append(f"pe_pr_{fi - 2}")
                P.op("vector", lambda g, fi=fi, fb=fb: g.tensor_tensor(znT[fb][:], qkts[fi % 4][:], rinvbc[fb][:], ALU.mult),
                     waits=zw, sig=f"v_zn_{fi}", sem="ve")
                prw = [f"v_zn_{fi}", "g_omg", "g_gw"]
                if fi >= 2:
                    prw.append(f"a_poly_{fi - 2}")
                if fi % 16 <= 1 and p >= 1:
                    prw.append(f"a_ot_{p * NCH - 1}")
                P.op("tensor", lambda g, p=p, fb=fb, prb=prb: g.matmul(
                    pb[prb][:], omg[:, p * 128:(p + 1) * 128], znT[fb][:], start=True, stop=True),
                    waits=prw, sig=f"pe_pr_{fi}", sem="pe")
                eaw = [f"pe_pr_{fi}", "a_q1", "v_q2"]
                P.op("vector", lambda g, fb=fb, prb=prb: g.tensor_scalar(nrm2g[fb][:], pb[prb][:], qs2s[:, 0:1],
                                                         qnegs[:, 0:1], ALU.mult, ALU.add),
                     waits=eaw, sig=f"v_ea_{fi}", sem="ve")
                P.op("vector", lambda g, fb=fb: g.tensor_scalar(nrm2g[fb][:], nrm2g[fb][:], 20.0, -20.0,
                                                         ALU.min, ALU.max),
                     waits=[f"v_ea_{fi}"], sig=f"v_cl_{fi}", sem="ve")
                P.op("scalar", lambda g, fb=fb: g.activation(rinvbc[fb][:], nrm2g[fb][:], AF.Exp),
                     waits=[f"v_cl_{fi}"], sig=f"a_prf_{fi}", sem="ae")
                pyw = [f"v_ea_{fi}", "v_q4"]
                if fi >= 2:
                    pyw.append(f"v_f_{fi - 2}")
                P.op("scalar", lambda g, fb=fb, prb=prb: g.activation(poly[fb][:], pb[prb][:], AF.Square, scale=qspp[:, 0:1]),
                     waits=pyw, sig=f"a_poly_{fi}", sem="ae")
                fw_ = [f"a_prf_{fi}", f"a_poly_{fi}"]
                if tb2 == 0 and p >= 1:
                    fw_.append(f"pe_ctx_{p * NCH - 1}")
                P.op("vector", lambda g, qk=qk, tb2=tb2, fb=fb: g.tensor_tensor(
                    qkf[qk][:, tb2 * 512:(tb2 + 1) * 512], rinvbc[fb][:], poly[fb][:], ALU.mult),
                    waits=fw_, sig=f"v_f_{fi}", sem="ve")

            for qk in range(2):
                fi = (p * 8 + tb) * 2 + qk
                feat_stage1(fi, qk)
                if fi % 16 >= 1:
                    feat_stage2(fi - 1, 1 - qk)
                if fi % 16 == 15:
                    feat_stage2(fi, qk)

        if p == 0 and stop == "feat0":
            return finish([f"v_f_{16 - 1}", f"a_ve_{7}"])
        pair_f = f"v_f_{(p + 1) * 16 - 1}"
        P.op("vector", lambda g: g.memset(stf[:], 0.0),
             waits=([f"pe_ctx_{p * NCH - 1}"] if p >= 1 else []), sig=f"v_st0_{p}", sem="ve")
        for c4 in range(8):
            for j in range(4):
                ch = c4 * 4 + j
                k = p * NCH + ch
                vbk2 = 2 + (ch % 2)
                tw = ["g_vones", pair_f, f"a_ve_{p * 8 + 7}"]
                if k >= 2:
                    tw.append(f"a_vte_{k - 2}")
                P.op("tensor", lambda g, ch=ch, vbk2=vbk2: g.transpose(
                    pbb[vbk2][:, 0:128], vT[:, ch * 128:(ch + 1) * 128], ident[:]),
                    waits=tw, sig=f"pe_vtr_{p}_{ch}", sem="pe")
                P.op("scalar", lambda g, ch=ch, vbk2=vbk2: g.copy(
                    Vt[:, ch * 130:(ch + 1) * 130].rearrange("p (j w) -> p j w", j=2)[:, :, 0:64],
                    pbb[vbk2][:, 0:128].rearrange("p (j w) -> p j w", j=2)),
                    waits=[f"pe_vtr_{p}_{ch}"], sig=f"a_vte_{k}", sem="ae")
                kbk = 6 + (ch % 2)
                ktw = [f"a_kte_{k - 2}"] if k >= 2 else []
                P.op("tensor", lambda g, ch=ch, kbk=kbk: g.transpose(
                    pbb[kbk][:, 0:128], qkf[1][:, ch * 128:(ch + 1) * 128], ident[:]),
                    waits=ktw, sig=f"pe_ktr_{p}_{ch}", sem="pe")
                kew = [f"pe_ktr_{p}_{ch}"]
                if k >= 2:
                    gp_, gc_ = (k - 2) // NCH, (k - 2) % NCH
                    kew.append(f"pe_upd2_{gp_}_{gc_ // 4}_{gc_ % 4}")
                P.op("scalar", lambda g, ch=ch, kbk=kbk: g.copy(Ktok[ch % 2][:], pbb[kbk][:, 0:128]),
                     waits=kew, sig=f"a_kte_{k}", sem="ae")
                uw = [f"a_kte_{k}", f"a_vte_{k}"]
                if c4 >= 2 and j == 0:
                    uw.append(f"a_ue_{p}_{c4 - 2}")
                P.op("tensor", lambda g, ch=ch, j=j: g.matmul(
                    pb[4][0:64, j * 65:(j + 1) * 65], Ktok[ch % 2][:, 0:64],
                    Vt[:, ch * 130: ch * 130 + 65], start=True, stop=True),
                    waits=uw, sem="pe")
                P.op("tensor", lambda g, ch=ch, j=j: g.matmul(
                    pb[4][64:128, j * 65:(j + 1) * 65], Ktok[ch % 2][:, 64:128],
                    Vt[:, ch * 130 + 65: ch * 130 + 130], start=True, stop=True,
                    tile_position=(0, 64)),
                    waits=[], sig=f"pe_upd2_{p}_{c4}_{j}", sem="pe")
            ew = [f"pe_upd2_{p}_{c4}_3"]
            if c4 >= 2:
                ew.append(f"v_sta_{p}_{(c4 - 2) * 4 + 3}")
            elif p >= 1:
                ew.append(f"v_sta_{p - 1}_{(c4 + 6) * 4 + 3}")
            P.op("scalar", lambda g, c4=c4: g.copy(upd[:, (c4 % 2) * 260:(c4 % 2) * 260 + 260],
                                                   pb[4][:, 0:260]),
                 waits=ew, sig=f"a_ue_{p}_{c4}", sem="ae")
            for j2 in range(4):
                ch2 = c4 * 4 + j2
                cw = [f"v_st0_{p}"] if ch2 == 0 else [f"v_sta_{p}_{ch2 - 1}"]
                if p >= 1 and ch2 <= 1:
                    cw.append(f"pe_ctx_{p * NCH - 1}")
                P.op("vector", lambda g, ch2=ch2: g.tensor_copy(stbf[:, ch2 * 65:(ch2 + 1) * 65], stf[:]),
                     waits=cw, sig=f"v_stb_{p}_{ch2}", sem="ve")
                P.op("vector", lambda g, ch2=ch2, c4=c4, j2=j2: g.tensor_tensor(
                    stf[:], stf[:], upd[:, (c4 % 2) * 260 + j2 * 65:
                                         (c4 % 2) * 260 + j2 * 65 + 65], ALU.add),
                    waits=[f"a_ue_{p}_{c4}", f"v_stb_{p}_{ch2}"], sig=f"v_sta_{p}_{ch2}", sem="ve")

        if p == 0 and stop == "stageA0":
            return finish([f"v_sta_0_31", f"a_ue_0_7", f"a_vte_{NCH - 1}"])
        for ch in range(NCH):
            k = p * NCH + ch
            sbk = ch % 2
            sw = [f"v_stb_{p}_{ch}", pair_f]
            if k >= 2:
                sw.append(f"v_smk_{k - 2}")
            P.op("tensor", lambda g, ch=ch, sbk=sbk: g.matmul(
                pb[sbk][:, 0:128], qkf[1][0:64, ch * 128:(ch + 1) * 128],
                qkf[0][0:64, ch * 128:(ch + 1) * 128], start=True, stop=True),
                waits=sw, sem="pe")
            P.op("tensor", lambda g, sbk=sbk: g.matmul(
                pb[sbk][0:1, 500:501], ident[:, 0:1], ident[:, 0:1],
                start=True, stop=True), waits=[], sem="pe")
            P.op("tensor", lambda g, ch=ch, sbk=sbk: g.matmul(
                pb[sbk][:, 128:256], qkf[1][64:128, ch * 128:(ch + 1) * 128],
                qkf[0][64:128, ch * 128:(ch + 1) * 128], start=True, stop=True),
                waits=[], sig=f"pe_S_{k}", sem="pe")
            if p == 0 and ch == 0 and stop == "sbS":
                return finish([f"pe_S_{k}"])
            P.op("vector", lambda g, ch=ch, sbk=sbk: g.tensor_tensor(
                smk[ch % 2][:], pb[sbk][:, 0:256], maskr[:], ALU.mult),
                waits=[f"pe_S_{k}", "init_done"], sig=f"v_smk_{k}", sem="ve")
            if p == 0 and ch == 0 and stop == "sbM":
                return finish([f"v_smk_{k}"])
            cbk = 2 + (ch % 2)
            cw = [f"v_smk_{k}", f"a_vte_{p * NCH + NCH - 1}"]
            if k >= 2:
                cw.append(f"v_ot_{k - 2}")
            P.op("tensor", lambda g, ch=ch, cbk=cbk: g.matmul(
                pb[cbk][:, 0:65], smk[ch % 2][:, 0:128], Vt[:, ch * 130: ch * 130 + 65],
                start=True, stop=False), waits=cw, sem="pe")
            P.op("tensor", lambda g, ch=ch, cbk=cbk: g.matmul(
                pb[cbk][:, 0:65], qkf[0][0:64, ch * 128:(ch + 1) * 128],
                stbf[0:64, ch * 65:(ch + 1) * 65], start=False, stop=True),
                waits=[], sem="pe")
            P.op("tensor", lambda g, ch=ch, cbk=cbk: g.matmul(
                pb[cbk][:, 65:130], smk[ch % 2][:, 128:256], Vt[:, ch * 130 + 65: ch * 130 + 130],
                start=True, stop=False), waits=[], sem="pe")
            P.op("tensor", lambda g, ch=ch, cbk=cbk: g.matmul(
                pb[cbk][:, 65:130], qkf[0][64:128, ch * 128:(ch + 1) * 128],
                stbf[64:128, ch * 65:(ch + 1) * 65], start=False, stop=True),
                waits=[], sig=f"pe_ctx_{k}", sem="pe")
            if p == 0 and ch == 0 and stop == "sbC":
                return finish([f"pe_ctx_{k}"])
            def emit_otr(kk, chch):
                obk = 6 + (chch % 2)
                ow = [f"v_ot_{kk}"]
                if kk >= 2:
                    ow.append(f"a_ot_{kk - 2}")
                P.op("tensor", lambda g, chch=chch, obk=obk: g.transpose(
                    pbb[obk][:, 0:128], otok[chch % 2][:], ident[:]),
                    waits=ow, sig=f"pe_otr_{kk}", sem="pe")
                P.op("scalar", lambda g, p=p, chch=chch, obk=obk: g.copy(
                    ATT[:, p * T + chch * 128: p * T + (chch + 1) * 128], pbb[obk][:, 0:128]),
                    waits=[f"pe_otr_{kk}"], sig=f"a_ot_{kk}", sem="ae")
            if ch >= 1:
                emit_otr(k - 1, ch - 1)
            dw = [f"pe_ctx_{k}"]
            if k >= 2:
                dw.append(f"pe_otr_{k - 2}")
            for h01 in range(2):
                base = h01 * 65
                c0 = (ch * 2 + h01) % 64
                P.op("vector", lambda g, cbk=cbk, base=base, c0=c0: g.tensor_scalar_max(
                    dcol[:, c0:c0 + 1], pb[cbk][:, base + 64: base + 65], 1e-6),
                    waits=(dw if h01 == 0 else []),
                    sig=f"v_dx_{k}_{h01}", sem="ve")
                P.op("vector", lambda g, c0=c0: g.reciprocal(dcol[:, 64 + c0:65 + c0], dcol[:, c0:c0 + 1]),
                     waits=[f"v_dx_{k}_{h01}"], sig=f"v_dr_{k}_{h01}", sem="ve")
                P.op("vector", lambda g, ch=ch, cbk=cbk, base=base, h01=h01, c0=c0: g.tensor_scalar_mul(
                    otok[ch % 2][:, h01 * 64:(h01 + 1) * 64], pb[cbk][:, base: base + 64],
                    dcol[:, 64 + c0:65 + c0]),
                    waits=[f"v_dr_{k}_{h01}"], sig=f"v_om_{k}_{h01}", sem="ve")
            P.alias(f"v_ot_{k}", "ve")
            if ch == NCH - 1:
                emit_otr(k, ch)

        if stop == f"pair{p}":
            return finish([f"v_ot_{(p + 1) * NCH - 1}"])

    att_done = f"a_ot_{NP * NCH - 1}"
    tap_r3, tap_r2 = [], []
    if "t_qf" in dbg_t:
        P.dma("sync", lambda g: g.dma_start(out=dbg_t["t_qf"], in_=qkf[0][:]),
              waits=[f"v_f_{NP * 16 - 1}"], sig="tap_qf", sem="stp")
        tap_r3.append("tap_qf")
    if "t_kf" in dbg_t:
        P.dma("sync", lambda g: g.dma_start(out=dbg_t["t_kf"], in_=qkf[1][:]),
              waits=[f"v_f_{NP * 16 - 1}"], sig="tap_kf", sem="stp")
        tap_r3.append("tap_kf")
    if "t_vt" in dbg_t:
        P.dma("sync", lambda g: g.dma_start(out=dbg_t["t_vt"], in_=vT[:]),
              waits=[f"a_ve_{NP * 8 - 1}"], sig="tap_vt", sem="stp")
        tap_r3.append("tap_vt")
    if "t_attnT" in dbg_t:
        P.dma("sync", lambda g: g.dma_start(out=dbg_t["t_attnT"], in_=ATT[:]),
              waits=[att_done], sig="tap_att", sem="stp")
        tap_r2.append("tap_att")
    if tap_r3 or tap_r2:
        P.alias("tapsA_done", "stp")
        tap_r3 = ["tapsA_done"] if tap_r3 else []
        tap_r2 = ["tapsA_done"] if tap_r2 else []
    P.flush()

    areset("r1")
    areset("r3")
    woutb = aalloc([128, 8192], BF16, "r1")
    x2 = [aalloc([128, 1024], F32, "r1") for i in range(2)]
    s_sb = aalloc([128, 1024], F32, "r1")
    sq2 = aalloc([128, 1024], F32, "r1")
    y32 = [aalloc([128, 1024], F32, "r1") for i in range(2)]
    ybf = aalloc([128, 1024], BF16, "r1")
    YT = aalloc([128, 8 * T], BF16, "r3")

    P.dma("gpsimd", lambda g: g.dma_start(out=woutb[:], in_=d["wout_src"]),
          waits=[att_done], sig="ld_wout", sem="swo")

    for t in range(32):
        lw = [f"v_y_{t - 2}"] if t >= 2 else [att_done]
        P.dma("sync", lambda g, t=t: g.dma_start(out=x2[t % 2][:], in_=xr[t]),
              waits=lw, sig=f"x2_{t}", sem=f"sx{t % 2}")
        for oh in range(2):
            mw = ["ld_wout", att_done, "g_bout", "init_done"]
            if t >= 2:
                mw.append(f"v_s_{t - 2}_{oh}")
            for gc in range(8):
                P.op("tensor", lambda g, t=t, oh=oh, gc=gc: g.matmul(
                    pb[oh][:], ATT[:, gc * T + t * 128: gc * T + (t + 1) * 128],
                    woutb[:, gc * 1024 + oh * 512: gc * 1024 + (oh + 1) * 512],
                    start=(gc == 0), stop=False),
                    waits=(mw if gc == 0 else []), sem="pe")
            P.op("tensor", lambda g, oh=oh: g.matmul(
                pb[oh][:], ones1b[:], boutb[0:1, oh * 512:(oh + 1) * 512],
                start=False, stop=True),
                waits=[], sig=f"pe_op_{t}_{oh}", sem="pe")
        for oh in range(2):
            P.op("vector", lambda g, t=t, oh=oh: g.scalar_tensor_tensor(
                s_sb[:, oh * 512:(oh + 1) * 512], pb[oh][:], 1.0,
                x2[t % 2][:, oh * 512:(oh + 1) * 512], ALU.mult, ALU.add,
                accum_out=lncol[:, oh: oh + 1]),
                waits=[f"pe_op_{t}_{oh}", f"x2_{t}"]
                + ([f"a_sqs_{t - 1}_{oh}", f"v_y_{t - 1}"] if t >= 1 else []),
                sig=f"v_s_{t}_{oh}", sem="ve")
        P.op("vector", lambda g: g.tensor_tensor(lncol[:, 2:3], lncol[:, 0:1], lncol[:, 1:2], ALU.add),
             waits=[f"v_s_{t}_0", f"v_s_{t}_1"], sig=f"v_la_{t}", sem="ve")
        P.op("vector", lambda g: g.tensor_scalar_mul(lncol[:, 2:3], lncol[:, 2:3], 1.0 / 1024.0),
             waits=[f"v_la_{t}"], sig=f"v_mean_{t}", sem="ve")
        for oh in range(2):
            P.op("scalar", lambda g, oh=oh: g.activation(
                sq2[:, oh * 512:(oh + 1) * 512], s_sb[:, oh * 512:(oh + 1) * 512], AF.Square,
                accum_out=lncol[:, 5 + oh: 6 + oh]),
                waits=[f"v_s_{t}_{oh}"], sig=f"a_sqs_{t}_{oh}", sem="ae")
        P.op("vector", lambda g: g.tensor_tensor(lncol[:, 4:5], lncol[:, 5:6], lncol[:, 6:7], ALU.add),
             waits=[f"a_sqs_{t}_0", f"a_sqs_{t}_1"], sig=f"v_lb_{t}", sem="ve")
        P.op("vector", lambda g: g.tensor_tensor(lncol[:, 3:4], lncol[:, 2:3], lncol[:, 2:3], ALU.mult),
             waits=[f"v_mean_{t}"], sig=f"v_lc_{t}", sem="ve")
        P.op("vector", lambda g: g.tensor_scalar(lncol[:, 4:5], lncol[:, 4:5], 1.0 / 1024.0,
                                                 lncol[:, 3:4], ALU.mult, ALU.subtract),
             waits=[f"v_lb_{t}", f"v_lc_{t}"], sig=f"v_var_{t}", sem="ve")
        P.op("scalar", lambda g: g.activation(lncol[:, 4:5], lncol[:, 4:5], AF.Ln, bias=epsc[:, 0:1]),
             waits=[f"v_var_{t}", "v_eps"], sig=f"a_lnv_{t}", sem="ae")
        P.op("scalar", lambda g: g.activation(lncol[:, 7:8], lncol[:, 4:5], AF.Exp, scale=-0.5),
             waits=[f"a_lnv_{t}"], sig=f"a_rstd_{t}", sem="ae")
        P.op("vector", lambda g, t=t: g.tensor_scalar(
            y32[t % 2][:], s_sb[:], lncol[:, 2:3], lncol[:, 7:8], ALU.subtract, ALU.mult),
            waits=[f"a_rstd_{t}"] + ([f"ywr_{t - 2}"] if t >= 2 else []), sig=f"v_y_{t}", sem="ve")
        bw = [f"v_y_{t}"] + ([f"pe_yt_{t - 1}_7"] if t >= 1 else [])
        P.op("vector", lambda g, t=t: g.tensor_copy(ybf[:], y32[t % 2][:]),
             waits=bw, sig=f"v_ybf_{t}", sem="ve")
        P.dma("sync", lambda g, t=t: g.dma_start(out=scr["y_scr"][t], in_=y32[t % 2][:]),
              waits=[f"v_y_{t}"], sig=f"ywr_{t}", sem=f"yw{t % 2}")
        for gc in range(8):
            bk = 2 + (gc % 2)
            tw = [f"v_ybf_{t}"]
            if (t, gc) > (0, 1):
                pt, pg = (t, gc - 2) if gc >= 2 else (t - 1, gc + 6)
                tw.append(f"a_yt_{pt}_{pg}")
            elif t == 0:
                tw.append(f"v_ot_{NP * NCH - 1}")
                tw += tap_r3
            P.op("tensor", lambda g, t=t, gc=gc, bk=bk: g.transpose(
                pbb[bk][:, 0:128], ybf[:, gc * 128:(gc + 1) * 128], ident[:]),
                waits=tw, sig=f"pe_yt_{t}_{gc}", sem="pe")
            P.op("scalar", lambda g, t=t, gc=gc, bk=bk: g.copy(
                YT[:, gc * T + t * 128: gc * T + (t + 1) * 128], pbb[bk][:, 0:128]),
                waits=[f"pe_yt_{t}_{gc}"], sig=f"a_yt_{t}_{gc}", sem="ae")

    yt_done = "a_yt_31_7"
    if "t_y" in dbg_t:
        P.dma("sync", lambda g: g.dma_start(out=dbg_t["t_y"], in_=scr["y_scr"][:]),
              waits=[f"ywr_{31}", f"ywr_{30}"], sig="tap_y", sem="stp")
    if stop == "phaseB":
        w = [yt_done, "ywr_31", "ywr_30"]
        if "t_y" in dbg_t:
            w.append("tap_y")
        return finish(w)
    P.flush()

    areset("r2")
    WR = aalloc([128, 1024], F32, "r2")
    r2_mark = _bump["r2"]
    scn = aalloc([128, 256], F32, "r2")
    scc = aalloc([128, 256], F32, "r2")
    mskd = aalloc([128, 256], F32, "r2")
    selb = aalloc([128, 256], F32, "r2")
    gsml = aalloc([128, 64], F32, "r2")
    rcol = aalloc([128, 32], F32, "r2")
    biasbc = aalloc([128, 32], F32, "r2")

    P.op("tensor", lambda g: g.matmul(pb[4][:, 0:32], ones1f[:], gbias[:], start=True, stop=True),
         waits=["init_done", yt_done, f"v_ng_{NP * 16 - 1}"] + tap_r2, sig="pe_gb", sem="pe")
    P.op("scalar", lambda g: g.copy(biasbc[:], pb[4][:, 0:32]), waits=["pe_gb"], sig="a_gb", sem="ae")

    def v3(x, i, j):
        return x[:].rearrange("p (a b) -> p a b", a=i, b=j)

    for tg in range(4):
        for j in range(8):
            tile = tg * 8 + j
            mw = [yt_done, "g_gw"]
            if tg == 0 and j == 0:
                mw.append(f"a_poly_{NP * 16 - 1}")
            if tg >= 1:
                mw.append(f"v_wr_{tg - 1}")
            for gc in range(8):
                P.op("tensor", lambda g, tile=tile, j=j, gc=gc: g.matmul(
                    pb[5 + j // 4][:, (j % 4) * 128:(j % 4) * 128 + 32],
                    YT[:, gc * T + tile * 128: gc * T + (tile + 1) * 128],
                    gwb[:, gc * 32:(gc + 1) * 32],
                    start=(gc == 0), stop=(gc == 7)),
                    waits=(mw if gc == 0 else []),
                    sig=(f"pe_lg_{tile}" if gc == 7 else None), sem="pe")
        for j in range(8):
            P.op("scalar", lambda g, j=j: g.activation(
                scn[:, j * 32:(j + 1) * 32], pb[5 + j // 4][:, (j % 4) * 128:(j % 4) * 128 + 32],
                AF.Sigmoid),
                waits=[f"pe_lg_{tg * 8 + (j // 4) * 4 + 3}"], sig=f"a_sg_{tg}_{j}", sem="ae")
        vv = lambda nm: f"{nm}_{tg}"
        gm1, gm2 = gsml[:, 0:32], gsml[:, 32:64]
        P.op("vector", lambda g: g.tensor_tensor(
            v3(scc, 8, 32), v3(scn, 8, 32),
            biasbc[:].unsqueeze(1).broadcast_to([128, 8, 32]), ALU.add),
            waits=[f"a_sg_{tg}_{j}" for j in range(8)] + ["a_gb"], sig=vv("v_sc"), sem="ve")
        P.op("vector", lambda g: g.reduce_max(
            gm1.rearrange("p (t q) -> p t q", q=4), v3(scc, 32, 8), axis=AX.X),
            waits=[vv("v_sc")], sig=vv("v_gm1"), sem="ve")
        P.op("vector", lambda g: g.tensor_tensor(
            v3(mskd, 32, 8), v3(scc, 32, 8),
            gm1.unsqueeze(2).broadcast_to([128, 32, 8]), ALU.is_equal),
            waits=[vv("v_gm1")], sig=vv("v_eq1"), sem="ve")
        P.op("vector", lambda g: g.scalar_tensor_tensor(mskd[:], mskd[:], -1e30, scc[:],
                                                        ALU.mult, ALU.add),
             waits=[vv("v_eq1")], sig=vv("v_t1"), sem="ve")
        P.op("vector", lambda g: g.reduce_max(
            gm2.rearrange("p (t q) -> p t q", q=4), v3(mskd, 32, 8), axis=AX.X),
            waits=[vv("v_t1")], sig=vv("v_gm2"), sem="ve")
        P.op("vector", lambda g: g.tensor_tensor(gm1, gm1, gm2, ALU.add),
             waits=[vv("v_gm2")], sig=vv("v_gs"), sem="ve")
        P.op("vector", lambda g: g.reduce_max(
            rcol[:, 0:8], gm1.rearrange("p (t q) -> p t q", q=4), axis=AX.X),
            waits=[vv("v_gs")], sig=vv("v_g1"), sem="ve")
        P.op("vector", lambda g: g.tensor_tensor(
            gm2.rearrange("p (t q) -> p t q", q=4), gm1.rearrange("p (t q) -> p t q", q=4),
            rcol[:, 0:8].unsqueeze(2).broadcast_to([128, 8, 4]), ALU.is_equal),
            waits=[vv("v_g1")], sig=vv("v_eqg"), sem="ve")
        P.op("vector", lambda g: g.scalar_tensor_tensor(gm2, gm2, -1e30, gm1, ALU.mult, ALU.add),
             waits=[vv("v_eqg")], sig=vv("v_tg"), sem="ve")
        P.op("vector", lambda g: g.reduce_max(
            rcol[:, 8:16], gm2.rearrange("p (t q) -> p t q", q=4), axis=AX.X),
            waits=[vv("v_tg")], sig=vv("v_g2"), sem="ve")
        P.op("vector", lambda g: g.tensor_tensor(
            gm2.rearrange("p (t q) -> p t q", q=4), gm1.rearrange("p (t q) -> p t q", q=4),
            rcol[:, 8:16].unsqueeze(2).broadcast_to([128, 8, 4]), ALU.is_ge),
            waits=[vv("v_g2")], sig=vv("v_gmask"), sem="ve")
        P.op("vector", lambda g: g.tensor_tensor(
            v3(mskd, 32, 8), v3(scc, 32, 8),
            gm2.unsqueeze(2).broadcast_to([128, 32, 8]), ALU.mult),
            waits=[vv("v_gmask")], sig=vv("v_msk"), sem="ve")
        P.op("vector", lambda g: g.memset(selb[:], 0.0),
             waits=([f"v_wr_{tg - 1}"] if tg >= 1 else []), sig=vv("v_sel0"), sem="ve")
        prev = vv("v_msk")
        for it in range(8):
            P.op("vector", lambda g: g.reduce_max(rcol[:, 16:24], v3(mskd, 8, 32), axis=AX.X),
                 waits=[prev], sig=vv(f"v_m8_{it}"), sem="ve")
            P.op("vector", lambda g: g.tensor_tensor(
                v3(scc, 8, 32), v3(mskd, 8, 32),
                rcol[:, 16:24].unsqueeze(2).broadcast_to([128, 8, 32]), ALU.is_equal),
                waits=[vv(f"v_m8_{it}")], sig=vv(f"v_eq8_{it}"), sem="ve")
            P.op("vector", lambda g: g.tensor_tensor(selb[:], selb[:], scc[:], ALU.add),
                 waits=[vv(f"v_eq8_{it}"), vv("v_sel0")], sig=vv(f"v_sa_{it}"), sem="ve")
            P.op("vector", lambda g: g.scalar_tensor_tensor(mskd[:], scc[:], -1e30, mskd[:],
                                                            ALU.mult, ALU.add),
                 waits=[vv(f"v_sa_{it}")], sig=vv(f"v_mu_{it}"), sem="ve")
            prev = vv(f"v_mu_{it}")
        P.op("vector", lambda g: g.tensor_tensor(selb[:], selb[:], scn[:], ALU.mult),
             waits=[prev], sig=vv("v_w"), sem="ve")
        P.op("vector", lambda g: g.reduce_sum(rcol[:, 24:32], v3(selb, 8, 32), axis=AX.X),
             waits=[vv("v_w")], sig=vv("v_ws"), sem="ve")
        P.op("vector", lambda g: g.tensor_scalar_add(rcol[:, 24:32], rcol[:, 24:32], 1e-20),
             waits=[vv("v_ws")], sig=vv("v_pa"), sem="ve")
        P.op("vector", lambda g: g.reciprocal(rcol[:, 16:24], rcol[:, 24:32]),
             waits=[vv("v_pa")], sig=vv("v_pb"), sem="ve")
        P.op("vector", lambda g: g.tensor_scalar_mul(rcol[:, 16:24], rcol[:, 16:24], 2.5),
             waits=[vv("v_pb")], sig=vv("v_rs"), sem="ve")
        P.op("vector", lambda g, tg=tg: g.tensor_tensor(
            WR[:, tg * 256:(tg + 1) * 256].rearrange("p (t e) -> p t e", e=32),
            v3(selb, 8, 32),
            rcol[:, 16:24].unsqueeze(2).broadcast_to([128, 8, 32]), ALU.mult),
            waits=[vv("v_rs")], sig=f"v_wr_{tg}", sem="ve")

    if "t_wroute" in dbg_t:
        P.dma("sync", lambda g: g.dma_start(out=dbg_t["t_wroute"], in_=WR[:]),
              waits=["v_wr_3"], sig="tap_wr", sem="stp")
    if stop == "phaseC":
        w = ["v_wr_3"]
        if "t_wroute" in dbg_t:
            w.append("tap_wr")
        return finish(w)
    P.flush()

    areset("r1")
    areset("r2", keep=r2_mark - _rbase["r2"])
    ACC = aalloc([128, 16 * 1024], F32, "r1")
    WGb = [aalloc([128, 4096], BF16, "r2") for i in range(2)]
    WUb = [aalloc([128, 4096], BF16, "r2") for i in range(2)]
    WDb = [aalloc([128, 4096], BF16, "r2") for i in range(2)]
    Hb = [aalloc([128, 2048], BF16, "r2") for i in range(2)]
    SGb = aalloc([128, 512], F32, "r2")

    n_tb = 4
    outr = out.rearrange("(n p) c -> p n c", p=128)
    for hf in range(2):
        aw = [f"ywr_{31}", f"ywr_{30}", yt_done]
        if hf == 1:
            aw.append("out_0")
        P.dma("sync", lambda g, hf=hf: g.dma_start(
            out=ACC[:].rearrange("p (n c) -> p n c", c=1024),
            in_=scr["y_scr"].rearrange("n p c -> p n c")[:, hf * 16:(hf + 1) * 16, :]),
            waits=aw, sig=f"accld_{hf}", sem="sac")
        for e in range(NE):
            pbuf = e % 2
            k = hf * NE + e
            ew = []
            if k == 0:
                ew.append("v_wr_3")
            if k >= 2:
                ew.append(f"pe_dn_{k - 2}_{n_tb - 1}_7")
            P.dma("gpsimd", lambda g, e=e, pbuf=pbuf: g.dma_start(out=WGb[pbuf][:], in_=d["eg_src"][e]),
                  waits=ew, sig=f"wg_{hf}_{e}", sem=f"eb{e % 2}")
            P.dma("gpsimd", lambda g, e=e, pbuf=pbuf: g.dma_start(out=WUb[pbuf][:], in_=d["eu_src"][e]),
                  waits=[], sig=f"wu_{hf}_{e}", sem=f"eb{e % 2}")
            P.dma("gpsimd", lambda g, e=e, pbuf=pbuf: g.dma_start(out=WDb[pbuf][:], in_=d["ed_src"][e]),
                  waits=[], sig=f"wd_{hf}_{e}", sem=f"eb{e % 2}")
            for tb in range(n_tb):
                tbk = k * n_tb + tb
                for it in range(4):
                    ki = tbk * 4 + it
                    gbk, ubk = it % 2, 2 + it % 2
                    gw_ = [f"wd_{hf}_{e}", yt_done]
                    if ki >= 2:
                        gw_.append(f"v_h_{ki - 2}")
                    if ki == 0:
                        gw_ += ["v_s_31_0", "v_s_31_1", "a_yt_31_6", "a_yt_31_7"]
                    for gc in range(8):
                        P.op("tensor", lambda g, hf=hf, tb=tb, it=it, gc=gc, pbuf=pbuf, gbk=gbk: g.matmul(
                            pb[gbk][:], WGb[pbuf][:, gc * 512 + it * 128: gc * 512 + it * 128 + 128],
                            YT[:, gc * T + (hf * 4 + tb) * 512: gc * T + (hf * 4 + tb + 1) * 512],
                            start=(gc == 0), stop=(gc == 7)),
                            waits=(gw_ if gc == 0 else []),
                            sig=(f"pe_g_{ki}" if gc == 7 else None), sem="pe")
                    for gc in range(8):
                        P.op("tensor", lambda g, hf=hf, tb=tb, it=it, gc=gc, pbuf=pbuf, ubk=ubk: g.matmul(
                            pb[ubk][:], WUb[pbuf][:, gc * 512 + it * 128: gc * 512 + it * 128 + 128],
                            YT[:, gc * T + (hf * 4 + tb) * 512: gc * T + (hf * 4 + tb + 1) * 512],
                            start=(gc == 0), stop=(gc == 7)),
                            waits=[], sig=(f"pe_u_{ki}" if gc == 7 else None), sem="pe")
                    sw = [f"pe_g_{ki}"]
                    if ki >= 1:
                        sw.append(f"v_h_{ki - 1}")
                    P.op("scalar", lambda g, gbk=gbk: g.activation(SGb[:], pb[gbk][:], AF.Silu),
                         waits=sw, sig=f"a_sil_{ki}", sem="ae")
                    hw = [f"a_sil_{ki}", f"pe_u_{ki}"]
                    if tbk >= 2 and it == 0:
                        pk2 = tbk - 2
                        hw.append(f"pe_dn_{pk2 // n_tb}_{pk2 % n_tb}_7")
                    P.op("vector", lambda g, tb=tb, it=it, ubk=ubk: g.tensor_tensor(
                        Hb[tb % 2][:, it * 512:(it + 1) * 512], SGb[:], pb[ubk][:], ALU.mult),
                        waits=hw, sig=f"v_h_{ki}", sem="ve")
                for ts_ in range(4):
                    for oh in range(2):
                        nd = ts_ * 2 + oh
                        dbk = 4 + nd % 4
                        ndk = tbk * 8 + nd
                        dw = [f"v_h_{tbk * 4 + 3}", f"wd_{hf}_{e}"]
                        if ndk >= 4:
                            dw.append(f"v_dn_{ndk - 4}")
                        if ndk < 4:
                            dw += ["a_sg_3_3", "a_sg_3_7", "a_gb"]
                        for ic in range(4):
                            P.op("tensor", lambda g, tb=tb, ts_=ts_, oh=oh, ic=ic, pbuf=pbuf, dbk=dbk: g.matmul(
                                pb[dbk][:], Hb[tb % 2][:, ic * 512 + ts_ * 128: ic * 512 + ts_ * 128 + 128],
                                WDb[pbuf][:, ic * 1024 + oh * 512: ic * 1024 + oh * 512 + 512],
                                start=(ic == 0), stop=(ic == 3)),
                                waits=(dw if ic == 0 else []),
                                sig=(f"pe_dn_{k}_{tb}_{nd}" if ic == 3 else None), sem="pe")
                        vw = [f"pe_dn_{k}_{tb}_{nd}"]
                        if e == 0 and tb == 0:
                            vw.append(f"accld_{hf}")
                        if e >= 1:
                            vw.append(f"v_dn_{ndk - n_tb * 8}")
                        if e >= 2:
                            vw.append(f"v_wr_{(hf * 16 + tb * 4 + ts_) // 8}")
                        gtile = tb * 4 + ts_
                        a_sl = lambda gtile=gtile, oh=oh: ACC[:, gtile * 1024 + oh * 512: gtile * 1024 + (oh + 1) * 512]
                        if e < 2:
                            P.op("vector", lambda g, a_sl=a_sl, dbk=dbk: g.scalar_tensor_tensor(
                                a_sl(), pb[dbk][:], 1.0, a_sl(), ALU.mult, ALU.add),
                                waits=vw, sig=f"v_dn_{ndk}", sem="ve")
                        else:
                            P.op("vector", lambda g, a_sl=a_sl, dbk=dbk, hf=hf, e=e, gtile=gtile: g.scalar_tensor_tensor(
                                a_sl(), pb[dbk][:],
                                WR[:, (hf * 16 + gtile) * 32 + (e - 2): (hf * 16 + gtile) * 32 + (e - 1)],
                                a_sl(), ALU.mult, ALU.add),
                                waits=vw, sig=f"v_dn_{ndk}", sem="ve")
        last_dn = f"v_dn_{((hf * NE + NE - 1) * n_tb + n_tb - 1) * 8 + 7}"
        for tt in range(16):
            t = hf * 16 + tt
            a0 = ACC[:, tt * 1024: tt * 1024 + 512]
            a1 = ACC[:, tt * 1024 + 512: tt * 1024 + 1024]
            s0w = [last_dn]
            if tt >= 1:
                s0w += [f"a2_sq_{t - 1}_1", f"v2_y_{t - 1}"]
            P.op("vector", lambda g, a0=a0: g.tensor_scalar(
                SGb[:], a0, 0.0, None, ALU.add, ALU.add, accum_out=lncol[:, 0:1]),
                waits=s0w, sig=f"v2_s0_{t}", sem="ve")
            P.op("vector", lambda g, a1=a1: g.tensor_scalar(
                SGb[:], a1, 0.0, None, ALU.add, ALU.add, accum_out=lncol[:, 1:2]),
                waits=[f"v2_s0_{t}"], sig=f"v2_s1_{t}", sem="ve")
            P.op("vector", lambda g: g.tensor_tensor(lncol[:, 2:3], lncol[:, 0:1], lncol[:, 1:2], ALU.add),
                 waits=[f"v2_s0_{t}", f"v2_s1_{t}"], sig=f"v2_la_{t}", sem="ve")
            P.op("vector", lambda g: g.tensor_scalar_mul(lncol[:, 2:3], lncol[:, 2:3], 1.0 / 1024.0),
                 waits=[f"v2_la_{t}"], sig=f"v2_mean_{t}", sem="ve")
            for oh in range(2):
                P.op("scalar", lambda g, tt=tt, oh=oh: g.activation(
                    SGb[:],
                    ACC[:, tt * 1024 + oh * 512: tt * 1024 + (oh + 1) * 512],
                    AF.Square, accum_out=lncol[:, 5 + oh: 6 + oh]),
                    waits=[f"v2_s1_{t}"] + ([f"a2_sq_{t}_0"] if oh == 1 else []),
                    sig=f"a2_sq_{t}_{oh}", sem="ae")
            P.op("vector", lambda g: g.tensor_tensor(lncol[:, 4:5], lncol[:, 5:6], lncol[:, 6:7], ALU.add),
                 waits=[f"a2_sq_{t}_0", f"a2_sq_{t}_1"], sig=f"v2_lb_{t}", sem="ve")
            P.op("vector", lambda g: g.tensor_tensor(lncol[:, 3:4], lncol[:, 2:3], lncol[:, 2:3], ALU.mult),
                 waits=[f"v2_mean_{t}"], sig=f"v2_lc_{t}", sem="ve")
            P.op("vector", lambda g: g.tensor_scalar(lncol[:, 4:5], lncol[:, 4:5], 1.0 / 1024.0,
                                                     lncol[:, 3:4], ALU.mult, ALU.subtract),
                 waits=[f"v2_lb_{t}", f"v2_lc_{t}"], sig=f"v2_var_{t}", sem="ve")
            P.op("scalar", lambda g: g.activation(lncol[:, 4:5], lncol[:, 4:5], AF.Ln, bias=epsc[:, 0:1]),
                 waits=[f"v2_var_{t}", "v_eps"], sig=f"a2_lnv_{t}", sem="ae")
            P.op("scalar", lambda g: g.activation(lncol[:, 7:8], lncol[:, 4:5], AF.Exp, scale=-0.5),
                 waits=[f"a2_lnv_{t}"], sig=f"a2_rstd_{t}", sem="ae")
            P.op("vector", lambda g, tt=tt: g.tensor_scalar(
                ACC[:, tt * 1024:(tt + 1) * 1024], ACC[:, tt * 1024:(tt + 1) * 1024],
                lncol[:, 2:3], lncol[:, 7:8], ALU.subtract, ALU.mult),
                waits=[f"a2_rstd_{t}"], sig=f"v2_y_{t}", sem="ve")
        P.dma("sync", lambda g, hf=hf: g.dma_start(
            out=outr[:, hf * 16:(hf + 1) * 16, :],
            in_=ACC[:].rearrange("p (n c) -> p n c", c=1024)),
            waits=[f"v2_y_{hf * 16 + tt2}" for tt2 in range(16)], sig=f"out_{hf}", sem="sout")

    fw = ["out_0", "out_1"]
    if P.counts["stp"] > 0:
        P.alias("taps_all", "stp")
        fw.append("taps_all")
    P.op("sync", None, waits=fw)
    P.flush()

    stack.close()
    return nc, dbg_t


def host_prep(ins):
    f32 = np.float32
    w_qkv = np.asarray(ins["L0_attn_w_qkv"], f32)
    wq = w_qkv[0:1024].reshape(16, 64, 1024)
    wk = w_qkv[1024:2048].reshape(16, 64, 1024)
    wv = w_qkv[2048:3072].reshape(16, 64, 1024)
    blocks = []
    for p in range(NP):
        h0, h1 = 2 * p, 2 * p + 1
        cols = np.concatenate([wq[h0].T, wq[h1].T, wk[h0].T, wk[h1].T,
                               wv[h0].T, wv[h1].T], axis=1)
        blocks.append(cols)
    wqkv_src = np.stack(blocks).reshape(NP, 8, 128, 384).transpose(0, 2, 1, 3).reshape(NP, 128, 3072)

    woutT = np.asarray(ins["L0_attn_w_out"], f32).T
    wout_src = woutT.reshape(8, 128, 1024).transpose(1, 0, 2).reshape(128, 8192)

    sgT = np.asarray(ins["L0_ffn_shared_gate"], f32).T
    suT = np.asarray(ins["L0_ffn_shared_up"], f32).T
    sdT = np.asarray(ins["L0_ffn_shared_down"], f32).T
    egT = np.transpose(np.asarray(ins["L0_ffn_expert_gate"], f32), (0, 2, 1))
    euT = np.transpose(np.asarray(ins["L0_ffn_expert_up"], f32), (0, 2, 1))
    edT = np.transpose(np.asarray(ins["L0_ffn_expert_down"], f32), (0, 2, 1))
    eg_full = np.concatenate([sgT[None, :, 0:512], sgT[None, :, 512:1024], egT], 0)
    eu_full = np.concatenate([suT[None, :, 0:512], suT[None, :, 512:1024], euT], 0)
    ed_full = np.concatenate([sdT[None, 0:512, :], sdT[None, 512:1024, :], edT], 0)
    eg_src = eg_full.reshape(NE, 8, 128, 512).transpose(0, 2, 1, 3).reshape(NE, 128, 4096)
    eu_src = eu_full.reshape(NE, 8, 128, 512).transpose(0, 2, 1, 3).reshape(NE, 128, 4096)
    ed_src = ed_full.reshape(NE, 4, 128, 1024).transpose(0, 2, 1, 3).reshape(NE, 128, 4096)

    gw_src = np.asarray(ins["L0_ffn_gate_w"], f32).T.reshape(8, 128, 32).transpose(1, 0, 2).reshape(128, 256)

    omega = np.asarray(ins["L0_attn_omega"], f32)
    omcat = np.transpose(omega, (1, 2, 0, 3)).reshape(16, 64, 64)
    omega_src = np.zeros((128, 1024), f32)
    for p in range(NP):
        omega_src[0:64, p * 128: p * 128 + 64] = omcat[2 * p]
        omega_src[64:128, p * 128 + 64: p * 128 + 128] = omcat[2 * p + 1]

    b_qkv = np.asarray(ins["L0_attn_b_qkv"], f32)
    bq = b_qkv[0:1024].reshape(16, 64)
    bk = b_qkv[1024:2048].reshape(16, 64)
    bv = b_qkv[2048:3072].reshape(16, 64)
    cols = []
    for p in range(NP):
        cols.append(np.concatenate([bq[2 * p], bq[2 * p + 1]]))
        cols.append(np.concatenate([bk[2 * p], bk[2 * p + 1]]))
        cols.append(np.concatenate([bv[2 * p], bv[2 * p + 1]]))
    bqkv_pp = np.stack(cols, axis=1)

    qn = np.asarray(ins["L0_attn_quad_nodes"], f32)
    qw = np.asarray(ins["L0_attn_quad_weights"], f32)
    pat = np.concatenate([np.repeat(qn, 32), np.repeat(qn, 32)])
    patw = np.concatenate([np.repeat(qw, 32), np.repeat(qw, 32)])
    quad_pp = np.stack([pat, patw], axis=1).astype(f32)

    bf = ml_dtypes.bfloat16
    blkdiag = np.zeros((128, 128), f32)
    blkdiag[0:64, 0:64] = 1.0
    blkdiag[64:128, 64:128] = 1.0
    common = dict(
        wqkv_src=np.ascontiguousarray(wqkv_src),
        wout_src=np.ascontiguousarray(wout_src),
        eg_src=np.ascontiguousarray(eg_src),
        eu_src=np.ascontiguousarray(eu_src),
        ed_src=np.ascontiguousarray(ed_src),
        gw_src=np.ascontiguousarray(gw_src),
        omega_src=omega_src,
        bqkv_pp=np.ascontiguousarray(bqkv_pp),
        bout_src=np.asarray(ins["L0_attn_b_out"], f32).reshape(1, 1024),
        quad_pp=quad_pp,
        gbias_src=np.asarray(ins["L0_ffn_gate_bias"], f32).reshape(1, 32),
        ident_bf=np.eye(128, dtype=bf),
        maskrep_bf=np.tile(np.triu(np.ones((128, 128), np.float32)), (1, 2)).astype(bf),
        blkdiag_bf=blkdiag.astype(bf),
        ones1_bf=np.ones((1, 128), bf),
        ones1_f=np.ones((1, 128), f32),
    )
    return common


def kernel(**ins):
    common = host_prep(ins)
    x_all = np.asarray(ins["L0_attn_x"], np.float32)
    nc, _ = build()
    in_maps = [dict(common, x=np.ascontiguousarray(x_all[b])) for b in range(8)]
    res = run_bass_kernel_spmd(nc, in_maps, core_ids=list(range(8)))
    return np.stack([res.results[b]["out"] for b in range(8)], axis=0)
